# version 15
# baseline (speedup 1.0000x reference)
import numpy as np
import ml_dtypes
import concourse.bass as bass
import concourse.mybir as mybir
from concourse.bass_utils import run_bass_kernel_spmd
from concourse.alu_op_type import AluOpType as OP
from contextlib import ExitStack

F32 = mybir.dt.float32
BF16 = mybir.dt.bfloat16
FP16 = mybir.dt.float16
I32 = mybir.dt.int32
AF = mybir.ActivationFunctionType
AX = mybir.AxisListType.X

B, S, D, FF, H, HD = 4, 2048, 2048, 8192, 16, 128
NT = S // 128
MT = 8
MQ = MT * 128
EPS = 1e-5
SCALE = float(1.0 / np.sqrt(HD))
HG = 4
NGRP = H // HG
FC1 = 32
ARN = 9
DEBUG = False
STOP_AFTER_S0 = False
S0_SKIP_MINE = False
S0_NO_XBAR = False


class P:
    def __init__(self, nc, stk):
        self.nc = nc
        self.eng = {"t": nc.tensor, "v": nc.vector, "s": nc.scalar,
                    "g": nc.gpsimd, "y": nc.sync}
        self.sems, self.cnt, self.waited, self.bufs = {}, {}, {}, {}
        for name in ("t", "v", "s", "g", "y", "gd", "yd", "sd", "xd", "bt"):
            self.sems[name] = stk.enter_context(nc.semaphore("sem_" + name))
            self.cnt[name] = 0

        self.gatetoks = []
        self.pending = {}
        self.issuer = {"gd": "g", "yd": "y", "sd": "s", "xd": "y"}

    def flush(self, sem):
        en = self.issuer[sem]
        val = self.cnt[sem]
        for tok in self.pending.get(sem, ()):
            tok[1] = val
        self.pending[sem] = []
        if val > 0 and self.waited.get((en, sem), -1) < val:
            self.eng[en].wait_ge(self.sems[sem], val)
            self.waited[(en, sem)] = val

    def gate(self):
        for sem in list(self.pending):
            if self.pending[sem]:
                self.flush(sem)
        self.gatetoks = [[sem, self.cnt[sem], "-"] for sem in self.sems
                         if self.cnt[sem] > 0]

    def b(self, *key):
        return self.bufs.setdefault(key, [None, []])

    def _wait(self, en, tok, tsem=None):
        if tok is None:
            return
        sem, val, src = tok
        if sem == tsem and sem in self.issuer:
            return
        if val is None:
            raise RuntimeError(f"consumer on {en} before flush of lazy {sem}")
        k = (en, sem)
        if self.waited.get(k, -1) >= val:
            return
        self.eng[en].wait_ge(self.sems[sem], val)
        self.waited[k] = val

    def emit(self, en, fn, reads=(), writes=(), rw=(), dma=False, xp=False,
             lazy=False):
        reads = [self.b(*k) for k in reads]
        writes = [self.b(*k) for k in writes]
        rw = [self.b(*k) for k in rw]
        tsem = "xd" if xp else (en + "d") if dma else None
        for buf in reads:
            self._wait(en, buf[0], tsem)
        for buf in writes + rw:
            if buf[0] is None and not buf[1]:
                for t in self.gatetoks:
                    self._wait(en, t)
            self._wait(en, buf[0], tsem)
            for t in buf[1]:
                self._wait(en, t, tsem)
        inst = fn()
        sem = "xd" if xp else (en + ("d" if dma else ""))
        n = 16 if (dma or xp) else 1
        self.cnt[sem] += n
        inst.then_inc(self.sems[sem], n)
        if (dma or xp) and not lazy:
            self.eng[en].wait_ge(self.sems[sem], self.cnt[sem])
            self.waited[(en, sem)] = self.cnt[sem]
        tok = [sem, None if ((dma or xp) and lazy) else self.cnt[sem],
               "-" if (dma or xp) else en]
        if (dma or xp) and lazy:
            self.pending.setdefault(sem, []).append(tok)
        for buf in reads:
            buf[1].append(tok)
        for buf in writes + rw:
            buf[0] = tok
            buf[1] = []
        return tok


def build_program():
    nc = bass.Bass("TRN2", target_bir_lowering=False, debug=False)
    stk = ExitStack()
    stk.__enter__()

    d_x = nc.dram_tensor("x", [S, D], F32, kind="ExternalInput").ap()
    d_xm = nc.dram_tensor("xmine", [MQ, D], F32, kind="ExternalInput").ap()
    d_wqkv = nc.dram_tensor("wqkv", [3 * D, D], F32, kind="ExternalInput").ap()
    d_wout = nc.dram_tensor("wout", [D, D], F32, kind="ExternalInput").ap()
    d_w1 = nc.dram_tensor("w1", [FF, D], F32, kind="ExternalInput").ap()
    d_w2 = nc.dram_tensor("w2", [D, FF], F32, kind="ExternalInput").ap()
    d_b1 = nc.dram_tensor("b1", [1, FF], F32, kind="ExternalInput").ap()
    d_gm = nc.dram_tensor("gamma", [1, FF], F32, kind="ExternalInput").ap()
    d_bt = nc.dram_tensor("beta", [1, FF], F32, kind="ExternalInput").ap()
    d_b2 = nc.dram_tensor("b2", [1, D], F32, kind="ExternalInput").ap()
    d_m0 = nc.dram_tensor("mask0", [128, 128], BF16, kind="ExternalInput").ap()
    d_m1 = nc.dram_tensor("mask1", [128, 128], BF16, kind="ExternalInput").ap()
    d_out = nc.dram_tensor("out", [MQ, D], F32, kind="ExternalOutput").ap()
    spill_kind = "ExternalOutput" if DEBUG else "Internal"
    d_kx = nc.dram_tensor("kx_spill", [MT, 128, NT * 128], BF16,
                          kind=spill_kind).ap()
    d_y = nc.dram_tensor("y_spill", [128, H, MQ], BF16, kind=spill_kind).ap()
    if DEBUG:
        d_dbg = nc.dram_tensor("dbg", [128, 256], F32,
                               kind="ExternalOutput").ap()
        d_dbg2 = nc.dram_tensor("dbg2", [128, 1600], F32,
                                kind="ExternalOutput").ap()
        d_dxn = nc.dram_tensor("dbg_xn", [128, NT * S], BF16,
                               kind="ExternalOutput").ap()

    p = P(nc, stk)
    E = p.eng
    BOOT = 1 << 24
    for _sn in ("bt", "t", "v", "s", "g", "y", "gd", "yd", "sd", "xd"):
        nc.gpsimd.sem_clear(p.sems[_sn])
    nc.gpsimd.sem_inc(p.sems["bt"], BOOT)
    for _e in ("t", "v", "s", "y"):
        p.eng[_e].wait_ge(p.sems["bt"], BOOT)

    def sball(st, name, shape, dt):
        return st.enter_context(nc.sbuf_tensor(name, shape, dt))

    ones_c_bf = sball(stk, "ones_c_bf", [128, 1], BF16)
    ones_c_f = sball(stk, "ones_c_f", [128, 1], F32)
    ones_r_f = sball(stk, "ones_r_f", [1, 128], F32)
    ident_bf = sball(stk, "ident_bf", [128, 128], BF16)
    m0 = sball(stk, "m0", [128, 128], BF16)
    m1 = sball(stk, "m1", [128, 128], BF16)
    snegrow = sball(stk, "snegrow", [1, MQ], BF16)
    colp = sball(stk, "colp", [128, 16 * MT], F32)
    (COL_SNEG, COL_R1, COL_R2, COL_M2, COL_A, COL_BB, COL_RS, COL_MS2,
     COL_S, COL_S2) = range(10)

    def col(w, mt):
        return colp[:, w * MT + mt: w * MT + mt + 1]

    def ck_(w, mt):
        return ("col", w, mt)

    thbc = sball(stk, "thbc", [128, 6], F32)
    TH1, NTH1, TH2, NTH2, MEAN1, MEAN2 = range(6)

    def thc(i):
        return thbc[:, i:i + 1]

    scal = sball(stk, "scal", [1, 32], F32)
    (SC_MQ2, SC_MK2, SC_NEGC, SC_T0, SC_T1, SC_NR1, SC_NR2) = range(7)
    SC_ROW = 16

    def sc(i):
        return scal[0:1, i: i + 1]

    def sk(i):
        return ("scal", i)

    negc_bc = sball(stk, "negc_bc", [128, 4], F32)
    acc1 = sball(stk, "acc1", [128, 1], F32)
    acc2 = sball(stk, "acc2", [128, 1], F32)
    accs = sball(stk, "accs", [128, 1], F32)
    stats1 = sball(stk, "stats1", [128, MT * FC1 * 2], F32)
    ln_sc = sball(stk, "ln_sc", [128, 10], F32)

    def lc(i):
        return ln_sc[:, i:i + 1]

    def lk(i):
        return ("lnc", i)

    tiny_bf = sball(stk, "tiny_bf", [128, 1], BF16)
    eps_col = sball(stk, "eps_col", [128, 1], F32)
    SK = [("scr", i) for i in range(4)]
    scr_bf = [sball(stk, f"scrbf{i}", [128, 512], BF16) for i in range(2)]
    SB = [("scrbf", i) for i in range(2)]

    p.emit("v", lambda: E["v"].memset(ones_c_bf[:, :], 1.0), writes=[("c", 1)])
    p.emit("v", lambda: E["v"].memset(ones_c_f[:, :], 1.0), writes=[("c", 2)])
    p.emit("v", lambda: E["v"].memset(ones_r_f[:, :], 1.0), writes=[("c", 3)])
    p.emit("g", lambda: E["g"].memset(ident_bf[:, :], 1.0), writes=[("c", 5)])
    p.emit("g", lambda: E["g"].affine_select(
        ident_bf[:, :], ident_bf[:, :], pattern=[[1, 128]],
        compare_op=OP.is_equal, fill=0.0, base=0, channel_multiplier=-1),
        rw=[("c", 5)])
    p.emit("g", lambda: E["g"].dma_start(out=m0[:, :], in_=d_m0[:, :]),
           writes=[("c", 6)], dma=True)
    p.emit("g", lambda: E["g"].dma_start(out=m1[:, :], in_=d_m1[:, :]),
           writes=[("c", 7)], dma=True)
    p.emit("v", lambda: E["v"].memset(eps_col[:, :], float(EPS)),
           writes=[("c", 8)])
    p.emit("v", lambda: E["v"].memset(acc1[:, :], 0.0), writes=[("acc1",)])
    p.emit("v", lambda: E["v"].memset(acc2[:, :], 0.0), writes=[("acc2",)])

    def ln_tile(xkey, xtile):
        p.emit("v", lambda: E["v"].reduce_sum(lc(3), xtile[:, :], axis=AX),
               reads=[xkey], writes=[lk(3)])
        def _sq(xtile=xtile):
            for c in range(4):
                E["s"].activation(sqdump[:, c * 512:(c + 1) * 512],
                                  xtile[:, c * 512:(c + 1) * 512], AF.Square,
                                  accum_out=stats[:, c:c + 1])
            E["s"].drain()
            return E["s"].nop()
        p.emit("s", _sq, reads=[xkey], writes=[("sqdump",), ("stats",)])
        p.emit("v", lambda: E["v"].reduce_sum(lc(5), stats[:, 0:4], axis=AX),
               reads=[("stats",)], writes=[lk(5)])
        p.emit("v", lambda: E["v"].tensor_scalar_mul(lc(0), lc(3), 1.0 / D),
               reads=[lk(3)], writes=[lk(0)])
        p.emit("v", lambda: E["v"].tensor_mul(lc(6), lc(0), lc(0)),
               reads=[lk(0)], writes=[lk(6)])
        p.emit("v", lambda: E["v"].scalar_tensor_tensor(
            lc(9), lc(5), 1.0 / D, lc(6), OP.mult, OP.subtract),
            reads=[lk(5), lk(6)], writes=[lk(9)])
        p.emit("s", lambda: E["s"].activation(
            lc(2), lc(9), AF.Sqrt, bias=eps_col[:, :], scale=1.0),
            reads=[lk(9), ("c", 8)], writes=[lk(2)])
        p.emit("v", lambda: E["v"].reciprocal(lc(1), lc(2)),
               reads=[lk(2)], writes=[lk(1)])
        p.emit("v", lambda: E["v"].scalar_tensor_tensor(
            lc(2), lc(0), -1.0, lc(1), OP.mult, OP.mult),
            reads=[lk(0), lk(1)], rw=[lk(2)])
        return lk(1), lk(2)

    stkA = ExitStack()
    stkA.__enter__()
    xnallT = sball(stkA, "xnallT", [128, NT * S], BF16)
    xnmT = sball(stkA, "xnmT", [128, NT * MQ], BF16)


    stk0 = ExitStack()
    stk0.__enter__()
    xs = [sball(stk0, f"xs{i}", [128, D], F32) for i in range(2)]
    xnbf = [sball(stk0, f"xnbf{i}", [128, D], BF16) for i in range(2)]
    kxs = sball(stk0, "kxs", [128, D], BF16)
    kxT = sball(stk0, "kxT", [128, NT * 128], BF16)
    sqdump = sball(stk0, "sqdump", [128, D], BF16)
    scr = [sball(stk0, f"scr0_{i}", [128, 512], F32) for i in range(4)]
    stats = sball(stk0, "stats", [128, 40], F32)
    scr_i = sball(stk0, "scr_i0", [128, 512], I32)

    for kt in range(NT + (0 if S0_SKIP_MINE else MT)):
        mine = kt >= NT
        mt = kt - NT
        i = kt % 2
        if not mine:
            p.emit("y", lambda kt=kt, i=i: E["y"].dma_start(
                out=xs[i][:, :], in_=d_x[kt * 128:(kt + 1) * 128, :]),
                writes=[("xs", i)], dma=True)
        else:
            p.emit("y", lambda mt=mt, i=i: E["y"].dma_start(
                out=xs[i][:, :], in_=d_xm[mt * 128:(mt + 1) * 128, :]),
                writes=[("xs", i)], dma=True)
            p.emit("g", lambda mt=mt, i=i: E["g"].dma_start(
                out=d_out[mt * 128:(mt + 1) * 128, :], in_=xs[i][:, :]),
                reads=[("xs", i)], writes=[("out", mt)], dma=True)
        t_rstd, t_ms = ln_tile(("xs", i), xs[i])
        p.emit("s", lambda i=i: E["s"].activation(
            xnbf[i][:, :], xs[i][:, :], AF.Identity, bias=lc(2), scale=lc(1)),
            reads=[("xs", i), t_rstd, t_ms], writes=[("xnbf", i)])
        if not mine:
            if not S0_NO_XBAR:
                dst = xnallT[:, :].rearrange("q (c t) -> q c t", t=S)[
                    :, :, kt * 128:(kt + 1) * 128]
                p.emit("y", lambda i=i, dst=dst: E["y"].dma_start_transpose(
                    dst, xnbf[i][:, :]),
                    reads=[("xnbf", i)], rw=[("xnT",)], xp=True)
            continue
        if not S0_NO_XBAR:
            dstm = xnmT[:, :].rearrange("q (c t) -> q c t", t=MQ)[
                :, :, mt * 128:(mt + 1) * 128]
            p.emit("y", lambda i=i, dstm=dstm: E["y"].dma_start_transpose(
                dstm, xnbf[i][:, :]),
                reads=[("xnbf", i)], rw=[("xnmT",)], xp=True)
        if DEBUG and mt == 0:
            p.emit("y", lambda: E["y"].dma_start(
                out=d_dbg2[:, 512:522], in_=ln_sc[:, :]),
                reads=[lk(0), lk(1), lk(2)], writes=[("dbgx", 1)], dma=True)
            p.emit("y", lambda: E["y"].dma_start(
                out=d_dbg2[:, 522:546], in_=stats[:, 0:24]),
                reads=[("stats",)], writes=[("dbgx", 2)], dma=True)
            p.emit("y", lambda i=i: E["y"].dma_start(
                out=d_dbg2[:, 1024:1536], in_=xs[i][:, 0:512]),
                reads=[("xs", i)], writes=[("dbgx", 3)], dma=True)
        for c in range(4):
            p.emit("v", lambda i=i, c=c: E["v"].tensor_scalar(
                scr[0][:, :], xs[i][:, c * 512:(c + 1) * 512],
                lc(1), lc(2), OP.mult, OP.add),
                reads=[("xs", i), t_rstd, t_ms], writes=[SK[0]])
            if DEBUG and mt == 0 and c == 0:
                p.emit("y", lambda: E["y"].dma_start(
                    out=d_dbg2[:, 0:512], in_=scr[0][:, :]),
                    reads=[SK[0]], writes=[("dbgx", 0)], dma=True)
            p.emit("v", lambda: E["v"].reduce_max(
                lc(3), scr[0][:, :], axis=AX, apply_absolute_value=True),
                reads=[SK[0]], writes=[lk(3)])
            if c == 0:
                p.emit("v", lambda: E["v"].tensor_copy(lc(4), lc(3)),
                       reads=[lk(3)], writes=[lk(4)])
            else:
                p.emit("v", lambda: E["v"].tensor_max(lc(4), lc(4), lc(3)),
                       reads=[lk(3)], rw=[lk(4)])
        p.emit("v", lambda: E["v"].tensor_scalar_max(lc(5), lc(4), float(EPS)),
               reads=[lk(4)], writes=[lk(5)])
        p.emit("v", lambda: E["v"].reciprocal(lc(6), lc(5)),
               reads=[lk(5)], writes=[lk(6)])
        p.emit("v", lambda mt=mt: E["v"].tensor_scalar_mul(
            col(COL_S, mt), lc(6), 127.0),
            reads=[lk(6)], writes=[ck_(COL_S, mt)])
        p.emit("v", lambda mt=mt: E["v"].tensor_scalar_mul(
            col(COL_SNEG, mt), col(COL_S, mt), -1.0),
            reads=[ck_(COL_S, mt)], writes=[ck_(COL_SNEG, mt)])
        p.emit("v", lambda mt=mt: E["v"].tensor_mul(
            col(COL_RS, mt), lc(1), col(COL_S, mt)),
            reads=[ck_(COL_S, mt), t_rstd], writes=[ck_(COL_RS, mt)])
        p.emit("v", lambda mt=mt: E["v"].tensor_mul(
            col(COL_MS2, mt), lc(2), col(COL_S, mt)),
            reads=[ck_(COL_S, mt), t_ms], writes=[ck_(COL_MS2, mt)])
        if DEBUG:
            p.emit("v", lambda mt=mt: E["v"].tensor_copy(
                colp[:, 80 + 5 * mt: 85 + 5 * mt], ln_sc[:, 0:5]),
                reads=[lk(0), lk(1), lk(4)], writes=[("dbgst", mt)])
        for c in range(4):
            p.emit("v", lambda i=i, c=c, mt=mt: E["v"].tensor_scalar(
                scr[0][:, :], xs[i][:, c * 512:(c + 1) * 512],
                col(COL_RS, mt), col(COL_MS2, mt), OP.mult, OP.add),
                reads=[("xs", i), ck_(COL_RS, mt), ck_(COL_MS2, mt)],
                writes=[SK[0]])
            p.emit("v", lambda: E["v"].tensor_copy(scr_i[:, :], scr[0][:, :]),
                   reads=[SK[0]], writes=[("scri",)])
            p.emit("v", lambda: E["v"].tensor_copy(scr[1][:, :], scr_i[:, :]),
                   reads=[("scri",)], writes=[SK[1]])
            p.emit("g", lambda: E["g"].tensor_sub(
                scr[2][:, :], scr[0][:, :], scr[1][:, :]),
                reads=[SK[0], SK[1]], writes=[SK[2]])
            p.emit("v", lambda: E["v"].scalar_tensor_tensor(
                scr[3][:, :], scr[2][:, :], 0.5, scr[1][:, :],
                OP.is_ge, OP.add),
                reads=[SK[2], SK[1]], writes=[SK[3]])
            p.emit("v", lambda c=c: E["v"].scalar_tensor_tensor(
                kxs[:, c * 512:(c + 1) * 512], scr[2][:, :], -0.5, scr[3][:, :],
                OP.is_le, OP.subtract),
                reads=[SK[2], SK[3]], rw=[("kxs",)])
        if not S0_NO_XBAR:
            dstk = kxT[:, :].rearrange("q (c t) -> q c t", t=128)
            p.emit("y", lambda dstk=dstk: E["y"].dma_start_transpose(
                dstk, kxs[:, :]),
                reads=[("kxs",)], writes=[("kxT",)], xp=True)
            p.emit("y", lambda mt=mt: E["y"].dma_start(
                out=d_kx[mt, :, :], in_=kxT[:, :]),
                reads=[("kxT",)], writes=[("kxd", mt)], dma=True, lazy=True)
            p.flush("yd")

    stkT = ExitStack()
    stkT.__enter__()
    ps_tr = stkT.enter_context(nc.psum_tensor("ps_tr", [1, 128], BF16))
    for mt in range(0 if S0_SKIP_MINE else MT):
        p.emit("v", lambda mt=mt: E["v"].tensor_copy(
            tiny_bf[:, :], col(COL_SNEG, mt)),
            reads=[ck_(COL_SNEG, mt)], writes=[("tinybf",)])
        p.emit("t", lambda: E["t"].transpose(
            ps_tr[:, :], tiny_bf[:, :], ident_bf[:, :]),
            reads=[("tinybf",), ("c", 5)], writes=[("PStr",)])
        p.emit("v", lambda mt=mt: E["v"].tensor_copy(
            snegrow[0:1, mt * 128:(mt + 1) * 128], ps_tr[:, :]),
            reads=[("PStr",)], rw=[("snegrow",)])
    stkT.__exit__(None, None, None)
    if STOP_AFTER_S0:
        if DEBUG:
            p.emit("y", lambda: E["y"].dma_start(out=d_dxn[:, :],
                   in_=xnallT[:, :]), reads=[("xnT",)], writes=[("dbg1",)],
                   dma=True)
            p.emit("y", lambda: E["y"].dma_start(out=d_dbg[:, 0:128],
                   in_=colp[:, :], allow_slow_non_contiguous=True),
                   reads=[ck_(COL_S, 0)], writes=[("dbg2",)], dma=True)
        for _sn in ("gd", "yd", "sd", "xd", "t", "v", "s", "y"):
            E["g"].wait_ge(p.sems[_sn], p.cnt[_sn])
        for _sn in ("bt", "t", "v", "s", "g", "y", "gd", "yd", "sd", "xd"):
            nc.gpsimd.sem_clear(p.sems[_sn])
        stk0.__exit__(None, None, None)
        stkA.__exit__(None, None, None)
        stk.__exit__(None, None, None)
        return nc
    stk0.__exit__(None, None, None)
    p.gate()

    stk12 = ExitStack()
    stk12.__enter__()
    QT = sball(stk12, "QT", [128, HG * MQ], BF16)
    KT = sball(stk12, "KT", [128, HG * S], BF16)
    Vt = sball(stk12, "Vt", [128, NT * 512], BF16)
    wqT = sball(stk12, "wqT", [128, NT * 128], BF16)
    wkT = sball(stk12, "wkT", [128, NT * 128], BF16)
    wvT = sball(stk12, "wvT", [128, NT * 512], BF16)
    wstage = [sball(stk12, "wstage0", [128, D], BF16)] * 2
    abuf = [sball(stk12, "abuf0", [128, 1024], BF16)] * 2
    Pb = [sball(stk12, f"Pb{i}", [128, MQ], BF16) for i in range(2)]
    sbc = [sball(stk12, f"sbc{i}", [128, 512], BF16) for i in range(2)]
    sqs = sball(stk12, "sqs", [128, MQ], BF16)
    srec = sball(stk12, "srec", [1, MQ], F32)
    absdump = sqs[:, 0:512]

    stkP = ExitStack()
    stkP.__enter__()
    PS4 = [stkP.enter_context(nc.psum_tensor(f"ps4_{i}", [128, 512], F32))
           for i in range(4)]
    ps_y = [stkP.enter_context(nc.psum_tensor(f"ps_y{i}", [128, 512], F32))
            for i in range(2)]
    ps_s = [stkP.enter_context(nc.psum_tensor(f"ps_s{i}", [1, 512], F32))
            for i in range(2)]
    K4 = [("PS4", i) for i in range(4)]
    KY = [("PSy", i) for i in range(2)]
    KS = [("PSs", i) for i in range(2)]

    absq = ([("w1", ch, h_) for ch in range(FF // 128) for h_ in range(2)] +
            [("w2", ch, fs) for ch in range(D // 128) for fs in range(8)])
    nab = [0]

    def emit_absmean(n):
        while n > 0 and absq:
            item = absq.pop(0)
            n -= 1
            i = nab[0] % 2
            nab[0] += 1
            if item[0] == "w1":
                ch, h_ = item[1], item[2]
                p.emit("g", lambda ch=ch, i=i, h_=h_: E["g"].dma_start(
                    out=abuf[i][:, :],
                    in_=d_w1[ch * 128:(ch + 1) * 128,
                             h_ * 1024:(h_ + 1) * 1024]),
                    writes=[("ab", 0)], dma=True)
                for q_ in range(2):
                    p.emit("s", lambda i=i, q_=q_: E["s"].activation(
                        absdump[:, :], abuf[i][:, q_ * 512:(q_ + 1) * 512],
                        AF.Abs, accum_out=accs[:, :]),
                        reads=[("ab", 0)], writes=[("accs",)], rw=[("sqs",)])
                    p.emit("v", lambda: E["v"].tensor_add(
                        acc1[:, :], acc1[:, :], accs[:, :]),
                        reads=[("accs",)], rw=[("acc1",)])
            else:
                ch, fs = item[1], item[2]
                p.emit("g", lambda ch=ch, fs=fs, i=i: E["g"].dma_start(
                    out=abuf[i][:, :],
                    in_=d_w2[ch * 128:(ch + 1) * 128,
                             fs * 1024:(fs + 1) * 1024]),
                    writes=[("ab", 0)], dma=True)
                p.emit("v", lambda i=i: E["v"].reduce_sum(
                    accs[:, :], abuf[i][:, :], axis=AX,
                    apply_absolute_value=True),
                    reads=[("ab", 0)], writes=[("accs",)])
                p.emit("v", lambda: E["v"].tensor_add(
                    acc2[:, :], acc2[:, :], accs[:, :]),
                    reads=[("accs",)], rw=[("acc2",)])

    nwq = [0]

    def wxpose(dram, row0, dstap, dkey):
        i = nwq[0] % 2
        nwq[0] += 1
        p.emit("g", lambda: E["g"].dma_start(
            out=wstage[i][:, :], in_=dram[row0:row0 + 128, :]),
            writes=[("ws", 0)], dma=True)
        p.emit("y", lambda: E["y"].dma_start_transpose(dstap, wstage[i][:, :]),
               reads=[("ws", 0)], writes=[dkey], xp=True)

    for hg in range(NGRP):
        for hl in range(HG):
            h = hg * HG + hl
            wq_dst = wqT[:, :].rearrange("q (c t) -> q c t", t=128)
            wk_dst = wkT[:, :].rearrange("q (c t) -> q c t", t=128)
            wv_dst = wvT[:, :].rearrange("q (c t) -> q c t", t=512)[
                :, :, hl * 128:(hl + 1) * 128]
            wxpose(d_wqkv, h * 128, wq_dst, ("wqT",))
            for qg in range(2):
                ps = PS4[qg]

                def mm_q(ps=ps, qg=qg):
                    inst = None
                    for d in range(16):
                        inst = E["t"].matmul(
                            ps[:, :], wqT[:, d * 128:(d + 1) * 128],
                            xnmT[:, d * MQ + qg * 512: d * MQ + (qg + 1) * 512],
                            start=(d == 0), stop=(d == 15))
                    return inst
                p.emit("t", mm_q, reads=[("wqT",), ("xnmT",)], writes=[K4[qg]])
                p.emit("v", lambda ps=ps, hl=hl, qg=qg: E["v"].tensor_copy(
                    QT[:, hl * MQ + qg * 512: hl * MQ + (qg + 1) * 512],
                    ps[:, :]),
                    reads=[K4[qg]], rw=[("QT", hl)])
            wxpose(d_wqkv, D + h * 128, wk_dst, ("wkT",))
            for kg in range(4):
                ps = PS4[kg]

                def mm_k(ps=ps, kg=kg):
                    inst = None
                    for d in range(16):
                        inst = E["t"].matmul(
                            ps[:, :], wkT[:, d * 128:(d + 1) * 128],
                            xnallT[:, d * S + kg * 512: d * S + (kg + 1) * 512],
                            start=(d == 0), stop=(d == 15))
                    return inst
                p.emit("t", mm_k, reads=[("wkT",), ("xnT",)], writes=[K4[kg]])
                p.emit("v", lambda ps=ps, hl=hl, kg=kg: E["v"].tensor_copy(
                    KT[:, hl * S + kg * 512: hl * S + (kg + 1) * 512],
                    ps[:, :]),
                    reads=[K4[kg]], rw=[("KT", hl)])
            wxpose(d_wqkv, 2 * D + h * 128, wv_dst, ("wvT", hl))
        for kt in range(NT):
            ps = PS4[kt % 4]

            def mm_v(ps=ps, kt=kt):
                inst = None
                for d in range(16):
                    inst = E["t"].matmul(
                        ps[:, :],
                        xnallT[:, d * S + kt * 128: d * S + (kt + 1) * 128],
                        wvT[:, d * 512:(d + 1) * 512],
                        start=(d == 0), stop=(d == 15))
                return inst
            p.emit("t", mm_v,
                   reads=[("xnT",), ("wvT", 0), ("wvT", 1), ("wvT", 2),
                          ("wvT", 3)],
                   writes=[K4[kt % 4]])
            p.emit("v", lambda ps=ps, kt=kt: E["v"].tensor_copy(
                Vt[:, kt * 512:(kt + 1) * 512], ps[:, :]),
                reads=[K4[kt % 4]], writes=[("Vt", kt)])

        CQ2, CK2, CPR, CNC = 7, 11, 22, 26
        for hl in range(HG):
            p.emit("s", lambda hl=hl: E["s"].activation(
                sqs[:, 0:MQ], QT[:, hl * MQ:(hl + 1) * MQ], AF.Square),
                reads=[("QT", hl)], writes=[("sqs",)])
            for qg in range(2):
                p.emit("t", lambda qg=qg: E["t"].matmul(
                    ps_s[qg][:, :], ones_c_bf[:, :],
                    sqs[:, qg * 512:(qg + 1) * 512], start=True, stop=True),
                    reads=[("sqs",), ("c", 1)], writes=[KS[qg]])
                p.emit("v", lambda qg=qg, hl=hl: E["v"].reduce_max(
                    sc(CQ2 + hl) if qg == 0 else sc(SC_T0),
                    ps_s[qg][:, :], axis=AX),
                    reads=[KS[qg]],
                    writes=[sk(CQ2 + hl) if qg == 0 else sk(SC_T0)])
            p.emit("v", lambda hl=hl: E["v"].tensor_max(
                sc(CQ2 + hl), sc(CQ2 + hl), sc(SC_T0)),
                reads=[sk(SC_T0)], rw=[sk(CQ2 + hl)])
            for kg in range(4):
                p.emit("s", lambda hl=hl, kg=kg: E["s"].activation(
                    sqs[:, 0:512],
                    KT[:, hl * S + kg * 512: hl * S + (kg + 1) * 512],
                    AF.Square),
                    reads=[("KT", hl)], rw=[("sqs",)])
                p.emit("t", lambda kg=kg: E["t"].matmul(
                    ps_s[kg % 2][:, :], ones_c_bf[:, :], sqs[:, 0:512],
                    start=True, stop=True),
                    reads=[("sqs",), ("c", 1)], writes=[KS[kg % 2]])
                p.emit("v", lambda kg=kg, hl=hl: E["v"].reduce_max(
                    sc(CK2 + hl) if kg == 0 else sc(SC_T0),
                    ps_s[kg % 2][:, :], axis=AX),
                    reads=[KS[kg % 2]],
                    writes=[sk(CK2 + hl) if kg == 0 else sk(SC_T0)])
                if kg:
                    p.emit("v", lambda hl=hl: E["v"].tensor_max(
                        sc(CK2 + hl), sc(CK2 + hl), sc(SC_T0)),
                        reads=[sk(SC_T0)], rw=[sk(CK2 + hl)])
            p.emit("v", lambda hl=hl: E["v"].tensor_mul(
                sc(CPR + hl), sc(CQ2 + hl), sc(CK2 + hl)),
                reads=[sk(CQ2 + hl), sk(CK2 + hl)], writes=[sk(CPR + hl)])
        p.emit("s", lambda: E["s"].activation(
            scal[0:1, CPR:CPR + 4], scal[0:1, CPR:CPR + 4], AF.Sqrt),
            rw=[sk(CPR + hl_) for hl_ in range(HG)])
        p.emit("v", lambda: E["v"].tensor_scalar_mul(
            scal[0:1, CNC:CNC + 4], scal[0:1, CPR:CPR + 4], -SCALE * 1.0001),
            reads=[sk(CPR + hl_) for hl_ in range(HG)],
            writes=[sk(CNC + hl_) for hl_ in range(HG)])
        for hl in range(HG):
            p.emit("t", lambda hl=hl: E["t"].matmul(
                PS4[3][:, 0:1], ones_r_f[:, :], sc(CNC + hl),
                start=True, stop=True),
                reads=[sk(CNC + hl), ("c", 3)], writes=[K4[3]])
            p.emit("v", lambda hl=hl: E["v"].tensor_copy(
                negc_bc[:, hl:hl + 1], PS4[3][:, 0:1]),
                reads=[K4[3]], writes=[("negc", hl)])

        for hl in range(HG):
            h = hg * HG + hl
            pieces = []
            for j in range(NT):
                qs = 128 * (j // 2)
                if qs < 512:
                    pieces.append((j, qs, qs, 512))
                    pieces.append((j, qs, 512, 1024))
                else:
                    pieces.append((j, qs, qs, 1024))

            def stmm(idx, hl=hl):
                j, qs, a, b2 = pieces[idx]
                w = b2 - a
                psi = idx % 2
                p.emit("t", lambda j=j, a=a, w=w, psi=psi, hl=hl:
                       E["t"].matmul(
                           PS4[psi][:, 0:w],
                           KT[:, hl * S + j * 128: hl * S + (j + 1) * 128],
                           QT[:, hl * MQ + a: hl * MQ + a + w],
                           start=True, stop=True),
                       reads=[("KT", hl), ("QT", hl)], writes=[K4[psi]])

            stmm(0)
            for idx in range(len(pieces)):
                j, qs, a, b2 = pieces[idx]
                w = b2 - a
                jb = j % 2
                ck = 0 if a < 512 else 1
                psi = idx % 2
                p.emit("s", lambda jb=jb, a=a, w=w, psi=psi, hl=hl:
                       E["s"].activation(
                           Pb[jb][:, a:a + w], PS4[psi][:, 0:w], AF.Exp,
                           bias=negc_bc[:, hl:hl + 1], scale=SCALE),
                       reads=[K4[psi], ("negc", hl)], rw=[("Pb", jb, ck)])
                if a == qs:
                    msk, mkey = (m0, 6) if j % 2 == 0 else (m1, 7)
                    p.emit("v", lambda jb=jb, qs=qs, msk=msk:
                           E["v"].tensor_mul(
                               Pb[jb][:, qs:qs + 128],
                               Pb[jb][:, qs:qs + 128], msk[:, :]),
                           reads=[("c", mkey)], rw=[("Pb", jb, ck)])
                if idx + 1 < len(pieces):
                    stmm(idx + 1)
                last = (j == 7) if ck == 0 else (j == 15)
                p.emit("t", lambda jb=jb, a=a, w=w, ck=ck, j=j, last=last:
                       E["t"].matmul(
                           ps_s[ck][:, a - ck * 512: a - ck * 512 + w],
                           ones_c_bf[:, :], Pb[jb][:, a:a + w],
                           start=(j == 0), stop=last,
                           skip_group_check=True),
                       reads=[("Pb", jb, ck), ("c", 1)], rw=[KS[ck]])
                p.emit("t", lambda jb=jb, a=a, w=w, ck=ck, j=j, hl=hl,
                       last=last: E["t"].matmul(
                           ps_y[ck][:, a - ck * 512: a - ck * 512 + w],
                           Vt[:, j * 512 + hl * 128:
                              j * 512 + (hl + 1) * 128],
                           Pb[jb][:, a:a + w],
                           start=(j == 0), stop=last,
                           skip_group_check=True),
                       reads=[("Pb", jb, ck), ("Vt", j)], rw=[KY[ck]])
            for ck in range(2):
                p.emit("v", lambda ck=ck: E["v"].reciprocal(
                    srec[0:1, ck * 512:(ck + 1) * 512], ps_s[ck][:, :]),
                    reads=[KS[ck]], rw=[("srec",)])
                p.emit("t", lambda ck=ck: E["t"].matmul(
                    PS4[2 + ck][:, :], ones_r_f[:, :],
                    srec[0:1, ck * 512:(ck + 1) * 512],
                    start=True, stop=True),
                    reads=[("srec",), ("c", 3)], writes=[K4[2 + ck]])
                p.emit("v", lambda ck=ck: E["v"].tensor_copy(
                    sbc[ck][:, :], PS4[2 + ck][:, :]),
                    reads=[K4[2 + ck]], writes=[("sbc", ck)])
                p.emit("v", lambda ck=ck: E["v"].tensor_mul(
                    scr_bf[ck][:, :], ps_y[ck][:, :], sbc[ck][:, :]),
                    reads=[KY[ck], ("sbc", ck)], writes=[SB[ck]])
                p.emit("y", lambda h=h, ck=ck: E["y"].dma_start(
                    out=d_y[:, h, ck * 512:(ck + 1) * 512],
                    in_=scr_bf[ck][:, :]),
                    reads=[SB[ck]], writes=[("ydone", h, ck)], dma=True,
                    lazy=True)
            p.flush("yd")
        emit_absmean(64)
    emit_absmean(10 ** 9)
    stkP.__exit__(None, None, None)
    stk12.__exit__(None, None, None)
    p.gate()

    stkM = ExitStack()
    stkM.__enter__()
    ps_m = stkM.enter_context(nc.psum_tensor("ps_m", [1, 8], F32))
    ps_b6 = stkM.enter_context(nc.psum_tensor("ps_b6", [128, 8], F32))
    for wi in range(2):
        accv = acc1 if wi == 0 else acc2
        acck = ("acc1",) if wi == 0 else ("acc2",)
        p.emit("t", lambda accv=accv, wi=wi: E["t"].matmul(
            ps_m[:, wi:wi + 1], ones_c_f[:, :], accv[:, :],
            start=True, stop=True),
            reads=[acck, ("c", 2)], writes=[("PSm", wi)])
        p.emit("v", lambda wi=wi: E["v"].tensor_scalar(
            sc(SC_T0), ps_m[:, wi:wi + 1], 1.0 / (FF * D), float(EPS),
            OP.mult, OP.max),
            reads=[("PSm", wi)], writes=[sk(SC_T0)])
        p.emit("v", lambda wi=wi: E["v"].tensor_copy(
            scal[0:1, SC_ROW + 4 + wi: SC_ROW + 5 + wi], sc(SC_T0)),
            reads=[sk(SC_T0)], writes=[("scal", SC_ROW + 4 + wi)])
        p.emit("v", lambda wi=wi: E["v"].tensor_scalar_mul(
            scal[0:1, SC_ROW + 2 * wi: SC_ROW + 2 * wi + 1], sc(SC_T0), 0.5),
            reads=[sk(SC_T0)], writes=[("scal", SC_ROW + 2 * wi)])
        p.emit("v", lambda wi=wi: E["v"].tensor_scalar_mul(
            scal[0:1, SC_ROW + 2 * wi + 1: SC_ROW + 2 * wi + 2],
            sc(SC_T0), -0.5),
            reads=[sk(SC_T0)], writes=[("scal", SC_ROW + 2 * wi + 1)])
        p.emit("v", lambda: E["v"].reciprocal(sc(SC_T1), sc(SC_T0)),
               reads=[sk(SC_T0)], writes=[sk(SC_T1)])
        p.emit("v", lambda wi=wi: E["v"].tensor_scalar_mul(
            sc(SC_NR1 if wi == 0 else SC_NR2), sc(SC_T1), -1.0),
            reads=[sk(SC_T1)],
            writes=[sk(SC_NR1 if wi == 0 else SC_NR2)])
    rowk = [("scal", SC_ROW + i) for i in range(6)]
    p.emit("t", lambda: E["t"].matmul(
        ps_b6[:, 0:6], ones_r_f[:, :], scal[0:1, SC_ROW:SC_ROW + 6],
        start=True, stop=True),
        reads=rowk + [("c", 3)], writes=[("PSb6",)])
    p.emit("v", lambda: E["v"].tensor_copy(thbc[:, :], ps_b6[:, 0:6]),
           reads=[("PSb6",)], writes=[("thbc",)])
    for mt in range(MT):
        p.emit("v", lambda mt=mt: E["v"].reciprocal(lc(0), col(COL_S, mt)),
               reads=[ck_(COL_S, mt)], writes=[lk(0)])
        p.emit("v", lambda mt=mt: E["v"].tensor_mul(
            col(COL_R1, mt), thc(MEAN1), lc(0)),
            reads=[lk(0), ("thbc",)], writes=[ck_(COL_R1, mt)])
    if DEBUG:
        p.emit("y", lambda: E["y"].dma_start(out=d_dxn[:, :], in_=xnallT[:, :]),
               reads=[("xnT",)], writes=[("dbg1",)], dma=True)
        p.emit("y", lambda: E["y"].dma_start(out=d_dbg[:, 0:128], in_=colp[:, :], allow_slow_non_contiguous=True),
               reads=[ck_(COL_S, 0)], writes=[("dbg2",)], dma=True)
        p.emit("y", lambda: E["y"].dma_start(out=d_dbg[:, 128:134], in_=thbc[:, :], allow_slow_non_contiguous=True),
               reads=[("thbc",)], writes=[("dbg3",)], dma=True)
        p.emit("y", lambda: E["y"].dma_start(out=d_dbg[0:1, 134:166], in_=scal[:, :]),
               reads=[sk(SC_NR1)], writes=[("dbg4",)], dma=True)
        p.emit("y", lambda: E["y"].dma_start(out=d_dbg[:, 166:167], in_=acc1[:, :], allow_slow_non_contiguous=True),
               reads=[("acc1",)], writes=[("dbg5",)], dma=True)
        p.emit("y", lambda: E["y"].dma_start(out=d_dbg[:, 167:168], in_=acc2[:, :], allow_slow_non_contiguous=True),
               reads=[("acc2",)], writes=[("dbg6",)], dma=True)
        p.emit("y", lambda: E["y"].dma_start(out=d_dbg[:, 168:169], in_=negc_bc[:, :], allow_slow_non_contiguous=True),
               reads=[("negc",)], writes=[("dbg7",)], dma=True)
    stkM.__exit__(None, None, None)
    stkA.__exit__(None, None, None)
    p.gate()

    stk3 = ExitStack()
    stk3.__enter__()
    woT = [sball(stk3, f"woT{i}", [128, NT * 512], BF16) for i in range(2)]
    yld = [sball(stk3, f"yld{i}", [128, NT * 128], BF16) for i in range(2)]
    w3stage = [sball(stk3, f"w3stage{i}", [128, D], BF16) for i in range(2)]
    stkP3 = ExitStack()
    stkP3.__enter__()
    PSO = [stkP3.enter_context(nc.psum_tensor(f"pso{i}", [128, 512], F32))
           for i in range(4)]
    ydk = [("ydone", h_, c_) for h_ in range(H) for c_ in range(2)]
    for eg in range(4):
        ib = eg % 2
        for sub in range(4):
            i = (eg * 4 + sub) % 2
            p.emit("g", lambda eg=eg, sub=sub, i=i: E["g"].dma_start(
                out=w3stage[i][:, :],
                in_=d_wout[eg * 512 + sub * 128:
                           eg * 512 + (sub + 1) * 128, :]),
                writes=[("w3s", i)], dma=True)
            dst = woT[ib][:, :].rearrange("q (c t) -> q c t", t=512)[
                :, :, sub * 128:(sub + 1) * 128]
            p.emit("y", lambda dst=dst, i=i: E["y"].dma_start_transpose(
                dst, w3stage[i][:, :]),
                reads=[("w3s", i)], rw=[("woT", ib)], xp=True)
        for mt in range(MT):
            im = mt % 2
            p.emit("y", lambda mt=mt, im=im: E["y"].dma_start(
                out=yld[im][:, :], in_=d_y[:, :, mt * 128:(mt + 1) * 128]),
                reads=ydk, writes=[("yld", im)], dma=True)
            ps = PSO[mt % 4]

            def mm_o(ps=ps, ib=ib, im=im):
                inst = None
                for d in range(16):
                    inst = E["t"].matmul(
                        ps[:, :], yld[im][:, d * 128:(d + 1) * 128],
                        woT[ib][:, d * 512:(d + 1) * 512],
                        start=(d == 0), stop=(d == 15))
                return inst
            p.emit("t", mm_o, reads=[("yld", im), ("woT", ib)],
                   writes=[("PSO", mt % 4)])
            p.emit("v", lambda ps=ps, mt=mt: E["v"].tensor_copy(
                scr[mt % 2][:, :], ps[:, :]),
                reads=[("PSO", mt % 4)], writes=[SK[mt % 2]])
            p.emit("g", lambda mt=mt, eg=eg: E["g"].dma_start(
                out=d_out[mt * 128:(mt + 1) * 128, eg * 512:(eg + 1) * 512],
                in_=scr[mt % 2][:, :], accum_op=OP.add),
                reads=[SK[mt % 2]], rw=[("out", mt)], dma=True)
    stkP3.__exit__(None, None, None)
    stk3.__exit__(None, None, None)
    p.gate()

    stkF = ExitStack()
    stkF.__enter__()
    garena = sball(stkF, "garena", [128, ARN * FF], BF16)
    gA = garena[:, :]
    gH = garena[:, :].bitcast(FP16)

    stk4 = ExitStack()
    stk4.__enter__()
    kxf = sball(stk4, "kxf", [128, MT * NT * 128], BF16)
    w1stage = [sball(stk4, f"w1stage{i}", [128, 512], F32)
               for i in range(2)]
    w1qT = [sball(stk4, f"w1qT{i}", [128, NT * 256], BF16) for i in range(2)]
    qpos = scr_bf[0][:, :]
    qneg = scr_bf[1][:, :]
    rowa = sball(stk4, "rowa4", [1, 256], F32)[0:1, :]
    rowbf3 = sball(stk4, "rowbf4", [1, 768], BF16)
    rowbf = [rowbf3[0:1, 0:256], rowbf3[0:1, 256:512], rowbf3[0:1, 512:768]]
    sqs4 = sball(stk4, "sqs44", [128, 256], F32)[:, :]
    stkP4 = ExitStack()
    stkP4.__enter__()
    PSH = [stkP4.enter_context(nc.psum_tensor(f"psh{i}", [128, 256], F32))
           for i in range(4)]

    def w1_rowa(fc):
        p.emit("s", lambda fc=fc: E["s"].dma_start(
            out=rowa, in_=d_b1[0:1, fc * 256:(fc + 1) * 256]),
            writes=[("rowa",)], dma=True, lazy=True)

    def w1_rowbf(fc):
        p.flush("sd")
        p.emit("v", lambda ib=fc % 2: E["v"].tensor_single_scalar(
            rowbf[ib], rowa, sc(SC_NR1), OP.mult),
            reads=[("rowa",), sk(SC_NR1)], writes=[("rowbf", fc % 2)])

    def w1_load_l(fc, l):
        fb, q = l // 4, l % 4
        r0 = fc * 256 + fb * 128
        p.emit("s", lambda r0=r0, q=q, l=l: E["s"].dma_start(
            out=w1stage[l % 2][:, :],
            in_=d_w1[r0:r0 + 128, q * 512:(q + 1) * 512]),
            writes=[("w1s", l % 2)], dma=True, lazy=True)

    def w1_quant_l(fc, l):
        ib = fc % 2
        fb, q = l // 4, l % 4
        p.flush("sd")
        p.emit("v", lambda l=l: E["v"].tensor_single_scalar(
            qpos[:, :], w1stage[l % 2][:, :], thc(TH1), OP.is_ge),
            reads=[("w1s", l % 2), ("thbc",)], writes=[("qp",)])
        p.emit("v", lambda l=l: E["v"].tensor_single_scalar(
            qneg[:, :], w1stage[l % 2][:, :], thc(NTH1), OP.is_le),
            reads=[("w1s", l % 2), ("thbc",)], writes=[("qn",)])
        p.emit("g", lambda: E["g"].tensor_sub(
            qpos[:, :], qneg[:, :], qpos[:, :]),
            reads=[("qn",)], rw=[("qp",)])
        dst = w1qT[ib][:, :].rearrange("q (c t) -> q c t", t=256)[
            :, 4 * q:4 * q + 4, fb * 128:(fb + 1) * 128]
        p.emit("y", lambda dst=dst: E["y"].dma_start_transpose(
            dst, qpos[:, :]),
            reads=[("qp",)], rw=[("w1qT", ib)], xp=True)

    def w1_prep(fc, step):
        if fc >= FC1:
            return
        if step == 0:
            w1_rowa(fc)
            w1_load_l(fc, 0)
            w1_load_l(fc, 1)
            w1_rowbf(fc)
        w1_quant_l(fc, step)
        if step + 2 < 8:
            w1_load_l(fc, step + 2)

    for mt in range(MT):
        p.emit("g", lambda mt=mt: E["g"].dma_start(
            out=kxf[:, mt * 2048:(mt + 1) * 2048], in_=d_kx[mt, :, :]),
            reads=[("kxd", mt)], writes=[("kxf", mt)], dma=True, lazy=True)
    p.flush("gd")
    for st in range(8):
        w1_prep(0, st)

    for fc in range(FC1):
        ib = fc % 2
        for mt in range(MT):
            ps = PSH[mt % 4]

            def mm_h(ps=ps, ib=ib, mt=mt):
                inst = None
                for d in range(16):
                    inst = E["t"].matmul(
                        ps[:, :],
                        kxf[:, mt * 2048 + d * 128: mt * 2048 + (d + 1) * 128],
                        w1qT[ib][:, d * 256:(d + 1) * 256],
                        start=(d == 0), stop=False,
                        skip_group_check=True)
                return inst
            p.emit("t", mm_h, reads=[("kxf", mt), ("w1qT", ib)],
                   rw=[("PSH", mt % 4)])
            p.emit("t", lambda ps=ps, mt=mt, ib=ib: E["t"].matmul(
                ps[:, :], snegrow[0:1, mt * 128:(mt + 1) * 128],
                rowbf[ib], start=False, stop=True,
                skip_group_check=True),
                reads=[("snegrow",), ("rowbf", ib)], rw=[("PSH", mt % 4)])
            gdst = gA[:, mt * FF + fc * 256: mt * FF + (fc + 1) * 256]
            p.emit("s", lambda ps=ps, mt=mt, gdst=gdst: E["s"].activation(
                gdst, ps[:, :], AF.Gelu, scale=col(COL_R1, mt)),
                reads=[("PSH", mt % 4), ck_(COL_R1, mt)],
                writes=[("ar", mt, fc // 2)])
            p.emit("v", lambda mt=mt, fc=fc, gdst=gdst: E["v"].reduce_sum(
                stats1[:, (mt * FC1 + fc) * 2:(mt * FC1 + fc) * 2 + 1],
                gdst, axis=AX),
                reads=[("ar", mt, fc // 2)], rw=[("stats1", mt)])
            p.emit("v", lambda gdst=gdst: E["v"].tensor_mul(
                sqs4, gdst, gdst),
                reads=[("ar", mt, fc // 2)], writes=[("sqs4",)])
            p.emit("v", lambda mt=mt, fc=fc: E["v"].reduce_sum(
                stats1[:, (mt * FC1 + fc) * 2 + 1:(mt * FC1 + fc) * 2 + 2],
                sqs4, axis=AX),
                reads=[("sqs4",)], rw=[("stats1", mt)])
            w1_prep(fc + 1, mt)
    stkP4.__exit__(None, None, None)
    stk4.__exit__(None, None, None)
    p.gate()

    stk56 = ExitStack()
    stk56.__enter__()
    s2negrow = sball(stk56, "s2negrow", [1, MQ], BF16)
    stk5 = ExitStack()
    stk5.__enter__()
    gp = sball(stk5, "gp", [128, 512], F32)
    scr_i = sball(stk5, "scr_i5", [128, 512], I32)
    scr = [sball(stk5, f"scr5_{i}", [128, 512], F32) for i in range(4)]
    bp = sball(stk5, "bp", [128, 512], F32)
    rowg = sball(stk5, "rowg", [1, 512], F32)
    rowh = sball(stk5, "rowh", [1, 512], F32)
    stkP5 = ExitStack()
    stkP5.__enter__()
    PSB = [stkP5.enter_context(nc.psum_tensor(f"psb{i}", [128, 512], F32))
           for i in range(2)]
    ps_tr2 = stkP5.enter_context(nc.psum_tensor("ps_tr2", [1, 128], BF16))

    for mt in range(MT):
        base = mt * FC1 * 2
        p.emit("v", lambda base=base: E["v"].reduce_sum(
            lc(3), stats1[:, base:base + FC1 * 2:2], axis=AX),
            reads=[("stats1", mt)], writes=[lk(3)])
        p.emit("v", lambda base=base: E["v"].reduce_sum(
            lc(5), stats1[:, base + 1:base + FC1 * 2:2], axis=AX),
            reads=[("stats1", mt)], writes=[lk(5)])
        p.emit("v", lambda: E["v"].tensor_scalar_mul(lc(0), lc(3), 1.0 / FF),
               reads=[lk(3)], writes=[lk(0)])
        p.emit("v", lambda: E["v"].tensor_mul(lc(6), lc(0), lc(0)),
               reads=[lk(0)], writes=[lk(6)])
        p.emit("v", lambda: E["v"].scalar_tensor_tensor(
            lc(9), lc(5), 1.0 / FF, lc(6), OP.mult, OP.subtract),
            reads=[lk(5), lk(6)], writes=[lk(9)])
        p.emit("s", lambda: E["s"].activation(
            lc(2), lc(9), AF.Sqrt, bias=eps_col[:, :], scale=1.0),
            reads=[lk(9), ("c", 8)], writes=[lk(2)])
        p.emit("v", lambda mt=mt: E["v"].reciprocal(col(COL_A, mt), lc(2)),
               reads=[lk(2)], writes=[ck_(COL_A, mt)])
        p.emit("v", lambda mt=mt: E["v"].scalar_tensor_tensor(
            col(COL_BB, mt), lc(0), -1.0, col(COL_A, mt), OP.mult, OP.mult),
            reads=[lk(0), ck_(COL_A, mt)], writes=[ck_(COL_BB, mt)])

    for fc in range(16):
        p.emit("y", lambda fc=fc: E["y"].dma_start(
            out=rowg[0:1, :], in_=d_gm[0:1, fc * 512:(fc + 1) * 512]),
            writes=[("rowg",)], dma=True)
        p.emit("y", lambda fc=fc: E["y"].dma_start(
            out=rowh[0:1, :], in_=d_bt[0:1, fc * 512:(fc + 1) * 512]),
            writes=[("rowh",)], dma=True)
        p.emit("t", lambda: E["t"].matmul(
            PSB[0][:, :], ones_r_f[:, :], rowg[0:1, :], start=True, stop=True),
            reads=[("rowg",), ("c", 3)], writes=[("PSB", 0)])
        p.emit("s", lambda: E["s"].activation(gp[:, :], PSB[0][:, :], AF.Copy),
               reads=[("PSB", 0)], writes=[("gp",)])
        p.emit("t", lambda: E["t"].matmul(
            PSB[1][:, :], ones_r_f[:, :], rowh[0:1, :], start=True, stop=True),
            reads=[("rowh",), ("c", 3)], writes=[("PSB", 1)])
        p.emit("s", lambda: E["s"].activation(bp[:, :], PSB[1][:, :], AF.Copy),
               reads=[("PSB", 1)], writes=[("bp",)])
        for mt in range(MT):
            gsl = gA[:, mt * FF + fc * 512: mt * FF + (fc + 1) * 512]
            hsl = gH[:, mt * FF + fc * 512: mt * FF + (fc + 1) * 512]
            ark = ("ar", mt, fc)
            p.emit("v", lambda gsl=gsl, mt=mt: E["v"].tensor_scalar(
                scr[0][:, :], gsl, col(COL_A, mt), col(COL_BB, mt),
                OP.mult, OP.add),
                reads=[ark, ck_(COL_A, mt), ck_(COL_BB, mt)], writes=[SK[0]])
            p.emit("g", lambda: E["g"].tensor_mul(
                scr[1][:, :], scr[0][:, :], gp[:, :]),
                reads=[SK[0], ("gp",)], writes=[SK[1]])
            p.emit("v", lambda hsl=hsl: E["v"].tensor_add(
                hsl, scr[1][:, :], bp[:, :]),
                reads=[SK[1], ("bp",)], rw=[ark])
            p.emit("v", lambda hsl=hsl: E["v"].reduce_max(
                lc(3), hsl, axis=AX, apply_absolute_value=True),
                reads=[ark], writes=[lk(3)])
            if fc == 0:
                p.emit("v", lambda mt=mt: E["v"].tensor_copy(
                    col(COL_M2, mt), lc(3)),
                    reads=[lk(3)], writes=[ck_(COL_M2, mt)])
            else:
                p.emit("v", lambda mt=mt: E["v"].tensor_max(
                    col(COL_M2, mt), col(COL_M2, mt), lc(3)),
                    reads=[lk(3)], rw=[ck_(COL_M2, mt)])
    for mt in range(MT):
        p.emit("v", lambda mt=mt: E["v"].tensor_scalar_max(
            lc(5), col(COL_M2, mt), float(EPS)),
            reads=[ck_(COL_M2, mt)], writes=[lk(5)])
        p.emit("v", lambda: E["v"].reciprocal(lc(6), lc(5)),
               reads=[lk(5)], writes=[lk(6)])
        p.emit("v", lambda mt=mt: E["v"].tensor_scalar_mul(
            col(COL_S2, mt), lc(6), 127.0),
            reads=[lk(6)], writes=[ck_(COL_S2, mt)])
        p.emit("v", lambda mt=mt: E["v"].tensor_mul(
            lc(7), thc(MEAN2), lc(5)),
            reads=[lk(5), ("thbc",)], writes=[lk(7)])
        p.emit("v", lambda mt=mt: E["v"].tensor_scalar_mul(
            col(COL_R2, mt), lc(7), 1.0 / 127.0),
            reads=[lk(7)], writes=[ck_(COL_R2, mt)])
        p.emit("v", lambda mt=mt: E["v"].tensor_scalar_mul(
            lc(8), col(COL_S2, mt), -1.0),
            reads=[ck_(COL_S2, mt)], writes=[lk(8)])
        p.emit("v", lambda: E["v"].tensor_copy(tiny_bf[:, :], lc(8)),
               reads=[lk(8)], writes=[("tinybf",)])
        p.emit("t", lambda: E["t"].transpose(
            ps_tr2[:, :], tiny_bf[:, :], ident_bf[:, :]),
            reads=[("tinybf",), ("c", 5)], writes=[("PStr2",)])
        p.emit("v", lambda mt=mt: E["v"].tensor_copy(
            s2negrow[0:1, mt * 128:(mt + 1) * 128], ps_tr2[:, :]),
            reads=[("PStr2",)], rw=[("s2negrow",)])

    for mt in range(MT - 1, -1, -1):
        for fc in range(16):
            hsl = gH[:, mt * FF + fc * 512: mt * FF + (fc + 1) * 512]
            ksl = gA[:, mt * FF + fc * 512: mt * FF + (fc + 1) * 512]
            ark = ("ar", mt, fc)
            p.emit("v", lambda hsl=hsl, mt=mt: E["v"].tensor_single_scalar(
                scr[0][:, :], hsl, col(COL_S2, mt), OP.mult),
                reads=[ark, ck_(COL_S2, mt)], writes=[SK[0]])
            p.emit("v", lambda: E["v"].tensor_copy(scr_i[:, :], scr[0][:, :]),
                   reads=[SK[0]], writes=[("scri",)])
            p.emit("v", lambda: E["v"].tensor_copy(scr[1][:, :], scr_i[:, :]),
                   reads=[("scri",)], writes=[SK[1]])
            p.emit("g", lambda: E["g"].tensor_sub(
                scr[2][:, :], scr[0][:, :], scr[1][:, :]),
                reads=[SK[0], SK[1]], writes=[SK[2]])
            p.emit("v", lambda: E["v"].scalar_tensor_tensor(
                scr[3][:, :], scr[2][:, :], 0.5, scr[1][:, :],
                OP.is_ge, OP.add),
                reads=[SK[2], SK[1]], writes=[SK[3]])
            p.emit("v", lambda ksl=ksl: E["v"].scalar_tensor_tensor(
                ksl, scr[2][:, :], -0.5, scr[3][:, :], OP.is_le, OP.subtract),
                reads=[SK[2], SK[3]], rw=[ark])
        src = gA[:, mt * FF:(mt + 1) * FF]
        dst = gA[:, (mt + 1) * FF:(mt + 2) * FF].rearrange(
            "q (c t) -> q c t", t=128)
        slot_reads = [("ar", mt, k) for k in range(16)]
        slot_writes = [("ar", mt + 1, k) for k in range(16)]
        p.emit("y", lambda src=src, dst=dst: E["y"].dma_start_transpose(
            dst, src),
            reads=slot_reads, writes=[("khT", mt)] + slot_writes, xp=True)
    stkP5.__exit__(None, None, None)
    stk5.__exit__(None, None, None)
    p.gate()

    stk6 = ExitStack()
    stk6.__enter__()
    w2stage = sball(stk6, "w2stage", [128, D], F32)
    scr = [sball(stk6, f"scr6_{i}", [128, 512], F32) for i in range(4)]
    q2p = sball(stk6, "q2p", [128, 512], BF16)
    q2n = sball(stk6, "q2n", [128, 512], BF16)
    w2qT = [sball(stk6, f"w2qT{i}", [128, 64 * 128], BF16) for i in range(2)]
    rowa2 = sball(stk6, "rowa2", [1, 128], F32)
    rowbf2 = [sball(stk6, f"rowbf2{i}", [1, 128], BF16) for i in range(2)]
    stkP6 = ExitStack()
    stkP6.__enter__()
    PS2 = [stkP6.enter_context(nc.psum_tensor(f"ps2_{i}", [128, 512], F32))
           for i in range(2)]
    khview = gA[:, :].rearrange("q (sl t) -> q sl t", t=FF)

    def w2_rowa(dc):
        p.emit("s", lambda dc=dc: E["s"].dma_start(
            out=rowa2[0:1, :], in_=d_b2[0:1, dc * 128:(dc + 1) * 128]),
            writes=[("rowa2",)], dma=True, lazy=True)

    def w2_rowbf(dc):
        p.flush("sd")
        p.emit("v", lambda ib=dc % 2: E["v"].tensor_single_scalar(
            rowbf2[ib][0:1, :], rowa2[0:1, :], sc(SC_NR2), OP.mult),
            reads=[("rowa2",), sk(SC_NR2)], writes=[("rowbf2", dc % 2)])

    def w2_load(dc, l):
        p.emit("s", lambda dc=dc, l=l: E["s"].dma_start(
            out=w2stage[:, :],
            in_=d_w2[dc * 128:(dc + 1) * 128, l * 2048:(l + 1) * 2048]),
            writes=[("w2s",)], dma=True, lazy=True)

    def w2_quant_q(dc, l, q):
        ib = dc % 2
        fs = l * 4 + q
        p.flush("sd")
        p.emit("v", lambda q=q: E["v"].tensor_single_scalar(
            q2p[:, :], w2stage[:, q * 512:(q + 1) * 512], thc(TH2), OP.is_ge),
            reads=[("w2s",), ("thbc",)], writes=[("q2p",)])
        p.emit("v", lambda q=q: E["v"].tensor_single_scalar(
            q2n[:, :], w2stage[:, q * 512:(q + 1) * 512], thc(NTH2), OP.is_le),
            reads=[("w2s",), ("thbc",)], writes=[("q2n",)])
        p.emit("g", lambda: E["g"].tensor_sub(
            q2p[:, :], q2n[:, :], q2p[:, :]),
            reads=[("q2n",)], rw=[("q2p",)])
        dst = w2qT[ib][:, :].rearrange("q (c t) -> q c t", t=128)[
            :, 4 * fs:4 * fs + 4, :]
        p.emit("y", lambda dst=dst: E["y"].dma_start_transpose(
            dst, q2p[:, :]),
            reads=[("q2p",)], rw=[("w2qT", ib)], xp=True)

    def w2_prep(dc, step):
        if dc >= 16:
            return
        if step == 0:
            w2_rowa(dc)
            w2_load(dc, 0)
            w2_rowbf(dc)
        elif step == 7:
            for q in range(4):
                w2_quant_q(dc, 3, q)
        else:
            l = (step - 1) // 2
            if step % 2 == 1:
                w2_quant_q(dc, l, 0)
                w2_quant_q(dc, l, 1)
            else:
                w2_quant_q(dc, l, 2)
                w2_quant_q(dc, l, 3)
                w2_load(dc, l + 1)

    for st in range(8):
        w2_prep(0, st)

    for dc in range(16):
        ib = dc % 2
        for pr in range(2):
            ps = PS2[pr]
            rhs = khview[:, 4 * pr + 1: 4 * pr + 5,
                         0:FF].rearrange("q sl (c t) -> q sl c t", t=128)
            khr = [("khT", 4 * pr + k) for k in range(4)]

            def mm_2(ps=ps, ib=ib, rhs=rhs):
                inst = None
                for f in range(64):
                    inst = E["t"].matmul(
                        ps[:, :], w2qT[ib][:, f * 128:(f + 1) * 128],
                        rhs[:, :, f, :], start=(f == 0), stop=False,
                        skip_group_check=True)
                return inst
            p.emit("t", mm_2, reads=khr + [("w2qT", ib)],
                   rw=[("PS2", pr)])
            p.emit("t", lambda ps=ps, pr=pr, ib=ib: E["t"].matmul(
                ps[:, :], rowbf2[ib][0:1, :],
                s2negrow[0:1, pr * 512:(pr + 1) * 512],
                start=False, stop=True, skip_group_check=True),
                reads=[("rowbf2", ib), ("s2negrow",)], rw=[("PS2", pr)])
            p.emit("v", lambda ps=ps: E["v"].tensor_copy(
                scr_bf[0][:, :], ps[:, :]),
                reads=[("PS2", pr)], writes=[SB[0]])
            dstb = scr_bf[1][:, :].rearrange("q (c t) -> q c t", t=128)
            p.emit("y", lambda dstb=dstb: E["y"].dma_start_transpose(
                dstb, scr_bf[0][:, :]),
                reads=[SB[0]], writes=[SB[1]], xp=True)
            for tb in range(4):
                mt = 4 * pr + tb
                p.emit("v", lambda tb=tb, mt=mt: E["v"].tensor_single_scalar(
                    scr[tb][:, 0:128], scr_bf[1][:, tb * 128:(tb + 1) * 128],
                    col(COL_R2, mt), OP.mult),
                    reads=[SB[1], ck_(COL_R2, mt)], writes=[SK[tb]])
                p.emit("g", lambda tb=tb, mt=mt, dc=dc: E["g"].dma_start(
                    out=d_out[mt * 128:(mt + 1) * 128,
                              dc * 128:(dc + 1) * 128],
                    in_=scr[tb][:, 0:128], accum_op=OP.add),
                    reads=[SK[tb]], rw=[("out", mt)], dma=True, lazy=True)
            p.flush("gd")
            for st in range(4):
                w2_prep(dc + 1, pr * 4 + st)
    stkP6.__exit__(None, None, None)
    stk6.__exit__(None, None, None)
    stk56.__exit__(None, None, None)
    stkF.__exit__(None, None, None)
    p.gate()

    for _sn in ("gd", "yd", "sd", "xd", "t", "v", "s", "y"):
        E["g"].wait_ge(p.sems[_sn], p.cnt[_sn])
    nc.gpsimd.sem_clear(p.sems["bt"])
    stk.__exit__(None, None, None)
    return nc


_NC_CACHE = [None]


def kernel(**inputs):
    x = np.ascontiguousarray(np.asarray(inputs["L0_attn_x"], np.float32))
    wqkv = np.ascontiguousarray(np.asarray(inputs["L0_attn_w_qkv"], np.float32))
    wout = np.ascontiguousarray(np.asarray(inputs["L0_attn_w_out"], np.float32))
    w1 = np.ascontiguousarray(np.asarray(inputs["L0_ffn_w1"], np.float32))
    b1 = np.ascontiguousarray(np.asarray(inputs["L0_ffn_b1"],
                                         np.float32).reshape(1, FF))
    gm = np.ascontiguousarray(np.asarray(inputs["L0_ffn_gamma"],
                                         np.float32).reshape(1, FF))
    bt = np.ascontiguousarray(np.asarray(inputs["L0_ffn_beta"],
                                         np.float32).reshape(1, FF))
    w2 = np.ascontiguousarray(np.asarray(inputs["L0_ffn_w2"], np.float32))
    b2 = np.ascontiguousarray(np.asarray(inputs["L0_ffn_b2"],
                                         np.float32).reshape(1, D))

    kr = np.arange(128)[:, None]
    qr = np.arange(128)[None, :]
    masks = {}
    for half in range(2):
        masks[half] = (
            np.ascontiguousarray(
                (kr <= 2 * qr + half).astype(ml_dtypes.bfloat16)),
            np.ascontiguousarray(
                (kr <= 2 * qr + half - 128).astype(ml_dtypes.bfloat16)))

    if _NC_CACHE[0] is None:
        _NC_CACHE[0] = build_program()
    nc = _NC_CACHE[0]

    in_maps = []
    for c in range(8):
        b, half = c // 2, c % 2
        in_maps.append({
            "x": x[b],
            "xmine": np.ascontiguousarray(x[b, half::2]),
            "wqkv": wqkv, "wout": wout, "w1": w1, "w2": w2,
            "b1": b1, "gamma": gm, "beta": bt, "b2": b2,
            "mask0": masks[half][0],
            "mask1": masks[half][1],
        })
    res = run_bass_kernel_spmd(nc, in_maps, core_ids=list(range(8)))
    out = np.empty((B, S, D), np.float32)
    for c in range(8):
        b, half = c // 2, c % 2
        out[b, half::2] = res.results[c]["out"]
    return out


# version 16
# speedup vs baseline: 1.0944x; 1.0944x over previous
import numpy as np
import ml_dtypes
import concourse.bass as bass
import concourse.mybir as mybir
from concourse.bass_utils import run_bass_kernel_spmd
from concourse.alu_op_type import AluOpType as OP
from contextlib import ExitStack

F32 = mybir.dt.float32
BF16 = mybir.dt.bfloat16
FP16 = mybir.dt.float16
I32 = mybir.dt.int32
AF = mybir.ActivationFunctionType
AX = mybir.AxisListType.X

B, S, D, FF, H, HD = 4, 2048, 2048, 8192, 16, 128
NT = S // 128
MT = 8
MQ = MT * 128
EPS = 1e-5
SCALE = float(1.0 / np.sqrt(HD))
HG = 4
NGRP = H // HG
FC1 = 32
ARN = 9
DEBUG = False
STOP_AFTER_S0 = False
S0_SKIP_MINE = False
S0_NO_XBAR = False


class P:
    def __init__(self, nc, stk):
        self.nc = nc
        self.eng = {"t": nc.tensor, "v": nc.vector, "s": nc.scalar,
                    "g": nc.gpsimd, "y": nc.sync}
        self.sems, self.cnt, self.waited, self.bufs = {}, {}, {}, {}
        for name in ("t", "v", "s", "g", "y", "gd", "yd", "sd", "xd", "bt"):
            self.sems[name] = stk.enter_context(nc.semaphore("sem_" + name))
            self.cnt[name] = 0

        self.gatetoks = []
        self.pending = {}
        self.issuer = {"gd": "g", "yd": "y", "sd": "s", "xd": "y"}

    def flush(self, sem):
        en = self.issuer[sem]
        val = self.cnt[sem]
        for tok in self.pending.get(sem, ()):
            tok[1] = val
        self.pending[sem] = []
        if val > 0 and self.waited.get((en, sem), -1) < val:
            self.eng[en].wait_ge(self.sems[sem], val)
            self.waited[(en, sem)] = val

    def gate(self):
        for sem in list(self.pending):
            if self.pending[sem]:
                self.flush(sem)
        self.gatetoks = [[sem, self.cnt[sem], "-"] for sem in self.sems
                         if self.cnt[sem] > 0]

    def b(self, *key):
        return self.bufs.setdefault(key, [None, []])

    def _wait(self, en, tok, tsem=None):
        if tok is None:
            return
        sem, val, src = tok
        if sem == tsem and sem in self.issuer:
            return
        if val is None:
            raise RuntimeError(f"consumer on {en} before flush of lazy {sem}")
        k = (en, sem)
        if self.waited.get(k, -1) >= val:
            return
        self.eng[en].wait_ge(self.sems[sem], val)
        self.waited[k] = val

    def emit(self, en, fn, reads=(), writes=(), rw=(), dma=False, xp=False,
             lazy=False):
        reads = [self.b(*k) for k in reads]
        writes = [self.b(*k) for k in writes]
        rw = [self.b(*k) for k in rw]
        tsem = "xd" if xp else (en + "d") if dma else None
        for buf in reads:
            self._wait(en, buf[0], tsem)
        for buf in writes + rw:
            if buf[0] is None and not buf[1]:
                for t in self.gatetoks:
                    self._wait(en, t)
            self._wait(en, buf[0], tsem)
            for t in buf[1]:
                self._wait(en, t, tsem)
        inst = fn()
        sem = "xd" if xp else (en + ("d" if dma else ""))
        n = 16 if (dma or xp) else 1
        self.cnt[sem] += n
        inst.then_inc(self.sems[sem], n)
        if (dma or xp) and not lazy:
            self.eng[en].wait_ge(self.sems[sem], self.cnt[sem])
            self.waited[(en, sem)] = self.cnt[sem]
        tok = [sem, None if ((dma or xp) and lazy) else self.cnt[sem],
               "-" if (dma or xp) else en]
        if (dma or xp) and lazy:
            self.pending.setdefault(sem, []).append(tok)
        for buf in reads:
            buf[1].append(tok)
        for buf in writes + rw:
            buf[0] = tok
            buf[1] = []
        return tok


def build_program():
    nc = bass.Bass("TRN2", target_bir_lowering=False, debug=False)
    stk = ExitStack()
    stk.__enter__()

    d_x = nc.dram_tensor("x", [S, D], F32, kind="ExternalInput").ap()
    d_xm = nc.dram_tensor("xmine", [MQ, D], F32, kind="ExternalInput").ap()
    d_wqkv = nc.dram_tensor("wqkv", [3 * D, D], F32, kind="ExternalInput").ap()
    d_wout = nc.dram_tensor("wout", [D, D], F32, kind="ExternalInput").ap()
    d_w1 = nc.dram_tensor("w1", [FF, D], F32, kind="ExternalInput").ap()
    d_w2 = nc.dram_tensor("w2", [D, FF], F32, kind="ExternalInput").ap()
    d_b1 = nc.dram_tensor("b1", [1, FF], F32, kind="ExternalInput").ap()
    d_gm = nc.dram_tensor("gamma", [1, FF], F32, kind="ExternalInput").ap()
    d_bt = nc.dram_tensor("beta", [1, FF], F32, kind="ExternalInput").ap()
    d_b2 = nc.dram_tensor("b2", [1, D], F32, kind="ExternalInput").ap()
    d_m0 = nc.dram_tensor("mask0", [128, 128], BF16, kind="ExternalInput").ap()
    d_m1 = nc.dram_tensor("mask1", [128, 128], BF16, kind="ExternalInput").ap()
    d_out = nc.dram_tensor("out", [MQ, D], F32, kind="ExternalOutput").ap()
    spill_kind = "ExternalOutput" if DEBUG else "Internal"
    d_kx = nc.dram_tensor("kx_spill", [MT, 128, NT * 128], BF16,
                          kind=spill_kind).ap()
    d_y = nc.dram_tensor("y_spill", [128, H, MQ], BF16, kind=spill_kind).ap()
    if DEBUG:
        d_dbg = nc.dram_tensor("dbg", [128, 256], F32,
                               kind="ExternalOutput").ap()
        d_dbg2 = nc.dram_tensor("dbg2", [128, 1600], F32,
                                kind="ExternalOutput").ap()
        d_dxn = nc.dram_tensor("dbg_xn", [128, NT * S], BF16,
                               kind="ExternalOutput").ap()

    p = P(nc, stk)
    E = p.eng
    BOOT = 1 << 24
    for _sn in ("bt", "t", "v", "s", "g", "y", "gd", "yd", "sd", "xd"):
        nc.gpsimd.sem_clear(p.sems[_sn])
    nc.gpsimd.sem_inc(p.sems["bt"], BOOT)
    for _e in ("t", "v", "s", "y"):
        p.eng[_e].wait_ge(p.sems["bt"], BOOT)

    def sball(st, name, shape, dt):
        return st.enter_context(nc.sbuf_tensor(name, shape, dt))

    ones_c_bf = sball(stk, "ones_c_bf", [128, 1], BF16)
    ones_c_f = sball(stk, "ones_c_f", [128, 1], F32)
    ones_r_f = sball(stk, "ones_r_f", [1, 128], F32)
    ident_bf = sball(stk, "ident_bf", [128, 128], BF16)
    m0 = sball(stk, "m0", [128, 128], BF16)
    m1 = sball(stk, "m1", [128, 128], BF16)
    snegrow = sball(stk, "snegrow", [1, MQ], BF16)
    colp = sball(stk, "colp", [128, 16 * MT], F32)
    (COL_SNEG, COL_R1, COL_R2, COL_M2, COL_A, COL_BB, COL_RS, COL_MS2,
     COL_S, COL_S2) = range(10)

    def col(w, mt):
        return colp[:, w * MT + mt: w * MT + mt + 1]

    def ck_(w, mt):
        return ("col", w, mt)

    thbc = sball(stk, "thbc", [128, 6], F32)
    TH1, NTH1, TH2, NTH2, MEAN1, MEAN2 = range(6)

    def thc(i):
        return thbc[:, i:i + 1]

    scal = sball(stk, "scal", [1, 32], F32)
    (SC_MQ2, SC_MK2, SC_NEGC, SC_T0, SC_T1, SC_NR1, SC_NR2) = range(7)
    SC_ROW = 16

    def sc(i):
        return scal[0:1, i: i + 1]

    def sk(i):
        return ("scal", i)

    negc_bc = sball(stk, "negc_bc", [128, 4], F32)
    acc1 = sball(stk, "acc1", [128, 1], F32)
    acc2 = sball(stk, "acc2", [128, 1], F32)
    accs = sball(stk, "accs", [128, 1], F32)
    stats1 = sball(stk, "stats1", [128, MT * FC1 * 2], F32)
    ln_sc = sball(stk, "ln_sc", [128, 10], F32)

    def lc(i):
        return ln_sc[:, i:i + 1]

    def lk(i):
        return ("lnc", i)

    tiny_bf = sball(stk, "tiny_bf", [128, 1], BF16)
    eps_col = sball(stk, "eps_col", [128, 1], F32)
    SK = [("scr", i) for i in range(4)]
    scr_bf = [sball(stk, f"scrbf{i}", [128, 512], BF16) for i in range(2)]
    SB = [("scrbf", i) for i in range(2)]

    p.emit("v", lambda: E["v"].memset(ones_c_bf[:, :], 1.0), writes=[("c", 1)])
    p.emit("v", lambda: E["v"].memset(ones_c_f[:, :], 1.0), writes=[("c", 2)])
    p.emit("v", lambda: E["v"].memset(ones_r_f[:, :], 1.0), writes=[("c", 3)])
    p.emit("g", lambda: E["g"].memset(ident_bf[:, :], 1.0), writes=[("c", 5)])
    p.emit("g", lambda: E["g"].affine_select(
        ident_bf[:, :], ident_bf[:, :], pattern=[[1, 128]],
        compare_op=OP.is_equal, fill=0.0, base=0, channel_multiplier=-1),
        rw=[("c", 5)])
    p.emit("g", lambda: E["g"].dma_start(out=m0[:, :], in_=d_m0[:, :]),
           writes=[("c", 6)], dma=True)
    p.emit("g", lambda: E["g"].dma_start(out=m1[:, :], in_=d_m1[:, :]),
           writes=[("c", 7)], dma=True)
    p.emit("v", lambda: E["v"].memset(eps_col[:, :], float(EPS)),
           writes=[("c", 8)])
    p.emit("v", lambda: E["v"].memset(acc1[:, :], 0.0), writes=[("acc1",)])
    p.emit("v", lambda: E["v"].memset(acc2[:, :], 0.0), writes=[("acc2",)])

    def ln_tile(xkey, xtile):
        p.emit("v", lambda: E["v"].reduce_sum(lc(3), xtile[:, :], axis=AX),
               reads=[xkey], writes=[lk(3)])
        def _sq(xtile=xtile):
            for c in range(4):
                E["s"].activation(sqdump[:, c * 512:(c + 1) * 512],
                                  xtile[:, c * 512:(c + 1) * 512], AF.Square,
                                  accum_out=stats[:, c:c + 1])
            E["s"].drain()
            return E["s"].nop()
        p.emit("s", _sq, reads=[xkey], writes=[("sqdump",), ("stats",)])
        p.emit("v", lambda: E["v"].reduce_sum(lc(5), stats[:, 0:4], axis=AX),
               reads=[("stats",)], writes=[lk(5)])
        p.emit("v", lambda: E["v"].tensor_scalar_mul(lc(0), lc(3), 1.0 / D),
               reads=[lk(3)], writes=[lk(0)])
        p.emit("v", lambda: E["v"].tensor_mul(lc(6), lc(0), lc(0)),
               reads=[lk(0)], writes=[lk(6)])
        p.emit("v", lambda: E["v"].scalar_tensor_tensor(
            lc(9), lc(5), 1.0 / D, lc(6), OP.mult, OP.subtract),
            reads=[lk(5), lk(6)], writes=[lk(9)])
        p.emit("s", lambda: E["s"].activation(
            lc(2), lc(9), AF.Sqrt, bias=eps_col[:, :], scale=1.0),
            reads=[lk(9), ("c", 8)], writes=[lk(2)])
        p.emit("v", lambda: E["v"].reciprocal(lc(1), lc(2)),
               reads=[lk(2)], writes=[lk(1)])
        p.emit("v", lambda: E["v"].scalar_tensor_tensor(
            lc(2), lc(0), -1.0, lc(1), OP.mult, OP.mult),
            reads=[lk(0), lk(1)], rw=[lk(2)])
        return lk(1), lk(2)

    stkA = ExitStack()
    stkA.__enter__()
    xnallT = sball(stkA, "xnallT", [128, NT * S], BF16)
    xnmT = sball(stkA, "xnmT", [128, NT * MQ], BF16)


    stk0 = ExitStack()
    stk0.__enter__()
    xs = [sball(stk0, f"xs{i}", [128, D], F32) for i in range(2)]
    xnbf = [sball(stk0, f"xnbf{i}", [128, D], BF16) for i in range(2)]
    kxs = sball(stk0, "kxs", [128, D], BF16)
    kxT = sball(stk0, "kxT", [128, NT * 128], BF16)
    sqdump = sball(stk0, "sqdump", [128, D], BF16)
    scr = [sball(stk0, f"scr0_{i}", [128, 512], F32) for i in range(4)]
    abuf = [sball(stk0, f"abuf{i}", [128, 1024], BF16) for i in range(2)]

    absq = ([("w1", ch, h_) for ch in range(FF // 128) for h_ in range(2)] +
            [("w2", ch, fs) for ch in range(D // 128) for fs in range(8)])
    nab = [0]

    def abs_load(item, i):
        if item[0] == "w1":
            ch, h_ = item[1], item[2]
            p.emit("g", lambda ch=ch, i=i, h_=h_: E["g"].dma_start(
                out=abuf[i][:, :],
                in_=d_w1[ch * 128:(ch + 1) * 128,
                         h_ * 1024:(h_ + 1) * 1024]),
                writes=[("ab", i)], dma=True, lazy=True)
        else:
            ch, fs = item[1], item[2]
            p.emit("g", lambda ch=ch, fs=fs, i=i: E["g"].dma_start(
                out=abuf[i][:, :],
                in_=d_w2[ch * 128:(ch + 1) * 128,
                         fs * 1024:(fs + 1) * 1024]),
                writes=[("ab", i)], dma=True, lazy=True)

    def emit_absmean(n):
        while n > 0 and absq:
            item = absq.pop(0)
            n -= 1
            i = nab[0] % 2
            if nab[0] == 0:
                abs_load(item, 0)
            if len(absq):
                abs_load(absq[0], (i + 1) % 2)
            p.flush("gd")
            nab[0] += 1
            acc = acc1 if item[0] == "w1" else acc2
            ak = ("acc1",) if item[0] == "w1" else ("acc2",)
            p.emit("v", lambda i=i: E["v"].reduce_sum(
                accs[:, :], abuf[i][:, :], axis=AX,
                apply_absolute_value=True),
                reads=[("ab", i)], writes=[("accs",)])
            p.emit("v", lambda acc=acc: E["v"].tensor_add(
                acc[:, :], acc[:, :], accs[:, :]),
                reads=[("accs",)], rw=[ak])

    stats = sball(stk0, "stats", [128, 40], F32)
    scr_i = sball(stk0, "scr_i0", [128, 512], I32)

    for kt in range(NT + (0 if S0_SKIP_MINE else MT)):
        mine = kt >= NT
        mt = kt - NT
        i = kt % 2
        if not mine:
            p.emit("y", lambda kt=kt, i=i: E["y"].dma_start(
                out=xs[i][:, :], in_=d_x[kt * 128:(kt + 1) * 128, :]),
                writes=[("xs", i)], dma=True)
        else:
            p.emit("y", lambda mt=mt, i=i: E["y"].dma_start(
                out=xs[i][:, :], in_=d_xm[mt * 128:(mt + 1) * 128, :]),
                writes=[("xs", i)], dma=True)
            p.emit("g", lambda mt=mt, i=i: E["g"].dma_start(
                out=d_out[mt * 128:(mt + 1) * 128, :], in_=xs[i][:, :]),
                reads=[("xs", i)], writes=[("out", mt)], dma=True)
        t_rstd, t_ms = ln_tile(("xs", i), xs[i])
        p.emit("s", lambda i=i: E["s"].activation(
            xnbf[i][:, :], xs[i][:, :], AF.Identity, bias=lc(2), scale=lc(1)),
            reads=[("xs", i), t_rstd, t_ms], writes=[("xnbf", i)])
        if not mine:
            if not S0_NO_XBAR:
                dst = xnallT[:, :].rearrange("q (c t) -> q c t", t=S)[
                    :, :, kt * 128:(kt + 1) * 128]
                p.emit("y", lambda i=i, dst=dst: E["y"].dma_start_transpose(
                    dst, xnbf[i][:, :]),
                    reads=[("xnbf", i)], rw=[("xnT",)], xp=True)
            emit_absmean(11)
            continue
        if not S0_NO_XBAR:
            dstm = xnmT[:, :].rearrange("q (c t) -> q c t", t=MQ)[
                :, :, mt * 128:(mt + 1) * 128]
            p.emit("y", lambda i=i, dstm=dstm: E["y"].dma_start_transpose(
                dstm, xnbf[i][:, :]),
                reads=[("xnbf", i)], rw=[("xnmT",)], xp=True)
        if DEBUG and mt == 0:
            p.emit("y", lambda: E["y"].dma_start(
                out=d_dbg2[:, 512:522], in_=ln_sc[:, :]),
                reads=[lk(0), lk(1), lk(2)], writes=[("dbgx", 1)], dma=True)
            p.emit("y", lambda: E["y"].dma_start(
                out=d_dbg2[:, 522:546], in_=stats[:, 0:24]),
                reads=[("stats",)], writes=[("dbgx", 2)], dma=True)
            p.emit("y", lambda i=i: E["y"].dma_start(
                out=d_dbg2[:, 1024:1536], in_=xs[i][:, 0:512]),
                reads=[("xs", i)], writes=[("dbgx", 3)], dma=True)
        for c in range(4):
            p.emit("v", lambda i=i, c=c: E["v"].tensor_scalar(
                scr[0][:, :], xs[i][:, c * 512:(c + 1) * 512],
                lc(1), lc(2), OP.mult, OP.add),
                reads=[("xs", i), t_rstd, t_ms], writes=[SK[0]])
            if DEBUG and mt == 0 and c == 0:
                p.emit("y", lambda: E["y"].dma_start(
                    out=d_dbg2[:, 0:512], in_=scr[0][:, :]),
                    reads=[SK[0]], writes=[("dbgx", 0)], dma=True)
            p.emit("v", lambda: E["v"].reduce_max(
                lc(3), scr[0][:, :], axis=AX, apply_absolute_value=True),
                reads=[SK[0]], writes=[lk(3)])
            if c == 0:
                p.emit("v", lambda: E["v"].tensor_copy(lc(4), lc(3)),
                       reads=[lk(3)], writes=[lk(4)])
            else:
                p.emit("v", lambda: E["v"].tensor_max(lc(4), lc(4), lc(3)),
                       reads=[lk(3)], rw=[lk(4)])
        p.emit("v", lambda: E["v"].tensor_scalar_max(lc(5), lc(4), float(EPS)),
               reads=[lk(4)], writes=[lk(5)])
        p.emit("v", lambda: E["v"].reciprocal(lc(6), lc(5)),
               reads=[lk(5)], writes=[lk(6)])
        p.emit("v", lambda mt=mt: E["v"].tensor_scalar_mul(
            col(COL_S, mt), lc(6), 127.0),
            reads=[lk(6)], writes=[ck_(COL_S, mt)])
        p.emit("v", lambda mt=mt: E["v"].tensor_scalar_mul(
            col(COL_SNEG, mt), col(COL_S, mt), -1.0),
            reads=[ck_(COL_S, mt)], writes=[ck_(COL_SNEG, mt)])
        p.emit("v", lambda mt=mt: E["v"].tensor_mul(
            col(COL_RS, mt), lc(1), col(COL_S, mt)),
            reads=[ck_(COL_S, mt), t_rstd], writes=[ck_(COL_RS, mt)])
        p.emit("v", lambda mt=mt: E["v"].tensor_mul(
            col(COL_MS2, mt), lc(2), col(COL_S, mt)),
            reads=[ck_(COL_S, mt), t_ms], writes=[ck_(COL_MS2, mt)])
        if DEBUG:
            p.emit("v", lambda mt=mt: E["v"].tensor_copy(
                colp[:, 80 + 5 * mt: 85 + 5 * mt], ln_sc[:, 0:5]),
                reads=[lk(0), lk(1), lk(4)], writes=[("dbgst", mt)])
        for c in range(4):
            p.emit("v", lambda i=i, c=c, mt=mt: E["v"].tensor_scalar(
                scr[0][:, :], xs[i][:, c * 512:(c + 1) * 512],
                col(COL_RS, mt), col(COL_MS2, mt), OP.mult, OP.add),
                reads=[("xs", i), ck_(COL_RS, mt), ck_(COL_MS2, mt)],
                writes=[SK[0]])
            p.emit("v", lambda: E["v"].tensor_copy(scr_i[:, :], scr[0][:, :]),
                   reads=[SK[0]], writes=[("scri",)])
            p.emit("v", lambda: E["v"].tensor_copy(scr[1][:, :], scr_i[:, :]),
                   reads=[("scri",)], writes=[SK[1]])
            p.emit("g", lambda: E["g"].tensor_sub(
                scr[2][:, :], scr[0][:, :], scr[1][:, :]),
                reads=[SK[0], SK[1]], writes=[SK[2]])
            p.emit("v", lambda: E["v"].scalar_tensor_tensor(
                scr[3][:, :], scr[2][:, :], 0.5, scr[1][:, :],
                OP.is_ge, OP.add),
                reads=[SK[2], SK[1]], writes=[SK[3]])
            p.emit("v", lambda c=c: E["v"].scalar_tensor_tensor(
                kxs[:, c * 512:(c + 1) * 512], scr[2][:, :], -0.5, scr[3][:, :],
                OP.is_le, OP.subtract),
                reads=[SK[2], SK[3]], rw=[("kxs",)])
        if not S0_NO_XBAR:
            dstk = kxT[:, :].rearrange("q (c t) -> q c t", t=128)
            p.emit("y", lambda dstk=dstk: E["y"].dma_start_transpose(
                dstk, kxs[:, :]),
                reads=[("kxs",)], writes=[("kxT",)], xp=True)
            p.emit("y", lambda mt=mt: E["y"].dma_start(
                out=d_kx[mt, :, :], in_=kxT[:, :]),
                reads=[("kxT",)], writes=[("kxd", mt)], dma=True, lazy=True)
            p.flush("yd")
        emit_absmean(11)

    stkT = ExitStack()
    stkT.__enter__()
    ps_tr = stkT.enter_context(nc.psum_tensor("ps_tr", [1, 128], BF16))
    for mt in range(0 if S0_SKIP_MINE else MT):
        p.emit("v", lambda mt=mt: E["v"].tensor_copy(
            tiny_bf[:, :], col(COL_SNEG, mt)),
            reads=[ck_(COL_SNEG, mt)], writes=[("tinybf",)])
        p.emit("t", lambda: E["t"].transpose(
            ps_tr[:, :], tiny_bf[:, :], ident_bf[:, :]),
            reads=[("tinybf",), ("c", 5)], writes=[("PStr",)])
        p.emit("v", lambda mt=mt: E["v"].tensor_copy(
            snegrow[0:1, mt * 128:(mt + 1) * 128], ps_tr[:, :]),
            reads=[("PStr",)], rw=[("snegrow",)])
    stkT.__exit__(None, None, None)
    if STOP_AFTER_S0:
        if DEBUG:
            p.emit("y", lambda: E["y"].dma_start(out=d_dxn[:, :],
                   in_=xnallT[:, :]), reads=[("xnT",)], writes=[("dbg1",)],
                   dma=True)
            p.emit("y", lambda: E["y"].dma_start(out=d_dbg[:, 0:128],
                   in_=colp[:, :], allow_slow_non_contiguous=True),
                   reads=[ck_(COL_S, 0)], writes=[("dbg2",)], dma=True)
        for _sn in ("gd", "yd", "sd", "xd", "t", "v", "s", "y"):
            E["g"].wait_ge(p.sems[_sn], p.cnt[_sn])
        for _sn in ("bt", "t", "v", "s", "g", "y", "gd", "yd", "sd", "xd"):
            nc.gpsimd.sem_clear(p.sems[_sn])
        stk0.__exit__(None, None, None)
        stkA.__exit__(None, None, None)
        stk.__exit__(None, None, None)
        return nc
    stk0.__exit__(None, None, None)
    p.gate()

    stk12 = ExitStack()
    stk12.__enter__()
    QT = sball(stk12, "QT", [128, HG * MQ], BF16)
    KT = sball(stk12, "KT", [128, HG * S], BF16)
    Vt = sball(stk12, "Vt", [128, NT * 512], BF16)
    wqT = sball(stk12, "wqT", [128, NT * 128], BF16)
    wkT = sball(stk12, "wkT", [128, NT * 128], BF16)
    wvT = sball(stk12, "wvT", [128, NT * 512], BF16)
    wstage = [sball(stk12, "wstage0", [128, D], BF16)] * 2
    Pb = [sball(stk12, f"Pb{i}", [128, MQ], BF16) for i in range(2)]
    sbc = [sball(stk12, f"sbc{i}", [128, 512], BF16) for i in range(2)]
    sqs = sball(stk12, "sqs", [128, MQ], BF16)
    srec = sball(stk12, "srec", [1, MQ], F32)

    stkP = ExitStack()
    stkP.__enter__()
    PS4 = [stkP.enter_context(nc.psum_tensor(f"ps4_{i}", [128, 512], F32))
           for i in range(4)]
    ps_y = [stkP.enter_context(nc.psum_tensor(f"ps_y{i}", [128, 512], F32))
            for i in range(2)]
    ps_s = [stkP.enter_context(nc.psum_tensor(f"ps_s{i}", [1, 512], F32))
            for i in range(2)]
    K4 = [("PS4", i) for i in range(4)]
    KY = [("PSy", i) for i in range(2)]
    KS = [("PSs", i) for i in range(2)]

    nwq = [0]

    def wxpose(dram, row0, dstap, dkey):
        i = nwq[0] % 2
        nwq[0] += 1
        p.emit("g", lambda: E["g"].dma_start(
            out=wstage[i][:, :], in_=dram[row0:row0 + 128, :]),
            writes=[("ws", 0)], dma=True)
        p.emit("y", lambda: E["y"].dma_start_transpose(dstap, wstage[i][:, :]),
               reads=[("ws", 0)], writes=[dkey], xp=True)

    for hg in range(NGRP):
        for hl in range(HG):
            h = hg * HG + hl
            wq_dst = wqT[:, :].rearrange("q (c t) -> q c t", t=128)
            wk_dst = wkT[:, :].rearrange("q (c t) -> q c t", t=128)
            wv_dst = wvT[:, :].rearrange("q (c t) -> q c t", t=512)[
                :, :, hl * 128:(hl + 1) * 128]
            wxpose(d_wqkv, h * 128, wq_dst, ("wqT",))
            for qg in range(2):
                ps = PS4[qg]

                def mm_q(ps=ps, qg=qg):
                    inst = None
                    for d in range(16):
                        inst = E["t"].matmul(
                            ps[:, :], wqT[:, d * 128:(d + 1) * 128],
                            xnmT[:, d * MQ + qg * 512: d * MQ + (qg + 1) * 512],
                            start=(d == 0), stop=(d == 15))
                    return inst
                p.emit("t", mm_q, reads=[("wqT",), ("xnmT",)], writes=[K4[qg]])
                p.emit("v", lambda ps=ps, hl=hl, qg=qg: E["v"].tensor_copy(
                    QT[:, hl * MQ + qg * 512: hl * MQ + (qg + 1) * 512],
                    ps[:, :]),
                    reads=[K4[qg]], rw=[("QT", hl)])
            wxpose(d_wqkv, D + h * 128, wk_dst, ("wkT",))
            for kg in range(4):
                ps = PS4[kg]

                def mm_k(ps=ps, kg=kg):
                    inst = None
                    for d in range(16):
                        inst = E["t"].matmul(
                            ps[:, :], wkT[:, d * 128:(d + 1) * 128],
                            xnallT[:, d * S + kg * 512: d * S + (kg + 1) * 512],
                            start=(d == 0), stop=(d == 15))
                    return inst
                p.emit("t", mm_k, reads=[("wkT",), ("xnT",)], writes=[K4[kg]])
                p.emit("v", lambda ps=ps, hl=hl, kg=kg: E["v"].tensor_copy(
                    KT[:, hl * S + kg * 512: hl * S + (kg + 1) * 512],
                    ps[:, :]),
                    reads=[K4[kg]], rw=[("KT", hl)])
            wxpose(d_wqkv, 2 * D + h * 128, wv_dst, ("wvT", hl))
        for kt in range(NT):
            ps = PS4[kt % 4]

            def mm_v(ps=ps, kt=kt):
                inst = None
                for d in range(16):
                    inst = E["t"].matmul(
                        ps[:, :],
                        xnallT[:, d * S + kt * 128: d * S + (kt + 1) * 128],
                        wvT[:, d * 512:(d + 1) * 512],
                        start=(d == 0), stop=(d == 15))
                return inst
            p.emit("t", mm_v,
                   reads=[("xnT",), ("wvT", 0), ("wvT", 1), ("wvT", 2),
                          ("wvT", 3)],
                   writes=[K4[kt % 4]])
            p.emit("v", lambda ps=ps, kt=kt: E["v"].tensor_copy(
                Vt[:, kt * 512:(kt + 1) * 512], ps[:, :]),
                reads=[K4[kt % 4]], writes=[("Vt", kt)])

        CQ2, CK2, CPR, CNC = 7, 11, 22, 26
        for hl in range(HG):
            p.emit("s", lambda hl=hl: E["s"].activation(
                sqs[:, 0:MQ], QT[:, hl * MQ:(hl + 1) * MQ], AF.Square),
                reads=[("QT", hl)], writes=[("sqs",)])
            for qg in range(2):
                p.emit("t", lambda qg=qg: E["t"].matmul(
                    ps_s[qg][:, :], ones_c_bf[:, :],
                    sqs[:, qg * 512:(qg + 1) * 512], start=True, stop=True),
                    reads=[("sqs",), ("c", 1)], writes=[KS[qg]])
                p.emit("v", lambda qg=qg, hl=hl: E["v"].reduce_max(
                    sc(CQ2 + hl) if qg == 0 else sc(SC_T0),
                    ps_s[qg][:, :], axis=AX),
                    reads=[KS[qg]],
                    writes=[sk(CQ2 + hl) if qg == 0 else sk(SC_T0)])
            p.emit("v", lambda hl=hl: E["v"].tensor_max(
                sc(CQ2 + hl), sc(CQ2 + hl), sc(SC_T0)),
                reads=[sk(SC_T0)], rw=[sk(CQ2 + hl)])
            for kg in range(4):
                p.emit("s", lambda hl=hl, kg=kg: E["s"].activation(
                    sqs[:, 0:512],
                    KT[:, hl * S + kg * 512: hl * S + (kg + 1) * 512],
                    AF.Square),
                    reads=[("KT", hl)], rw=[("sqs",)])
                p.emit("t", lambda kg=kg: E["t"].matmul(
                    ps_s[kg % 2][:, :], ones_c_bf[:, :], sqs[:, 0:512],
                    start=True, stop=True),
                    reads=[("sqs",), ("c", 1)], writes=[KS[kg % 2]])
                p.emit("v", lambda kg=kg, hl=hl: E["v"].reduce_max(
                    sc(CK2 + hl) if kg == 0 else sc(SC_T0),
                    ps_s[kg % 2][:, :], axis=AX),
                    reads=[KS[kg % 2]],
                    writes=[sk(CK2 + hl) if kg == 0 else sk(SC_T0)])
                if kg:
                    p.emit("v", lambda hl=hl: E["v"].tensor_max(
                        sc(CK2 + hl), sc(CK2 + hl), sc(SC_T0)),
                        reads=[sk(SC_T0)], rw=[sk(CK2 + hl)])
            p.emit("v", lambda hl=hl: E["v"].tensor_mul(
                sc(CPR + hl), sc(CQ2 + hl), sc(CK2 + hl)),
                reads=[sk(CQ2 + hl), sk(CK2 + hl)], writes=[sk(CPR + hl)])
        p.emit("s", lambda: E["s"].activation(
            scal[0:1, CPR:CPR + 4], scal[0:1, CPR:CPR + 4], AF.Sqrt),
            rw=[sk(CPR + hl_) for hl_ in range(HG)])
        p.emit("v", lambda: E["v"].tensor_scalar_mul(
            scal[0:1, CNC:CNC + 4], scal[0:1, CPR:CPR + 4], -SCALE * 1.0001),
            reads=[sk(CPR + hl_) for hl_ in range(HG)],
            writes=[sk(CNC + hl_) for hl_ in range(HG)])
        for hl in range(HG):
            p.emit("t", lambda hl=hl: E["t"].matmul(
                PS4[3][:, 0:1], ones_r_f[:, :], sc(CNC + hl),
                start=True, stop=True),
                reads=[sk(CNC + hl), ("c", 3)], writes=[K4[3]])
            p.emit("v", lambda hl=hl: E["v"].tensor_copy(
                negc_bc[:, hl:hl + 1], PS4[3][:, 0:1]),
                reads=[K4[3]], writes=[("negc", hl)])

        for hl in range(HG):
            h = hg * HG + hl
            pieces = []
            for j in range(NT):
                qs = 128 * (j // 2)
                if qs < 512:
                    pieces.append((j, qs, qs, 512))
                    pieces.append((j, qs, 512, 1024))
                else:
                    pieces.append((j, qs, qs, 1024))

            def stmm(idx, hl=hl):
                j, qs, a, b2 = pieces[idx]
                w = b2 - a
                psi = idx % 2
                p.emit("t", lambda j=j, a=a, w=w, psi=psi, hl=hl:
                       E["t"].matmul(
                           PS4[psi][:, 0:w],
                           KT[:, hl * S + j * 128: hl * S + (j + 1) * 128],
                           QT[:, hl * MQ + a: hl * MQ + a + w],
                           start=True, stop=True),
                       reads=[("KT", hl), ("QT", hl)], writes=[K4[psi]])

            stmm(0)
            for idx in range(len(pieces)):
                j, qs, a, b2 = pieces[idx]
                w = b2 - a
                jb = j % 2
                ck = 0 if a < 512 else 1
                psi = idx % 2
                p.emit("s", lambda jb=jb, a=a, w=w, psi=psi, hl=hl:
                       E["s"].activation(
                           Pb[jb][:, a:a + w], PS4[psi][:, 0:w], AF.Exp,
                           bias=negc_bc[:, hl:hl + 1], scale=SCALE),
                       reads=[K4[psi], ("negc", hl)], rw=[("Pb", jb, ck)])
                if a == qs:
                    msk, mkey = (m0, 6) if j % 2 == 0 else (m1, 7)
                    p.emit("v", lambda jb=jb, qs=qs, msk=msk:
                           E["v"].tensor_mul(
                               Pb[jb][:, qs:qs + 128],
                               Pb[jb][:, qs:qs + 128], msk[:, :]),
                           reads=[("c", mkey)], rw=[("Pb", jb, ck)])
                if idx + 1 < len(pieces):
                    stmm(idx + 1)
                last = (j == 7) if ck == 0 else (j == 15)
                p.emit("t", lambda jb=jb, a=a, w=w, ck=ck, j=j, last=last:
                       E["t"].matmul(
                           ps_s[ck][:, a - ck * 512: a - ck * 512 + w],
                           ones_c_bf[:, :], Pb[jb][:, a:a + w],
                           start=(j == 0), stop=last,
                           skip_group_check=True),
                       reads=[("Pb", jb, ck), ("c", 1)], rw=[KS[ck]])
                p.emit("t", lambda jb=jb, a=a, w=w, ck=ck, j=j, hl=hl,
                       last=last: E["t"].matmul(
                           ps_y[ck][:, a - ck * 512: a - ck * 512 + w],
                           Vt[:, j * 512 + hl * 128:
                              j * 512 + (hl + 1) * 128],
                           Pb[jb][:, a:a + w],
                           start=(j == 0), stop=last,
                           skip_group_check=True),
                       reads=[("Pb", jb, ck), ("Vt", j)], rw=[KY[ck]])
            for ck in range(2):
                p.emit("v", lambda ck=ck: E["v"].reciprocal(
                    srec[0:1, ck * 512:(ck + 1) * 512], ps_s[ck][:, :]),
                    reads=[KS[ck]], rw=[("srec",)])
                p.emit("t", lambda ck=ck: E["t"].matmul(
                    PS4[2 + ck][:, :], ones_r_f[:, :],
                    srec[0:1, ck * 512:(ck + 1) * 512],
                    start=True, stop=True),
                    reads=[("srec",), ("c", 3)], writes=[K4[2 + ck]])
                p.emit("v", lambda ck=ck: E["v"].tensor_copy(
                    sbc[ck][:, :], PS4[2 + ck][:, :]),
                    reads=[K4[2 + ck]], writes=[("sbc", ck)])
                p.emit("v", lambda ck=ck: E["v"].tensor_mul(
                    scr_bf[ck][:, :], ps_y[ck][:, :], sbc[ck][:, :]),
                    reads=[KY[ck], ("sbc", ck)], writes=[SB[ck]])
                p.emit("y", lambda h=h, ck=ck: E["y"].dma_start(
                    out=d_y[:, h, ck * 512:(ck + 1) * 512],
                    in_=scr_bf[ck][:, :]),
                    reads=[SB[ck]], writes=[("ydone", h, ck)], dma=True,
                    lazy=True)
            p.flush("yd")
    stkP.__exit__(None, None, None)
    stk12.__exit__(None, None, None)
    p.gate()

    stkM = ExitStack()
    stkM.__enter__()
    ps_m = stkM.enter_context(nc.psum_tensor("ps_m", [1, 8], F32))
    ps_b6 = stkM.enter_context(nc.psum_tensor("ps_b6", [128, 8], F32))
    for wi in range(2):
        accv = acc1 if wi == 0 else acc2
        acck = ("acc1",) if wi == 0 else ("acc2",)
        p.emit("t", lambda accv=accv, wi=wi: E["t"].matmul(
            ps_m[:, wi:wi + 1], ones_c_f[:, :], accv[:, :],
            start=True, stop=True),
            reads=[acck, ("c", 2)], writes=[("PSm", wi)])
        p.emit("v", lambda wi=wi: E["v"].tensor_scalar(
            sc(SC_T0), ps_m[:, wi:wi + 1], 1.0 / (FF * D), float(EPS),
            OP.mult, OP.max),
            reads=[("PSm", wi)], writes=[sk(SC_T0)])
        p.emit("v", lambda wi=wi: E["v"].tensor_copy(
            scal[0:1, SC_ROW + 4 + wi: SC_ROW + 5 + wi], sc(SC_T0)),
            reads=[sk(SC_T0)], writes=[("scal", SC_ROW + 4 + wi)])
        p.emit("v", lambda wi=wi: E["v"].tensor_scalar_mul(
            scal[0:1, SC_ROW + 2 * wi: SC_ROW + 2 * wi + 1], sc(SC_T0), 0.5),
            reads=[sk(SC_T0)], writes=[("scal", SC_ROW + 2 * wi)])
        p.emit("v", lambda wi=wi: E["v"].tensor_scalar_mul(
            scal[0:1, SC_ROW + 2 * wi + 1: SC_ROW + 2 * wi + 2],
            sc(SC_T0), -0.5),
            reads=[sk(SC_T0)], writes=[("scal", SC_ROW + 2 * wi + 1)])
        p.emit("v", lambda: E["v"].reciprocal(sc(SC_T1), sc(SC_T0)),
               reads=[sk(SC_T0)], writes=[sk(SC_T1)])
        p.emit("v", lambda wi=wi: E["v"].tensor_scalar_mul(
            sc(SC_NR1 if wi == 0 else SC_NR2), sc(SC_T1), -1.0),
            reads=[sk(SC_T1)],
            writes=[sk(SC_NR1 if wi == 0 else SC_NR2)])
    rowk = [("scal", SC_ROW + i) for i in range(6)]
    p.emit("t", lambda: E["t"].matmul(
        ps_b6[:, 0:6], ones_r_f[:, :], scal[0:1, SC_ROW:SC_ROW + 6],
        start=True, stop=True),
        reads=rowk + [("c", 3)], writes=[("PSb6",)])
    p.emit("v", lambda: E["v"].tensor_copy(thbc[:, :], ps_b6[:, 0:6]),
           reads=[("PSb6",)], writes=[("thbc",)])
    for mt in range(MT):
        p.emit("v", lambda mt=mt: E["v"].reciprocal(lc(0), col(COL_S, mt)),
               reads=[ck_(COL_S, mt)], writes=[lk(0)])
        p.emit("v", lambda mt=mt: E["v"].tensor_mul(
            col(COL_R1, mt), thc(MEAN1), lc(0)),
            reads=[lk(0), ("thbc",)], writes=[ck_(COL_R1, mt)])
    if DEBUG:
        p.emit("y", lambda: E["y"].dma_start(out=d_dxn[:, :], in_=xnallT[:, :]),
               reads=[("xnT",)], writes=[("dbg1",)], dma=True)
        p.emit("y", lambda: E["y"].dma_start(out=d_dbg[:, 0:128], in_=colp[:, :], allow_slow_non_contiguous=True),
               reads=[ck_(COL_S, 0)], writes=[("dbg2",)], dma=True)
        p.emit("y", lambda: E["y"].dma_start(out=d_dbg[:, 128:134], in_=thbc[:, :], allow_slow_non_contiguous=True),
               reads=[("thbc",)], writes=[("dbg3",)], dma=True)
        p.emit("y", lambda: E["y"].dma_start(out=d_dbg[0:1, 134:166], in_=scal[:, :]),
               reads=[sk(SC_NR1)], writes=[("dbg4",)], dma=True)
        p.emit("y", lambda: E["y"].dma_start(out=d_dbg[:, 166:167], in_=acc1[:, :], allow_slow_non_contiguous=True),
               reads=[("acc1",)], writes=[("dbg5",)], dma=True)
        p.emit("y", lambda: E["y"].dma_start(out=d_dbg[:, 167:168], in_=acc2[:, :], allow_slow_non_contiguous=True),
               reads=[("acc2",)], writes=[("dbg6",)], dma=True)
        p.emit("y", lambda: E["y"].dma_start(out=d_dbg[:, 168:169], in_=negc_bc[:, :], allow_slow_non_contiguous=True),
               reads=[("negc",)], writes=[("dbg7",)], dma=True)
    stkM.__exit__(None, None, None)
    stkA.__exit__(None, None, None)
    p.gate()

    stk3 = ExitStack()
    stk3.__enter__()
    woT = [sball(stk3, f"woT{i}", [128, NT * 512], BF16) for i in range(2)]
    yld = [sball(stk3, f"yld{i}", [128, NT * 128], BF16) for i in range(2)]
    w3stage = [sball(stk3, f"w3stage{i}", [128, D], BF16) for i in range(2)]
    stkP3 = ExitStack()
    stkP3.__enter__()
    PSO = [stkP3.enter_context(nc.psum_tensor(f"pso{i}", [128, 512], F32))
           for i in range(4)]
    ydk = [("ydone", h_, c_) for h_ in range(H) for c_ in range(2)]
    for eg in range(4):
        ib = eg % 2
        for sub in range(4):
            i = (eg * 4 + sub) % 2
            p.emit("g", lambda eg=eg, sub=sub, i=i: E["g"].dma_start(
                out=w3stage[i][:, :],
                in_=d_wout[eg * 512 + sub * 128:
                           eg * 512 + (sub + 1) * 128, :]),
                writes=[("w3s", i)], dma=True)
            dst = woT[ib][:, :].rearrange("q (c t) -> q c t", t=512)[
                :, :, sub * 128:(sub + 1) * 128]
            p.emit("y", lambda dst=dst, i=i: E["y"].dma_start_transpose(
                dst, w3stage[i][:, :]),
                reads=[("w3s", i)], rw=[("woT", ib)], xp=True)
        for mt in range(MT):
            im = mt % 2
            p.emit("y", lambda mt=mt, im=im: E["y"].dma_start(
                out=yld[im][:, :], in_=d_y[:, :, mt * 128:(mt + 1) * 128]),
                reads=ydk, writes=[("yld", im)], dma=True)
            ps = PSO[mt % 4]

            def mm_o(ps=ps, ib=ib, im=im):
                inst = None
                for d in range(16):
                    inst = E["t"].matmul(
                        ps[:, :], yld[im][:, d * 128:(d + 1) * 128],
                        woT[ib][:, d * 512:(d + 1) * 512],
                        start=(d == 0), stop=(d == 15))
                return inst
            p.emit("t", mm_o, reads=[("yld", im), ("woT", ib)],
                   writes=[("PSO", mt % 4)])
            p.emit("v", lambda ps=ps, mt=mt: E["v"].tensor_copy(
                scr[mt % 2][:, :], ps[:, :]),
                reads=[("PSO", mt % 4)], writes=[SK[mt % 2]])
            p.emit("g", lambda mt=mt, eg=eg: E["g"].dma_start(
                out=d_out[mt * 128:(mt + 1) * 128, eg * 512:(eg + 1) * 512],
                in_=scr[mt % 2][:, :], accum_op=OP.add),
                reads=[SK[mt % 2]], rw=[("out", mt)], dma=True)
    stkP3.__exit__(None, None, None)
    stk3.__exit__(None, None, None)
    p.gate()

    stkF = ExitStack()
    stkF.__enter__()
    garena = sball(stkF, "garena", [128, ARN * FF], BF16)
    gA = garena[:, :]
    gH = garena[:, :].bitcast(FP16)

    stk4 = ExitStack()
    stk4.__enter__()
    kxf = sball(stk4, "kxf", [128, MT * NT * 128], BF16)
    w1stage = [sball(stk4, f"w1stage{i}", [128, 512], F32)
               for i in range(2)]
    w1qT = [sball(stk4, f"w1qT{i}", [128, NT * 256], BF16) for i in range(2)]
    qpos = scr_bf[0][:, :]
    qneg = scr_bf[1][:, :]
    rowa = sball(stk4, "rowa4", [1, 256], F32)[0:1, :]
    rowbf3 = sball(stk4, "rowbf4", [1, 768], BF16)
    rowbf = [rowbf3[0:1, 0:256], rowbf3[0:1, 256:512], rowbf3[0:1, 512:768]]
    sqs4 = sball(stk4, "sqs44", [128, 256], F32)[:, :]
    stkP4 = ExitStack()
    stkP4.__enter__()
    PSH = [stkP4.enter_context(nc.psum_tensor(f"psh{i}", [128, 256], F32))
           for i in range(4)]

    def w1_rowa(fc):
        p.emit("s", lambda fc=fc: E["s"].dma_start(
            out=rowa, in_=d_b1[0:1, fc * 256:(fc + 1) * 256]),
            writes=[("rowa",)], dma=True, lazy=True)

    def w1_rowbf(fc):
        p.flush("sd")
        p.emit("v", lambda ib=fc % 2: E["v"].tensor_single_scalar(
            rowbf[ib], rowa, sc(SC_NR1), OP.mult),
            reads=[("rowa",), sk(SC_NR1)], writes=[("rowbf", fc % 2)])

    def w1_load_l(fc, l):
        fb, q = l // 4, l % 4
        r0 = fc * 256 + fb * 128
        p.emit("s", lambda r0=r0, q=q, l=l: E["s"].dma_start(
            out=w1stage[l % 2][:, :],
            in_=d_w1[r0:r0 + 128, q * 512:(q + 1) * 512]),
            writes=[("w1s", l % 2)], dma=True, lazy=True)

    def w1_quant_l(fc, l):
        ib = fc % 2
        fb, q = l // 4, l % 4
        p.flush("sd")
        p.emit("v", lambda l=l: E["v"].tensor_single_scalar(
            qpos[:, :], w1stage[l % 2][:, :], thc(TH1), OP.is_ge),
            reads=[("w1s", l % 2), ("thbc",)], writes=[("qp",)])
        p.emit("v", lambda l=l: E["v"].tensor_single_scalar(
            qneg[:, :], w1stage[l % 2][:, :], thc(NTH1), OP.is_le),
            reads=[("w1s", l % 2), ("thbc",)], writes=[("qn",)])
        p.emit("g", lambda: E["g"].tensor_sub(
            qpos[:, :], qneg[:, :], qpos[:, :]),
            reads=[("qn",)], rw=[("qp",)])
        dst = w1qT[ib][:, :].rearrange("q (c t) -> q c t", t=256)[
            :, 4 * q:4 * q + 4, fb * 128:(fb + 1) * 128]
        p.emit("y", lambda dst=dst: E["y"].dma_start_transpose(
            dst, qpos[:, :]),
            reads=[("qp",)], rw=[("w1qT", ib)], xp=True)

    def w1_prep(fc, step):
        if fc >= FC1:
            return
        if step == 0:
            w1_rowa(fc)
            w1_load_l(fc, 0)
            w1_load_l(fc, 1)
            w1_rowbf(fc)
        w1_quant_l(fc, step)
        if step + 2 < 8:
            w1_load_l(fc, step + 2)

    for mt in range(MT):
        p.emit("g", lambda mt=mt: E["g"].dma_start(
            out=kxf[:, mt * 2048:(mt + 1) * 2048], in_=d_kx[mt, :, :]),
            reads=[("kxd", mt)], writes=[("kxf", mt)], dma=True, lazy=True)
    p.flush("gd")
    for st in range(8):
        w1_prep(0, st)

    for fc in range(FC1):
        ib = fc % 2
        for mt in range(MT):
            ps = PSH[mt % 4]

            def mm_h(ps=ps, ib=ib, mt=mt):
                inst = None
                for d in range(16):
                    inst = E["t"].matmul(
                        ps[:, :],
                        kxf[:, mt * 2048 + d * 128: mt * 2048 + (d + 1) * 128],
                        w1qT[ib][:, d * 256:(d + 1) * 256],
                        start=(d == 0), stop=False,
                        skip_group_check=True)
                return inst
            p.emit("t", mm_h, reads=[("kxf", mt), ("w1qT", ib)],
                   rw=[("PSH", mt % 4)])
            p.emit("t", lambda ps=ps, mt=mt, ib=ib: E["t"].matmul(
                ps[:, :], snegrow[0:1, mt * 128:(mt + 1) * 128],
                rowbf[ib], start=False, stop=True,
                skip_group_check=True),
                reads=[("snegrow",), ("rowbf", ib)], rw=[("PSH", mt % 4)])
            gdst = gA[:, mt * FF + fc * 256: mt * FF + (fc + 1) * 256]
            p.emit("s", lambda ps=ps, mt=mt, gdst=gdst: E["s"].activation(
                gdst, ps[:, :], AF.Gelu, scale=col(COL_R1, mt)),
                reads=[("PSH", mt % 4), ck_(COL_R1, mt)],
                writes=[("ar", mt, fc // 2)])
            p.emit("v", lambda mt=mt, fc=fc, gdst=gdst: E["v"].reduce_sum(
                stats1[:, (mt * FC1 + fc) * 2:(mt * FC1 + fc) * 2 + 1],
                gdst, axis=AX),
                reads=[("ar", mt, fc // 2)], rw=[("stats1", mt)])
            p.emit("v", lambda gdst=gdst: E["v"].tensor_mul(
                sqs4, gdst, gdst),
                reads=[("ar", mt, fc // 2)], writes=[("sqs4",)])
            p.emit("v", lambda mt=mt, fc=fc: E["v"].reduce_sum(
                stats1[:, (mt * FC1 + fc) * 2 + 1:(mt * FC1 + fc) * 2 + 2],
                sqs4, axis=AX),
                reads=[("sqs4",)], rw=[("stats1", mt)])
            w1_prep(fc + 1, mt)
    stkP4.__exit__(None, None, None)
    stk4.__exit__(None, None, None)
    p.gate()

    stk56 = ExitStack()
    stk56.__enter__()
    s2negrow = sball(stk56, "s2negrow", [1, MQ], BF16)
    stk5 = ExitStack()
    stk5.__enter__()
    gp = sball(stk5, "gp", [128, 512], F32)
    scr_i = sball(stk5, "scr_i5", [128, 512], I32)
    scr = [sball(stk5, f"scr5_{i}", [128, 512], F32) for i in range(4)]
    bp = sball(stk5, "bp", [128, 512], F32)
    rowg = sball(stk5, "rowg", [1, 512], F32)
    rowh = sball(stk5, "rowh", [1, 512], F32)
    stkP5 = ExitStack()
    stkP5.__enter__()
    PSB = [stkP5.enter_context(nc.psum_tensor(f"psb{i}", [128, 512], F32))
           for i in range(2)]
    ps_tr2 = stkP5.enter_context(nc.psum_tensor("ps_tr2", [1, 128], BF16))

    for mt in range(MT):
        base = mt * FC1 * 2
        p.emit("v", lambda base=base: E["v"].reduce_sum(
            lc(3), stats1[:, base:base + FC1 * 2:2], axis=AX),
            reads=[("stats1", mt)], writes=[lk(3)])
        p.emit("v", lambda base=base: E["v"].reduce_sum(
            lc(5), stats1[:, base + 1:base + FC1 * 2:2], axis=AX),
            reads=[("stats1", mt)], writes=[lk(5)])
        p.emit("v", lambda: E["v"].tensor_scalar_mul(lc(0), lc(3), 1.0 / FF),
               reads=[lk(3)], writes=[lk(0)])
        p.emit("v", lambda: E["v"].tensor_mul(lc(6), lc(0), lc(0)),
               reads=[lk(0)], writes=[lk(6)])
        p.emit("v", lambda: E["v"].scalar_tensor_tensor(
            lc(9), lc(5), 1.0 / FF, lc(6), OP.mult, OP.subtract),
            reads=[lk(5), lk(6)], writes=[lk(9)])
        p.emit("s", lambda: E["s"].activation(
            lc(2), lc(9), AF.Sqrt, bias=eps_col[:, :], scale=1.0),
            reads=[lk(9), ("c", 8)], writes=[lk(2)])
        p.emit("v", lambda mt=mt: E["v"].reciprocal(col(COL_A, mt), lc(2)),
               reads=[lk(2)], writes=[ck_(COL_A, mt)])
        p.emit("v", lambda mt=mt: E["v"].scalar_tensor_tensor(
            col(COL_BB, mt), lc(0), -1.0, col(COL_A, mt), OP.mult, OP.mult),
            reads=[lk(0), ck_(COL_A, mt)], writes=[ck_(COL_BB, mt)])

    for fc in range(16):
        p.emit("y", lambda fc=fc: E["y"].dma_start(
            out=rowg[0:1, :], in_=d_gm[0:1, fc * 512:(fc + 1) * 512]),
            writes=[("rowg",)], dma=True)
        p.emit("y", lambda fc=fc: E["y"].dma_start(
            out=rowh[0:1, :], in_=d_bt[0:1, fc * 512:(fc + 1) * 512]),
            writes=[("rowh",)], dma=True)
        p.emit("t", lambda: E["t"].matmul(
            PSB[0][:, :], ones_r_f[:, :], rowg[0:1, :], start=True, stop=True),
            reads=[("rowg",), ("c", 3)], writes=[("PSB", 0)])
        p.emit("s", lambda: E["s"].activation(gp[:, :], PSB[0][:, :], AF.Copy),
               reads=[("PSB", 0)], writes=[("gp",)])
        p.emit("t", lambda: E["t"].matmul(
            PSB[1][:, :], ones_r_f[:, :], rowh[0:1, :], start=True, stop=True),
            reads=[("rowh",), ("c", 3)], writes=[("PSB", 1)])
        p.emit("s", lambda: E["s"].activation(bp[:, :], PSB[1][:, :], AF.Copy),
               reads=[("PSB", 1)], writes=[("bp",)])
        for mt in range(MT):
            gsl = gA[:, mt * FF + fc * 512: mt * FF + (fc + 1) * 512]
            hsl = gH[:, mt * FF + fc * 512: mt * FF + (fc + 1) * 512]
            ark = ("ar", mt, fc)
            p.emit("v", lambda gsl=gsl, mt=mt: E["v"].tensor_scalar(
                scr[0][:, :], gsl, col(COL_A, mt), col(COL_BB, mt),
                OP.mult, OP.add),
                reads=[ark, ck_(COL_A, mt), ck_(COL_BB, mt)], writes=[SK[0]])
            p.emit("g", lambda: E["g"].tensor_mul(
                scr[1][:, :], scr[0][:, :], gp[:, :]),
                reads=[SK[0], ("gp",)], writes=[SK[1]])
            p.emit("v", lambda hsl=hsl: E["v"].tensor_add(
                hsl, scr[1][:, :], bp[:, :]),
                reads=[SK[1], ("bp",)], rw=[ark])
            p.emit("v", lambda hsl=hsl: E["v"].reduce_max(
                lc(3), hsl, axis=AX, apply_absolute_value=True),
                reads=[ark], writes=[lk(3)])
            if fc == 0:
                p.emit("v", lambda mt=mt: E["v"].tensor_copy(
                    col(COL_M2, mt), lc(3)),
                    reads=[lk(3)], writes=[ck_(COL_M2, mt)])
            else:
                p.emit("v", lambda mt=mt: E["v"].tensor_max(
                    col(COL_M2, mt), col(COL_M2, mt), lc(3)),
                    reads=[lk(3)], rw=[ck_(COL_M2, mt)])
    for mt in range(MT):
        p.emit("v", lambda mt=mt: E["v"].tensor_scalar_max(
            lc(5), col(COL_M2, mt), float(EPS)),
            reads=[ck_(COL_M2, mt)], writes=[lk(5)])
        p.emit("v", lambda: E["v"].reciprocal(lc(6), lc(5)),
               reads=[lk(5)], writes=[lk(6)])
        p.emit("v", lambda mt=mt: E["v"].tensor_scalar_mul(
            col(COL_S2, mt), lc(6), 127.0),
            reads=[lk(6)], writes=[ck_(COL_S2, mt)])
        p.emit("v", lambda mt=mt: E["v"].tensor_mul(
            lc(7), thc(MEAN2), lc(5)),
            reads=[lk(5), ("thbc",)], writes=[lk(7)])
        p.emit("v", lambda mt=mt: E["v"].tensor_scalar_mul(
            col(COL_R2, mt), lc(7), 1.0 / 127.0),
            reads=[lk(7)], writes=[ck_(COL_R2, mt)])
        p.emit("v", lambda mt=mt: E["v"].tensor_scalar_mul(
            lc(8), col(COL_S2, mt), -1.0),
            reads=[ck_(COL_S2, mt)], writes=[lk(8)])
        p.emit("v", lambda: E["v"].tensor_copy(tiny_bf[:, :], lc(8)),
               reads=[lk(8)], writes=[("tinybf",)])
        p.emit("t", lambda: E["t"].transpose(
            ps_tr2[:, :], tiny_bf[:, :], ident_bf[:, :]),
            reads=[("tinybf",), ("c", 5)], writes=[("PStr2",)])
        p.emit("v", lambda mt=mt: E["v"].tensor_copy(
            s2negrow[0:1, mt * 128:(mt + 1) * 128], ps_tr2[:, :]),
            reads=[("PStr2",)], rw=[("s2negrow",)])

    for mt in range(MT - 1, -1, -1):
        for fc in range(16):
            hsl = gH[:, mt * FF + fc * 512: mt * FF + (fc + 1) * 512]
            ksl = gA[:, mt * FF + fc * 512: mt * FF + (fc + 1) * 512]
            ark = ("ar", mt, fc)
            p.emit("v", lambda hsl=hsl, mt=mt: E["v"].tensor_single_scalar(
                scr[0][:, :], hsl, col(COL_S2, mt), OP.mult),
                reads=[ark, ck_(COL_S2, mt)], writes=[SK[0]])
            p.emit("v", lambda: E["v"].tensor_copy(scr_i[:, :], scr[0][:, :]),
                   reads=[SK[0]], writes=[("scri",)])
            p.emit("v", lambda: E["v"].tensor_copy(scr[1][:, :], scr_i[:, :]),
                   reads=[("scri",)], writes=[SK[1]])
            p.emit("g", lambda: E["g"].tensor_sub(
                scr[2][:, :], scr[0][:, :], scr[1][:, :]),
                reads=[SK[0], SK[1]], writes=[SK[2]])
            p.emit("v", lambda: E["v"].scalar_tensor_tensor(
                scr[3][:, :], scr[2][:, :], 0.5, scr[1][:, :],
                OP.is_ge, OP.add),
                reads=[SK[2], SK[1]], writes=[SK[3]])
            p.emit("v", lambda ksl=ksl: E["v"].scalar_tensor_tensor(
                ksl, scr[2][:, :], -0.5, scr[3][:, :], OP.is_le, OP.subtract),
                reads=[SK[2], SK[3]], rw=[ark])
        src = gA[:, mt * FF:(mt + 1) * FF]
        dst = gA[:, (mt + 1) * FF:(mt + 2) * FF].rearrange(
            "q (c t) -> q c t", t=128)
        slot_reads = [("ar", mt, k) for k in range(16)]
        slot_writes = [("ar", mt + 1, k) for k in range(16)]
        p.emit("y", lambda src=src, dst=dst: E["y"].dma_start_transpose(
            dst, src),
            reads=slot_reads, writes=[("khT", mt)] + slot_writes, xp=True)
    stkP5.__exit__(None, None, None)
    stk5.__exit__(None, None, None)
    p.gate()

    stk6 = ExitStack()
    stk6.__enter__()
    w2stage = sball(stk6, "w2stage", [128, D], F32)
    scr = [sball(stk6, f"scr6_{i}", [128, 512], F32) for i in range(4)]
    q2p = sball(stk6, "q2p", [128, 512], BF16)
    q2n = sball(stk6, "q2n", [128, 512], BF16)
    w2qT = [sball(stk6, f"w2qT{i}", [128, 64 * 128], BF16) for i in range(2)]
    rowa2 = sball(stk6, "rowa2", [1, 128], F32)
    rowbf2 = [sball(stk6, f"rowbf2{i}", [1, 128], BF16) for i in range(2)]
    stkP6 = ExitStack()
    stkP6.__enter__()
    PS2 = [stkP6.enter_context(nc.psum_tensor(f"ps2_{i}", [128, 512], F32))
           for i in range(2)]
    khview = gA[:, :].rearrange("q (sl t) -> q sl t", t=FF)

    def w2_rowa(dc):
        p.emit("s", lambda dc=dc: E["s"].dma_start(
            out=rowa2[0:1, :], in_=d_b2[0:1, dc * 128:(dc + 1) * 128]),
            writes=[("rowa2",)], dma=True, lazy=True)

    def w2_rowbf(dc):
        p.flush("sd")
        p.emit("v", lambda ib=dc % 2: E["v"].tensor_single_scalar(
            rowbf2[ib][0:1, :], rowa2[0:1, :], sc(SC_NR2), OP.mult),
            reads=[("rowa2",), sk(SC_NR2)], writes=[("rowbf2", dc % 2)])

    def w2_load(dc, l):
        p.emit("s", lambda dc=dc, l=l: E["s"].dma_start(
            out=w2stage[:, :],
            in_=d_w2[dc * 128:(dc + 1) * 128, l * 2048:(l + 1) * 2048]),
            writes=[("w2s",)], dma=True, lazy=True)

    def w2_quant_q(dc, l, q):
        ib = dc % 2
        fs = l * 4 + q
        p.flush("sd")
        p.emit("v", lambda q=q: E["v"].tensor_single_scalar(
            q2p[:, :], w2stage[:, q * 512:(q + 1) * 512], thc(TH2), OP.is_ge),
            reads=[("w2s",), ("thbc",)], writes=[("q2p",)])
        p.emit("v", lambda q=q: E["v"].tensor_single_scalar(
            q2n[:, :], w2stage[:, q * 512:(q + 1) * 512], thc(NTH2), OP.is_le),
            reads=[("w2s",), ("thbc",)], writes=[("q2n",)])
        p.emit("g", lambda: E["g"].tensor_sub(
            q2p[:, :], q2n[:, :], q2p[:, :]),
            reads=[("q2n",)], rw=[("q2p",)])
        dst = w2qT[ib][:, :].rearrange("q (c t) -> q c t", t=128)[
            :, 4 * fs:4 * fs + 4, :]
        p.emit("y", lambda dst=dst: E["y"].dma_start_transpose(
            dst, q2p[:, :]),
            reads=[("q2p",)], rw=[("w2qT", ib)], xp=True)

    def w2_prep(dc, step):
        if dc >= 16:
            return
        if step == 0:
            w2_rowa(dc)
            w2_load(dc, 0)
            w2_rowbf(dc)
        elif step == 7:
            for q in range(4):
                w2_quant_q(dc, 3, q)
        else:
            l = (step - 1) // 2
            if step % 2 == 1:
                w2_quant_q(dc, l, 0)
                w2_quant_q(dc, l, 1)
            else:
                w2_quant_q(dc, l, 2)
                w2_quant_q(dc, l, 3)
                w2_load(dc, l + 1)

    for st in range(8):
        w2_prep(0, st)

    for dc in range(16):
        ib = dc % 2
        for pr in range(2):
            ps = PS2[pr]
            rhs = khview[:, 4 * pr + 1: 4 * pr + 5,
                         0:FF].rearrange("q sl (c t) -> q sl c t", t=128)
            khr = [("khT", 4 * pr + k) for k in range(4)]

            def mm_2(ps=ps, ib=ib, rhs=rhs):
                inst = None
                for f in range(64):
                    inst = E["t"].matmul(
                        ps[:, :], w2qT[ib][:, f * 128:(f + 1) * 128],
                        rhs[:, :, f, :], start=(f == 0), stop=False,
                        skip_group_check=True)
                return inst
            p.emit("t", mm_2, reads=khr + [("w2qT", ib)],
                   rw=[("PS2", pr)])
            p.emit("t", lambda ps=ps, pr=pr, ib=ib: E["t"].matmul(
                ps[:, :], rowbf2[ib][0:1, :],
                s2negrow[0:1, pr * 512:(pr + 1) * 512],
                start=False, stop=True, skip_group_check=True),
                reads=[("rowbf2", ib), ("s2negrow",)], rw=[("PS2", pr)])
            p.emit("v", lambda ps=ps: E["v"].tensor_copy(
                scr_bf[0][:, :], ps[:, :]),
                reads=[("PS2", pr)], writes=[SB[0]])
            dstb = scr_bf[1][:, :].rearrange("q (c t) -> q c t", t=128)
            p.emit("y", lambda dstb=dstb: E["y"].dma_start_transpose(
                dstb, scr_bf[0][:, :]),
                reads=[SB[0]], writes=[SB[1]], xp=True)
            for tb in range(4):
                mt = 4 * pr + tb
                p.emit("v", lambda tb=tb, mt=mt: E["v"].tensor_single_scalar(
                    scr[tb][:, 0:128], scr_bf[1][:, tb * 128:(tb + 1) * 128],
                    col(COL_R2, mt), OP.mult),
                    reads=[SB[1], ck_(COL_R2, mt)], writes=[SK[tb]])
                p.emit("g", lambda tb=tb, mt=mt, dc=dc: E["g"].dma_start(
                    out=d_out[mt * 128:(mt + 1) * 128,
                              dc * 128:(dc + 1) * 128],
                    in_=scr[tb][:, 0:128], accum_op=OP.add),
                    reads=[SK[tb]], rw=[("out", mt)], dma=True, lazy=True)
            p.flush("gd")
            for st in range(4):
                w2_prep(dc + 1, pr * 4 + st)
    stkP6.__exit__(None, None, None)
    stk6.__exit__(None, None, None)
    stk56.__exit__(None, None, None)
    stkF.__exit__(None, None, None)
    p.gate()

    for _sn in ("gd", "yd", "sd", "xd", "t", "v", "s", "y"):
        E["g"].wait_ge(p.sems[_sn], p.cnt[_sn])
    nc.gpsimd.sem_clear(p.sems["bt"])
    stk.__exit__(None, None, None)
    return nc


_NC_CACHE = [None]


def kernel(**inputs):
    x = np.ascontiguousarray(np.asarray(inputs["L0_attn_x"], np.float32))
    wqkv = np.ascontiguousarray(np.asarray(inputs["L0_attn_w_qkv"], np.float32))
    wout = np.ascontiguousarray(np.asarray(inputs["L0_attn_w_out"], np.float32))
    w1 = np.ascontiguousarray(np.asarray(inputs["L0_ffn_w1"], np.float32))
    b1 = np.ascontiguousarray(np.asarray(inputs["L0_ffn_b1"],
                                         np.float32).reshape(1, FF))
    gm = np.ascontiguousarray(np.asarray(inputs["L0_ffn_gamma"],
                                         np.float32).reshape(1, FF))
    bt = np.ascontiguousarray(np.asarray(inputs["L0_ffn_beta"],
                                         np.float32).reshape(1, FF))
    w2 = np.ascontiguousarray(np.asarray(inputs["L0_ffn_w2"], np.float32))
    b2 = np.ascontiguousarray(np.asarray(inputs["L0_ffn_b2"],
                                         np.float32).reshape(1, D))

    kr = np.arange(128)[:, None]
    qr = np.arange(128)[None, :]
    masks = {}
    for half in range(2):
        masks[half] = (
            np.ascontiguousarray(
                (kr <= 2 * qr + half).astype(ml_dtypes.bfloat16)),
            np.ascontiguousarray(
                (kr <= 2 * qr + half - 128).astype(ml_dtypes.bfloat16)))

    if _NC_CACHE[0] is None:
        _NC_CACHE[0] = build_program()
    nc = _NC_CACHE[0]

    in_maps = []
    for c in range(8):
        b, half = c // 2, c % 2
        in_maps.append({
            "x": x[b],
            "xmine": np.ascontiguousarray(x[b, half::2]),
            "wqkv": wqkv, "wout": wout, "w1": w1, "w2": w2,
            "b1": b1, "gamma": gm, "beta": bt, "b2": b2,
            "mask0": masks[half][0],
            "mask1": masks[half][1],
        })
    res = run_bass_kernel_spmd(nc, in_maps, core_ids=list(range(8)))
    out = np.empty((B, S, D), np.float32)
    for c in range(8):
        b, half = c // 2, c % 2
        out[b, half::2] = res.results[c]["out"]
    return out


# version 18 (final)
# speedup vs baseline: 1.1581x; 1.0582x over previous
import numpy as np
import ml_dtypes
import concourse.bass as bass
import concourse.mybir as mybir
from concourse.bass_utils import run_bass_kernel_spmd
from concourse.alu_op_type import AluOpType as OP
from contextlib import ExitStack

F32 = mybir.dt.float32
BF16 = mybir.dt.bfloat16
FP16 = mybir.dt.float16
I32 = mybir.dt.int32
AF = mybir.ActivationFunctionType
AX = mybir.AxisListType.X

B, S, D, FF, H, HD = 4, 2048, 2048, 8192, 16, 128
NT = S // 128
MT = 8
MQ = MT * 128
EPS = 1e-5
SCALE = float(1.0 / np.sqrt(HD))
HG = 4
NGRP = H // HG
FC1 = 32
ARN = 9
DEBUG = False
STOP_AFTER_S0 = False
S0_SKIP_MINE = False
S0_NO_XBAR = False


class P:
    def __init__(self, nc, stk):
        self.nc = nc
        self.eng = {"t": nc.tensor, "v": nc.vector, "s": nc.scalar,
                    "g": nc.gpsimd, "y": nc.sync}
        self.sems, self.cnt, self.waited, self.bufs = {}, {}, {}, {}
        for name in ("t", "v", "s", "g", "y", "gd", "yd", "sd", "xd", "bt"):
            self.sems[name] = stk.enter_context(nc.semaphore("sem_" + name))
            self.cnt[name] = 0

        self.gatetoks = []
        self.pending = {}
        self.issuer = {"gd": "g", "yd": "y", "sd": "s", "xd": "y"}

    def flush(self, sem):
        en = self.issuer[sem]
        val = self.cnt[sem]
        for tok in self.pending.get(sem, ()):
            tok[1] = val
        self.pending[sem] = []
        if val > 0 and self.waited.get((en, sem), -1) < val:
            self.eng[en].wait_ge(self.sems[sem], val)
            self.waited[(en, sem)] = val

    def gate(self):
        for sem in list(self.pending):
            if self.pending[sem]:
                self.flush(sem)
        self.gatetoks = [[sem, self.cnt[sem], "-"] for sem in self.sems
                         if self.cnt[sem] > 0]

    def b(self, *key):
        return self.bufs.setdefault(key, [None, []])

    def _wait(self, en, tok, tsem=None):
        if tok is None:
            return
        sem, val, src = tok
        if sem == tsem and sem in self.issuer:
            return
        if val is None:
            raise RuntimeError(f"consumer on {en} before flush of lazy {sem}")
        k = (en, sem)
        if self.waited.get(k, -1) >= val:
            return
        self.eng[en].wait_ge(self.sems[sem], val)
        self.waited[k] = val

    def emit(self, en, fn, reads=(), writes=(), rw=(), dma=False, xp=False,
             lazy=False):
        reads = [self.b(*k) for k in reads]
        writes = [self.b(*k) for k in writes]
        rw = [self.b(*k) for k in rw]
        tsem = "xd" if xp else (en + "d") if dma else None
        for buf in reads:
            self._wait(en, buf[0], tsem)
        for buf in writes + rw:
            if buf[0] is None and not buf[1]:
                for t in self.gatetoks:
                    self._wait(en, t)
            self._wait(en, buf[0], tsem)
            for t in buf[1]:
                self._wait(en, t, tsem)
        inst = fn()
        sem = "xd" if xp else (en + ("d" if dma else ""))
        n = 16 if (dma or xp) else 1
        self.cnt[sem] += n
        inst.then_inc(self.sems[sem], n)
        if (dma or xp) and not lazy:
            self.eng[en].wait_ge(self.sems[sem], self.cnt[sem])
            self.waited[(en, sem)] = self.cnt[sem]
        tok = [sem, None if ((dma or xp) and lazy) else self.cnt[sem],
               "-" if (dma or xp) else en]
        if (dma or xp) and lazy:
            self.pending.setdefault(sem, []).append(tok)
        for buf in reads:
            buf[1].append(tok)
        for buf in writes + rw:
            buf[0] = tok
            buf[1] = []
        return tok


def build_program():
    nc = bass.Bass("TRN2", target_bir_lowering=False, debug=False)
    stk = ExitStack()
    stk.__enter__()

    d_x = nc.dram_tensor("x", [S, D], F32, kind="ExternalInput").ap()
    d_xm = nc.dram_tensor("xmine", [MQ, D], F32, kind="ExternalInput").ap()
    d_wqkv = nc.dram_tensor("wqkv", [3 * D, D], F32, kind="ExternalInput").ap()
    d_wout = nc.dram_tensor("wout", [D, D], F32, kind="ExternalInput").ap()
    d_w1 = nc.dram_tensor("w1", [FF, D], F32, kind="ExternalInput").ap()
    d_w2 = nc.dram_tensor("w2", [D, FF], F32, kind="ExternalInput").ap()
    d_b1 = nc.dram_tensor("b1", [1, FF], F32, kind="ExternalInput").ap()
    d_gm = nc.dram_tensor("gamma", [1, FF], F32, kind="ExternalInput").ap()
    d_bt = nc.dram_tensor("beta", [1, FF], F32, kind="ExternalInput").ap()
    d_b2 = nc.dram_tensor("b2", [1, D], F32, kind="ExternalInput").ap()
    d_m0 = nc.dram_tensor("mask0", [128, 128], BF16, kind="ExternalInput").ap()
    d_m1 = nc.dram_tensor("mask1", [128, 128], BF16, kind="ExternalInput").ap()
    d_out = nc.dram_tensor("out", [MQ, D], F32, kind="ExternalOutput").ap()
    spill_kind = "ExternalOutput" if DEBUG else "Internal"
    d_kx = nc.dram_tensor("kx_spill", [MT, 128, NT * 128], BF16,
                          kind=spill_kind).ap()
    d_y = nc.dram_tensor("y_spill", [128, H, MQ], BF16, kind=spill_kind).ap()
    if DEBUG:
        d_dbg = nc.dram_tensor("dbg", [128, 256], F32,
                               kind="ExternalOutput").ap()
        d_dbg2 = nc.dram_tensor("dbg2", [128, 1600], F32,
                                kind="ExternalOutput").ap()
        d_dxn = nc.dram_tensor("dbg_xn", [128, NT * S], BF16,
                               kind="ExternalOutput").ap()

    p = P(nc, stk)
    E = p.eng
    BOOT = 1 << 24
    for _sn in ("bt", "t", "v", "s", "g", "y", "gd", "yd", "sd", "xd"):
        nc.gpsimd.sem_clear(p.sems[_sn])
    nc.gpsimd.sem_inc(p.sems["bt"], BOOT)
    for _e in ("t", "v", "s", "y"):
        p.eng[_e].wait_ge(p.sems["bt"], BOOT)

    def sball(st, name, shape, dt):
        return st.enter_context(nc.sbuf_tensor(name, shape, dt))

    ones_c_bf = sball(stk, "ones_c_bf", [128, 1], BF16)
    ones_c_f = sball(stk, "ones_c_f", [128, 1], F32)
    ones_r_f = sball(stk, "ones_r_f", [1, 128], F32)
    ident_bf = sball(stk, "ident_bf", [128, 128], BF16)
    m0 = sball(stk, "m0", [128, 128], BF16)
    m1 = sball(stk, "m1", [128, 128], BF16)
    snegrow = sball(stk, "snegrow", [1, MQ], BF16)
    colp = sball(stk, "colp", [128, 16 * MT], F32)
    (COL_SNEG, COL_R1, COL_R2, COL_M2, COL_A, COL_BB, COL_RS, COL_MS2,
     COL_S, COL_S2) = range(10)

    def col(w, mt):
        return colp[:, w * MT + mt: w * MT + mt + 1]

    def ck_(w, mt):
        return ("col", w, mt)

    thbc = sball(stk, "thbc", [128, 6], F32)
    TH1, NTH1, TH2, NTH2, MEAN1, MEAN2 = range(6)

    def thc(i):
        return thbc[:, i:i + 1]

    scal = sball(stk, "scal", [1, 32], F32)
    (SC_MQ2, SC_MK2, SC_NEGC, SC_T0, SC_T1, SC_NR1, SC_NR2) = range(7)
    SC_ROW = 16

    def sc(i):
        return scal[0:1, i: i + 1]

    def sk(i):
        return ("scal", i)

    negc_bc = sball(stk, "negc_bc", [128, 4], F32)
    acc1 = sball(stk, "acc1", [128, 1], F32)
    acc2 = sball(stk, "acc2", [128, 1], F32)
    accs = sball(stk, "accs", [128, 1], F32)
    stats1 = sball(stk, "stats1", [128, MT * FC1 * 2], F32)
    ln_sc = sball(stk, "ln_sc", [128, 10], F32)

    def lc(i):
        return ln_sc[:, i:i + 1]

    def lk(i):
        return ("lnc", i)

    tiny_bf = sball(stk, "tiny_bf", [128, 1], BF16)
    eps_col = sball(stk, "eps_col", [128, 1], F32)
    SK = [("scr", i) for i in range(4)]
    scr_bf = [sball(stk, f"scrbf{i}", [128, 512], BF16) for i in range(2)]
    SB = [("scrbf", i) for i in range(2)]

    p.emit("v", lambda: E["v"].memset(ones_c_bf[:, :], 1.0), writes=[("c", 1)])
    p.emit("v", lambda: E["v"].memset(ones_c_f[:, :], 1.0), writes=[("c", 2)])
    p.emit("v", lambda: E["v"].memset(ones_r_f[:, :], 1.0), writes=[("c", 3)])
    p.emit("g", lambda: E["g"].memset(ident_bf[:, :], 1.0), writes=[("c", 5)])
    p.emit("g", lambda: E["g"].affine_select(
        ident_bf[:, :], ident_bf[:, :], pattern=[[1, 128]],
        compare_op=OP.is_equal, fill=0.0, base=0, channel_multiplier=-1),
        rw=[("c", 5)])
    p.emit("g", lambda: E["g"].dma_start(out=m0[:, :], in_=d_m0[:, :]),
           writes=[("c", 6)], dma=True)
    p.emit("g", lambda: E["g"].dma_start(out=m1[:, :], in_=d_m1[:, :]),
           writes=[("c", 7)], dma=True)
    p.emit("v", lambda: E["v"].memset(eps_col[:, :], float(EPS)),
           writes=[("c", 8)])
    p.emit("v", lambda: E["v"].memset(acc1[:, :], 0.0), writes=[("acc1",)])
    p.emit("v", lambda: E["v"].memset(acc2[:, :], 0.0), writes=[("acc2",)])

    def ln_tile(xkey, xtile):
        p.emit("v", lambda: E["v"].reduce_sum(lc(3), xtile[:, :], axis=AX),
               reads=[xkey], writes=[lk(3)])
        def _sq(xtile=xtile):
            for c in range(4):
                E["s"].activation(sqdump[:, c * 512:(c + 1) * 512],
                                  xtile[:, c * 512:(c + 1) * 512], AF.Square,
                                  accum_out=stats[:, c:c + 1])
            E["s"].drain()
            return E["s"].nop()
        p.emit("s", _sq, reads=[xkey], writes=[("sqdump",), ("stats",)])
        p.emit("v", lambda: E["v"].reduce_sum(lc(5), stats[:, 0:4], axis=AX),
               reads=[("stats",)], writes=[lk(5)])
        p.emit("v", lambda: E["v"].tensor_scalar_mul(lc(0), lc(3), 1.0 / D),
               reads=[lk(3)], writes=[lk(0)])
        p.emit("v", lambda: E["v"].tensor_mul(lc(6), lc(0), lc(0)),
               reads=[lk(0)], writes=[lk(6)])
        p.emit("v", lambda: E["v"].scalar_tensor_tensor(
            lc(9), lc(5), 1.0 / D, lc(6), OP.mult, OP.subtract),
            reads=[lk(5), lk(6)], writes=[lk(9)])
        p.emit("s", lambda: E["s"].activation(
            lc(2), lc(9), AF.Sqrt, bias=eps_col[:, :], scale=1.0),
            reads=[lk(9), ("c", 8)], writes=[lk(2)])
        p.emit("v", lambda: E["v"].reciprocal(lc(1), lc(2)),
               reads=[lk(2)], writes=[lk(1)])
        p.emit("v", lambda: E["v"].scalar_tensor_tensor(
            lc(2), lc(0), -1.0, lc(1), OP.mult, OP.mult),
            reads=[lk(0), lk(1)], rw=[lk(2)])
        return lk(1), lk(2)

    stkA = ExitStack()
    stkA.__enter__()
    xnallT = sball(stkA, "xnallT", [128, NT * S], BF16)
    xnmT = sball(stkA, "xnmT", [128, NT * MQ], BF16)


    stk0 = ExitStack()
    stk0.__enter__()
    xs = [sball(stk0, f"xs{i}", [128, D], F32) for i in range(2)]
    xnbf = [sball(stk0, f"xnbf{i}", [128, D], BF16) for i in range(2)]
    kxs = sball(stk0, "kxs", [128, D], BF16)
    kxT = sball(stk0, "kxT", [128, NT * 128], BF16)
    sqdump = sball(stk0, "sqdump", [128, D], BF16)
    scr = [sball(stk0, f"scr0_{i}", [128, 512], F32) for i in range(4)]
    abuf = [sball(stk0, f"abuf{i}", [128, 1024], BF16) for i in range(2)]

    absq = ([("w1", ch, h_) for ch in range(FF // 128) for h_ in range(2)] +
            [("w2", ch, fs) for ch in range(D // 128) for fs in range(8)])
    nab = [0]

    def abs_load(item, i):
        if item[0] == "w1":
            ch, h_ = item[1], item[2]
            p.emit("g", lambda ch=ch, i=i, h_=h_: E["g"].dma_start(
                out=abuf[i][:, :],
                in_=d_w1[ch * 128:(ch + 1) * 128,
                         h_ * 1024:(h_ + 1) * 1024]),
                writes=[("ab", i)], dma=True, lazy=True)
        else:
            ch, fs = item[1], item[2]
            p.emit("g", lambda ch=ch, fs=fs, i=i: E["g"].dma_start(
                out=abuf[i][:, :],
                in_=d_w2[ch * 128:(ch + 1) * 128,
                         fs * 1024:(fs + 1) * 1024]),
                writes=[("ab", i)], dma=True, lazy=True)

    def emit_absmean(n):
        while n > 0 and absq:
            item = absq.pop(0)
            n -= 1
            i = nab[0] % 2
            if nab[0] == 0:
                abs_load(item, 0)
            if len(absq):
                abs_load(absq[0], (i + 1) % 2)
            p.flush("gd")
            nab[0] += 1
            acc = acc1 if item[0] == "w1" else acc2
            ak = ("acc1",) if item[0] == "w1" else ("acc2",)
            p.emit("v", lambda i=i: E["v"].reduce_sum(
                accs[:, :], abuf[i][:, :], axis=AX,
                apply_absolute_value=True),
                reads=[("ab", i)], writes=[("accs",)])
            p.emit("v", lambda acc=acc: E["v"].tensor_add(
                acc[:, :], acc[:, :], accs[:, :]),
                reads=[("accs",)], rw=[ak])

    stats = sball(stk0, "stats", [128, 40], F32)
    scr_i = sball(stk0, "scr_i0", [128, 512], I32)

    for kt in range(NT + (0 if S0_SKIP_MINE else MT)):
        mine = kt >= NT
        mt = kt - NT
        i = kt % 2
        if not mine:
            p.emit("y", lambda kt=kt, i=i: E["y"].dma_start(
                out=xs[i][:, :], in_=d_x[kt * 128:(kt + 1) * 128, :]),
                writes=[("xs", i)], dma=True)
        else:
            p.emit("y", lambda mt=mt, i=i: E["y"].dma_start(
                out=xs[i][:, :], in_=d_xm[mt * 128:(mt + 1) * 128, :]),
                writes=[("xs", i)], dma=True)
            p.emit("g", lambda mt=mt, i=i: E["g"].dma_start(
                out=d_out[mt * 128:(mt + 1) * 128, :], in_=xs[i][:, :]),
                reads=[("xs", i)], writes=[("out", mt)], dma=True)
        t_rstd, t_ms = ln_tile(("xs", i), xs[i])
        p.emit("s", lambda i=i: E["s"].activation(
            xnbf[i][:, :], xs[i][:, :], AF.Identity, bias=lc(2), scale=lc(1)),
            reads=[("xs", i), t_rstd, t_ms], writes=[("xnbf", i)])
        if not mine:
            if not S0_NO_XBAR:
                dst = xnallT[:, :].rearrange("q (c t) -> q c t", t=S)[
                    :, :, kt * 128:(kt + 1) * 128]
                p.emit("y", lambda i=i, dst=dst: E["y"].dma_start_transpose(
                    dst, xnbf[i][:, :]),
                    reads=[("xnbf", i)], rw=[("xnT",)], xp=True)
            emit_absmean(11)
            continue
        if not S0_NO_XBAR:
            dstm = xnmT[:, :].rearrange("q (c t) -> q c t", t=MQ)[
                :, :, mt * 128:(mt + 1) * 128]
            p.emit("y", lambda i=i, dstm=dstm: E["y"].dma_start_transpose(
                dstm, xnbf[i][:, :]),
                reads=[("xnbf", i)], rw=[("xnmT",)], xp=True)
        if DEBUG and mt == 0:
            p.emit("y", lambda: E["y"].dma_start(
                out=d_dbg2[:, 512:522], in_=ln_sc[:, :]),
                reads=[lk(0), lk(1), lk(2)], writes=[("dbgx", 1)], dma=True)
            p.emit("y", lambda: E["y"].dma_start(
                out=d_dbg2[:, 522:546], in_=stats[:, 0:24]),
                reads=[("stats",)], writes=[("dbgx", 2)], dma=True)
            p.emit("y", lambda i=i: E["y"].dma_start(
                out=d_dbg2[:, 1024:1536], in_=xs[i][:, 0:512]),
                reads=[("xs", i)], writes=[("dbgx", 3)], dma=True)
        for c in range(4):
            p.emit("v", lambda i=i, c=c: E["v"].tensor_scalar(
                scr[0][:, :], xs[i][:, c * 512:(c + 1) * 512],
                lc(1), lc(2), OP.mult, OP.add),
                reads=[("xs", i), t_rstd, t_ms], writes=[SK[0]])
            if DEBUG and mt == 0 and c == 0:
                p.emit("y", lambda: E["y"].dma_start(
                    out=d_dbg2[:, 0:512], in_=scr[0][:, :]),
                    reads=[SK[0]], writes=[("dbgx", 0)], dma=True)
            p.emit("v", lambda: E["v"].reduce_max(
                lc(3), scr[0][:, :], axis=AX, apply_absolute_value=True),
                reads=[SK[0]], writes=[lk(3)])
            if c == 0:
                p.emit("v", lambda: E["v"].tensor_copy(lc(4), lc(3)),
                       reads=[lk(3)], writes=[lk(4)])
            else:
                p.emit("v", lambda: E["v"].tensor_max(lc(4), lc(4), lc(3)),
                       reads=[lk(3)], rw=[lk(4)])
        p.emit("v", lambda: E["v"].tensor_scalar_max(lc(5), lc(4), float(EPS)),
               reads=[lk(4)], writes=[lk(5)])
        p.emit("v", lambda: E["v"].reciprocal(lc(6), lc(5)),
               reads=[lk(5)], writes=[lk(6)])
        p.emit("v", lambda mt=mt: E["v"].tensor_scalar_mul(
            col(COL_S, mt), lc(6), 127.0),
            reads=[lk(6)], writes=[ck_(COL_S, mt)])
        p.emit("v", lambda mt=mt: E["v"].tensor_scalar_mul(
            col(COL_SNEG, mt), col(COL_S, mt), -1.0),
            reads=[ck_(COL_S, mt)], writes=[ck_(COL_SNEG, mt)])
        p.emit("v", lambda mt=mt: E["v"].tensor_mul(
            col(COL_RS, mt), lc(1), col(COL_S, mt)),
            reads=[ck_(COL_S, mt), t_rstd], writes=[ck_(COL_RS, mt)])
        p.emit("v", lambda mt=mt: E["v"].tensor_mul(
            col(COL_MS2, mt), lc(2), col(COL_S, mt)),
            reads=[ck_(COL_S, mt), t_ms], writes=[ck_(COL_MS2, mt)])
        if DEBUG:
            p.emit("v", lambda mt=mt: E["v"].tensor_copy(
                colp[:, 80 + 5 * mt: 85 + 5 * mt], ln_sc[:, 0:5]),
                reads=[lk(0), lk(1), lk(4)], writes=[("dbgst", mt)])
        for c in range(4):
            p.emit("v", lambda i=i, c=c, mt=mt: E["v"].tensor_scalar(
                scr[0][:, :], xs[i][:, c * 512:(c + 1) * 512],
                col(COL_RS, mt), col(COL_MS2, mt), OP.mult, OP.add),
                reads=[("xs", i), ck_(COL_RS, mt), ck_(COL_MS2, mt)],
                writes=[SK[0]])
            p.emit("v", lambda: E["v"].tensor_copy(scr_i[:, :], scr[0][:, :]),
                   reads=[SK[0]], writes=[("scri",)])
            p.emit("v", lambda: E["v"].tensor_copy(scr[1][:, :], scr_i[:, :]),
                   reads=[("scri",)], writes=[SK[1]])
            p.emit("g", lambda: E["g"].tensor_sub(
                scr[2][:, :], scr[0][:, :], scr[1][:, :]),
                reads=[SK[0], SK[1]], writes=[SK[2]])
            p.emit("v", lambda: E["v"].scalar_tensor_tensor(
                scr[3][:, :], scr[2][:, :], 0.5, scr[1][:, :],
                OP.is_ge, OP.add),
                reads=[SK[2], SK[1]], writes=[SK[3]])
            p.emit("v", lambda c=c: E["v"].scalar_tensor_tensor(
                kxs[:, c * 512:(c + 1) * 512], scr[2][:, :], -0.5, scr[3][:, :],
                OP.is_le, OP.subtract),
                reads=[SK[2], SK[3]], rw=[("kxs",)])
        if not S0_NO_XBAR:
            dstk = kxT[:, :].rearrange("q (c t) -> q c t", t=128)
            p.emit("y", lambda dstk=dstk: E["y"].dma_start_transpose(
                dstk, kxs[:, :]),
                reads=[("kxs",)], writes=[("kxT",)], xp=True)
            p.emit("y", lambda mt=mt: E["y"].dma_start(
                out=d_kx[mt, :, :], in_=kxT[:, :]),
                reads=[("kxT",)], writes=[("kxd", mt)], dma=True, lazy=True)
            p.flush("yd")
        emit_absmean(11)

    stkT = ExitStack()
    stkT.__enter__()
    ps_tr = stkT.enter_context(nc.psum_tensor("ps_tr", [1, 128], BF16))
    for mt in range(0 if S0_SKIP_MINE else MT):
        p.emit("v", lambda mt=mt: E["v"].tensor_copy(
            tiny_bf[:, :], col(COL_SNEG, mt)),
            reads=[ck_(COL_SNEG, mt)], writes=[("tinybf",)])
        p.emit("t", lambda: E["t"].transpose(
            ps_tr[:, :], tiny_bf[:, :], ident_bf[:, :]),
            reads=[("tinybf",), ("c", 5)], writes=[("PStr",)])
        p.emit("v", lambda mt=mt: E["v"].tensor_copy(
            snegrow[0:1, mt * 128:(mt + 1) * 128], ps_tr[:, :]),
            reads=[("PStr",)], rw=[("snegrow",)])
    stkT.__exit__(None, None, None)
    if STOP_AFTER_S0:
        if DEBUG:
            p.emit("y", lambda: E["y"].dma_start(out=d_dxn[:, :],
                   in_=xnallT[:, :]), reads=[("xnT",)], writes=[("dbg1",)],
                   dma=True)
            p.emit("y", lambda: E["y"].dma_start(out=d_dbg[:, 0:128],
                   in_=colp[:, :], allow_slow_non_contiguous=True),
                   reads=[ck_(COL_S, 0)], writes=[("dbg2",)], dma=True)
        for _sn in ("gd", "yd", "sd", "xd", "t", "v", "s", "y"):
            E["g"].wait_ge(p.sems[_sn], p.cnt[_sn])
        for _sn in ("bt", "t", "v", "s", "g", "y", "gd", "yd", "sd", "xd"):
            nc.gpsimd.sem_clear(p.sems[_sn])
        stk0.__exit__(None, None, None)
        stkA.__exit__(None, None, None)
        stk.__exit__(None, None, None)
        return nc
    stk0.__exit__(None, None, None)
    p.gate()

    stk12 = ExitStack()
    stk12.__enter__()
    QT = sball(stk12, "QT", [128, HG * MQ], BF16)
    KT = sball(stk12, "KT", [128, HG * S], BF16)
    Vt = sball(stk12, "Vt", [128, NT * 512], BF16)
    wqT = sball(stk12, "wqT", [128, NT * 128], BF16)
    wkT = sball(stk12, "wkT", [128, NT * 128], BF16)
    wvT = sball(stk12, "wvT", [128, NT * 512], BF16)
    wstage = [sball(stk12, "wstage0", [128, D], BF16)] * 2
    Pb = [sball(stk12, f"Pb{i}", [128, MQ], BF16) for i in range(2)]
    sbc = [sball(stk12, f"sbc{i}", [128, 512], BF16) for i in range(2)]
    sqs = sball(stk12, "sqs", [128, MQ], BF16)
    srec = sball(stk12, "srec", [1, MQ], F32)

    stkP = ExitStack()
    stkP.__enter__()
    PS4 = [stkP.enter_context(nc.psum_tensor(f"ps4_{i}", [128, 512], F32))
           for i in range(4)]
    ps_y = [stkP.enter_context(nc.psum_tensor(f"ps_y{i}", [128, 512], F32))
            for i in range(2)]
    ps_s = [stkP.enter_context(nc.psum_tensor(f"ps_s{i}", [1, 512], F32))
            for i in range(2)]
    K4 = [("PS4", i) for i in range(4)]
    KY = [("PSy", i) for i in range(2)]
    KS = [("PSs", i) for i in range(2)]

    nwq = [0]

    def wxpose(dram, row0, dstap, dkey):
        i = nwq[0] % 2
        nwq[0] += 1
        p.emit("g", lambda: E["g"].dma_start(
            out=wstage[i][:, :], in_=dram[row0:row0 + 128, :]),
            writes=[("ws", 0)], dma=True)
        p.emit("y", lambda: E["y"].dma_start_transpose(dstap, wstage[i][:, :]),
               reads=[("ws", 0)], writes=[dkey], xp=True)

    for hg in range(NGRP):
        for hl in range(HG):
            h = hg * HG + hl
            wq_dst = wqT[:, :].rearrange("q (c t) -> q c t", t=128)
            wk_dst = wkT[:, :].rearrange("q (c t) -> q c t", t=128)
            wv_dst = wvT[:, :].rearrange("q (c t) -> q c t", t=512)[
                :, :, hl * 128:(hl + 1) * 128]
            wxpose(d_wqkv, h * 128, wq_dst, ("wqT",))
            for qg in range(2):
                ps = PS4[qg]

                def mm_q(ps=ps, qg=qg):
                    inst = None
                    for d in range(16):
                        inst = E["t"].matmul(
                            ps[:, :], wqT[:, d * 128:(d + 1) * 128],
                            xnmT[:, d * MQ + qg * 512: d * MQ + (qg + 1) * 512],
                            start=(d == 0), stop=(d == 15))
                    return inst
                p.emit("t", mm_q, reads=[("wqT",), ("xnmT",)], writes=[K4[qg]])
                p.emit("v", lambda ps=ps, hl=hl, qg=qg: E["v"].tensor_copy(
                    QT[:, hl * MQ + qg * 512: hl * MQ + (qg + 1) * 512],
                    ps[:, :]),
                    reads=[K4[qg]], rw=[("QT", hl)])
            wxpose(d_wqkv, D + h * 128, wk_dst, ("wkT",))
            for kg in range(4):
                ps = PS4[kg]

                def mm_k(ps=ps, kg=kg):
                    inst = None
                    for d in range(16):
                        inst = E["t"].matmul(
                            ps[:, :], wkT[:, d * 128:(d + 1) * 128],
                            xnallT[:, d * S + kg * 512: d * S + (kg + 1) * 512],
                            start=(d == 0), stop=(d == 15))
                    return inst
                p.emit("t", mm_k, reads=[("wkT",), ("xnT",)], writes=[K4[kg]])
                p.emit("v", lambda ps=ps, hl=hl, kg=kg: E["v"].tensor_copy(
                    KT[:, hl * S + kg * 512: hl * S + (kg + 1) * 512],
                    ps[:, :]),
                    reads=[K4[kg]], rw=[("KT", hl)])
            wxpose(d_wqkv, 2 * D + h * 128, wv_dst, ("wvT", hl))
        for kt in range(NT):
            ps = PS4[kt % 4]

            def mm_v(ps=ps, kt=kt):
                inst = None
                for d in range(16):
                    inst = E["t"].matmul(
                        ps[:, :],
                        xnallT[:, d * S + kt * 128: d * S + (kt + 1) * 128],
                        wvT[:, d * 512:(d + 1) * 512],
                        start=(d == 0), stop=(d == 15))
                return inst
            p.emit("t", mm_v,
                   reads=[("xnT",), ("wvT", 0), ("wvT", 1), ("wvT", 2),
                          ("wvT", 3)],
                   writes=[K4[kt % 4]])
            p.emit("v", lambda ps=ps, kt=kt: E["v"].tensor_copy(
                Vt[:, kt * 512:(kt + 1) * 512], ps[:, :]),
                reads=[K4[kt % 4]], writes=[("Vt", kt)])

        CQ2, CK2, CPR, CNC = 7, 11, 22, 26
        for hl in range(HG):
            p.emit("s", lambda hl=hl: E["s"].activation(
                sqs[:, 0:MQ], QT[:, hl * MQ:(hl + 1) * MQ], AF.Square),
                reads=[("QT", hl)], writes=[("sqs",)])
            for qg in range(2):
                p.emit("t", lambda qg=qg: E["t"].matmul(
                    ps_s[qg][:, :], ones_c_bf[:, :],
                    sqs[:, qg * 512:(qg + 1) * 512], start=True, stop=True),
                    reads=[("sqs",), ("c", 1)], writes=[KS[qg]])
                p.emit("v", lambda qg=qg, hl=hl: E["v"].reduce_max(
                    sc(CQ2 + hl) if qg == 0 else sc(SC_T0),
                    ps_s[qg][:, :], axis=AX),
                    reads=[KS[qg]],
                    writes=[sk(CQ2 + hl) if qg == 0 else sk(SC_T0)])
            p.emit("v", lambda hl=hl: E["v"].tensor_max(
                sc(CQ2 + hl), sc(CQ2 + hl), sc(SC_T0)),
                reads=[sk(SC_T0)], rw=[sk(CQ2 + hl)])
            for kg in range(4):
                p.emit("s", lambda hl=hl, kg=kg: E["s"].activation(
                    sqs[:, 0:512],
                    KT[:, hl * S + kg * 512: hl * S + (kg + 1) * 512],
                    AF.Square),
                    reads=[("KT", hl)], rw=[("sqs",)])
                p.emit("t", lambda kg=kg: E["t"].matmul(
                    ps_s[kg % 2][:, :], ones_c_bf[:, :], sqs[:, 0:512],
                    start=True, stop=True),
                    reads=[("sqs",), ("c", 1)], writes=[KS[kg % 2]])
                p.emit("v", lambda kg=kg, hl=hl: E["v"].reduce_max(
                    sc(CK2 + hl) if kg == 0 else sc(SC_T0),
                    ps_s[kg % 2][:, :], axis=AX),
                    reads=[KS[kg % 2]],
                    writes=[sk(CK2 + hl) if kg == 0 else sk(SC_T0)])
                if kg:
                    p.emit("v", lambda hl=hl: E["v"].tensor_max(
                        sc(CK2 + hl), sc(CK2 + hl), sc(SC_T0)),
                        reads=[sk(SC_T0)], rw=[sk(CK2 + hl)])
            p.emit("v", lambda hl=hl: E["v"].tensor_mul(
                sc(CPR + hl), sc(CQ2 + hl), sc(CK2 + hl)),
                reads=[sk(CQ2 + hl), sk(CK2 + hl)], writes=[sk(CPR + hl)])
        p.emit("s", lambda: E["s"].activation(
            scal[0:1, CPR:CPR + 4], scal[0:1, CPR:CPR + 4], AF.Sqrt),
            rw=[sk(CPR + hl_) for hl_ in range(HG)])
        p.emit("v", lambda: E["v"].tensor_scalar_mul(
            scal[0:1, CNC:CNC + 4], scal[0:1, CPR:CPR + 4], -SCALE * 1.0001),
            reads=[sk(CPR + hl_) for hl_ in range(HG)],
            writes=[sk(CNC + hl_) for hl_ in range(HG)])
        for hl in range(HG):
            p.emit("t", lambda hl=hl: E["t"].matmul(
                PS4[3][:, 0:1], ones_r_f[:, :], sc(CNC + hl),
                start=True, stop=True),
                reads=[sk(CNC + hl), ("c", 3)], writes=[K4[3]])
            p.emit("v", lambda hl=hl: E["v"].tensor_copy(
                negc_bc[:, hl:hl + 1], PS4[3][:, 0:1]),
                reads=[K4[3]], writes=[("negc", hl)])

        for hl in range(HG):
            h = hg * HG + hl
            pieces = []
            for j in range(NT):
                qs = 128 * (j // 2)
                if qs < 512:
                    pieces.append((j, qs, qs, 512))
                    pieces.append((j, qs, 512, 1024))
                else:
                    pieces.append((j, qs, qs, 1024))

            def stmm(idx, hl=hl):
                j, qs, a, b2 = pieces[idx]
                w = b2 - a
                psi = idx % 2
                p.emit("t", lambda j=j, a=a, w=w, psi=psi, hl=hl:
                       E["t"].matmul(
                           PS4[psi][:, 0:w],
                           KT[:, hl * S + j * 128: hl * S + (j + 1) * 128],
                           QT[:, hl * MQ + a: hl * MQ + a + w],
                           start=True, stop=True),
                       reads=[("KT", hl), ("QT", hl)], writes=[K4[psi]])

            stmm(0)
            for idx in range(len(pieces)):
                j, qs, a, b2 = pieces[idx]
                w = b2 - a
                jb = j % 2
                ck = 0 if a < 512 else 1
                psi = idx % 2
                p.emit("s", lambda jb=jb, a=a, w=w, psi=psi, hl=hl:
                       E["s"].activation(
                           Pb[jb][:, a:a + w], PS4[psi][:, 0:w], AF.Exp,
                           bias=negc_bc[:, hl:hl + 1], scale=SCALE),
                       reads=[K4[psi], ("negc", hl)], rw=[("Pb", jb, ck)])
                if a == qs:
                    msk, mkey = (m0, 6) if j % 2 == 0 else (m1, 7)
                    p.emit("v", lambda jb=jb, qs=qs, msk=msk:
                           E["v"].tensor_mul(
                               Pb[jb][:, qs:qs + 128],
                               Pb[jb][:, qs:qs + 128], msk[:, :]),
                           reads=[("c", mkey)], rw=[("Pb", jb, ck)])
                if idx + 1 < len(pieces):
                    stmm(idx + 1)
                last = (j == 7) if ck == 0 else (j == 15)
                p.emit("t", lambda jb=jb, a=a, w=w, ck=ck, j=j, last=last:
                       E["t"].matmul(
                           ps_s[ck][:, a - ck * 512: a - ck * 512 + w],
                           ones_c_bf[:, :], Pb[jb][:, a:a + w],
                           start=(j == 0), stop=last,
                           skip_group_check=True),
                       reads=[("Pb", jb, ck), ("c", 1)], rw=[KS[ck]])
                p.emit("t", lambda jb=jb, a=a, w=w, ck=ck, j=j, hl=hl,
                       last=last: E["t"].matmul(
                           ps_y[ck][:, a - ck * 512: a - ck * 512 + w],
                           Vt[:, j * 512 + hl * 128:
                              j * 512 + (hl + 1) * 128],
                           Pb[jb][:, a:a + w],
                           start=(j == 0), stop=last,
                           skip_group_check=True),
                       reads=[("Pb", jb, ck), ("Vt", j)], rw=[KY[ck]])
            for ck in range(2):
                p.emit("v", lambda ck=ck: E["v"].reciprocal(
                    srec[0:1, ck * 512:(ck + 1) * 512], ps_s[ck][:, :]),
                    reads=[KS[ck]], rw=[("srec",)])
                p.emit("t", lambda ck=ck: E["t"].matmul(
                    PS4[2 + ck][:, :], ones_r_f[:, :],
                    srec[0:1, ck * 512:(ck + 1) * 512],
                    start=True, stop=True),
                    reads=[("srec",), ("c", 3)], writes=[K4[2 + ck]])
                p.emit("v", lambda ck=ck: E["v"].tensor_copy(
                    sbc[ck][:, :], PS4[2 + ck][:, :]),
                    reads=[K4[2 + ck]], writes=[("sbc", ck)])
                p.emit("v", lambda ck=ck: E["v"].tensor_mul(
                    scr_bf[ck][:, :], ps_y[ck][:, :], sbc[ck][:, :]),
                    reads=[KY[ck], ("sbc", ck)], writes=[SB[ck]])
                p.emit("y", lambda h=h, ck=ck: E["y"].dma_start(
                    out=d_y[:, h, ck * 512:(ck + 1) * 512],
                    in_=scr_bf[ck][:, :]),
                    reads=[SB[ck]], writes=[("ydone", h, ck)], dma=True,
                    lazy=True)
            p.flush("yd")
    stkP.__exit__(None, None, None)
    stk12.__exit__(None, None, None)
    p.gate()

    stkM = ExitStack()
    stkM.__enter__()
    ps_m = stkM.enter_context(nc.psum_tensor("ps_m", [1, 8], F32))
    ps_b6 = stkM.enter_context(nc.psum_tensor("ps_b6", [128, 8], F32))
    for wi in range(2):
        accv = acc1 if wi == 0 else acc2
        acck = ("acc1",) if wi == 0 else ("acc2",)
        p.emit("t", lambda accv=accv, wi=wi: E["t"].matmul(
            ps_m[:, wi:wi + 1], ones_c_f[:, :], accv[:, :],
            start=True, stop=True),
            reads=[acck, ("c", 2)], writes=[("PSm", wi)])
        p.emit("v", lambda wi=wi: E["v"].tensor_scalar(
            sc(SC_T0), ps_m[:, wi:wi + 1], 1.0 / (FF * D), float(EPS),
            OP.mult, OP.max),
            reads=[("PSm", wi)], writes=[sk(SC_T0)])
        p.emit("v", lambda wi=wi: E["v"].tensor_copy(
            scal[0:1, SC_ROW + 4 + wi: SC_ROW + 5 + wi], sc(SC_T0)),
            reads=[sk(SC_T0)], writes=[("scal", SC_ROW + 4 + wi)])
        p.emit("v", lambda wi=wi: E["v"].tensor_scalar_mul(
            scal[0:1, SC_ROW + 2 * wi: SC_ROW + 2 * wi + 1], sc(SC_T0), 0.5),
            reads=[sk(SC_T0)], writes=[("scal", SC_ROW + 2 * wi)])
        p.emit("v", lambda wi=wi: E["v"].tensor_scalar_mul(
            scal[0:1, SC_ROW + 2 * wi + 1: SC_ROW + 2 * wi + 2],
            sc(SC_T0), -0.5),
            reads=[sk(SC_T0)], writes=[("scal", SC_ROW + 2 * wi + 1)])
        p.emit("v", lambda: E["v"].reciprocal(sc(SC_T1), sc(SC_T0)),
               reads=[sk(SC_T0)], writes=[sk(SC_T1)])
        p.emit("v", lambda wi=wi: E["v"].tensor_scalar_mul(
            sc(SC_NR1 if wi == 0 else SC_NR2), sc(SC_T1), -1.0),
            reads=[sk(SC_T1)],
            writes=[sk(SC_NR1 if wi == 0 else SC_NR2)])
    rowk = [("scal", SC_ROW + i) for i in range(6)]
    p.emit("t", lambda: E["t"].matmul(
        ps_b6[:, 0:6], ones_r_f[:, :], scal[0:1, SC_ROW:SC_ROW + 6],
        start=True, stop=True),
        reads=rowk + [("c", 3)], writes=[("PSb6",)])
    p.emit("v", lambda: E["v"].tensor_copy(thbc[:, :], ps_b6[:, 0:6]),
           reads=[("PSb6",)], writes=[("thbc",)])
    for mt in range(MT):
        p.emit("v", lambda mt=mt: E["v"].reciprocal(lc(0), col(COL_S, mt)),
               reads=[ck_(COL_S, mt)], writes=[lk(0)])
        p.emit("v", lambda mt=mt: E["v"].tensor_mul(
            col(COL_R1, mt), thc(MEAN1), lc(0)),
            reads=[lk(0), ("thbc",)], writes=[ck_(COL_R1, mt)])
    if DEBUG:
        p.emit("y", lambda: E["y"].dma_start(out=d_dxn[:, :], in_=xnallT[:, :]),
               reads=[("xnT",)], writes=[("dbg1",)], dma=True)
        p.emit("y", lambda: E["y"].dma_start(out=d_dbg[:, 0:128], in_=colp[:, :], allow_slow_non_contiguous=True),
               reads=[ck_(COL_S, 0)], writes=[("dbg2",)], dma=True)
        p.emit("y", lambda: E["y"].dma_start(out=d_dbg[:, 128:134], in_=thbc[:, :], allow_slow_non_contiguous=True),
               reads=[("thbc",)], writes=[("dbg3",)], dma=True)
        p.emit("y", lambda: E["y"].dma_start(out=d_dbg[0:1, 134:166], in_=scal[:, :]),
               reads=[sk(SC_NR1)], writes=[("dbg4",)], dma=True)
        p.emit("y", lambda: E["y"].dma_start(out=d_dbg[:, 166:167], in_=acc1[:, :], allow_slow_non_contiguous=True),
               reads=[("acc1",)], writes=[("dbg5",)], dma=True)
        p.emit("y", lambda: E["y"].dma_start(out=d_dbg[:, 167:168], in_=acc2[:, :], allow_slow_non_contiguous=True),
               reads=[("acc2",)], writes=[("dbg6",)], dma=True)
        p.emit("y", lambda: E["y"].dma_start(out=d_dbg[:, 168:169], in_=negc_bc[:, :], allow_slow_non_contiguous=True),
               reads=[("negc",)], writes=[("dbg7",)], dma=True)
    stkM.__exit__(None, None, None)
    stkA.__exit__(None, None, None)
    p.gate()

    stk3 = ExitStack()
    stk3.__enter__()
    woT = [sball(stk3, f"woT{i}", [128, NT * 512], BF16) for i in range(2)]
    yld = [sball(stk3, f"yld{i}", [128, NT * 128], BF16) for i in range(2)]
    w3stage = [sball(stk3, f"w3stage{i}", [128, D], BF16) for i in range(2)]
    stkP3 = ExitStack()
    stkP3.__enter__()
    PSO = [stkP3.enter_context(nc.psum_tensor(f"pso{i}", [128, 512], F32))
           for i in range(4)]
    ydk = [("ydone", h_, c_) for h_ in range(H) for c_ in range(2)]
    for eg in range(4):
        ib = eg % 2
        for sub in range(4):
            i = (eg * 4 + sub) % 2
            p.emit("g", lambda eg=eg, sub=sub, i=i: E["g"].dma_start(
                out=w3stage[i][:, :],
                in_=d_wout[eg * 512 + sub * 128:
                           eg * 512 + (sub + 1) * 128, :]),
                writes=[("w3s", i)], dma=True)
            dst = woT[ib][:, :].rearrange("q (c t) -> q c t", t=512)[
                :, :, sub * 128:(sub + 1) * 128]
            p.emit("y", lambda dst=dst, i=i: E["y"].dma_start_transpose(
                dst, w3stage[i][:, :]),
                reads=[("w3s", i)], rw=[("woT", ib)], xp=True)
        for mt in range(MT):
            im = mt % 2
            p.emit("y", lambda mt=mt, im=im: E["y"].dma_start(
                out=yld[im][:, :], in_=d_y[:, :, mt * 128:(mt + 1) * 128]),
                reads=ydk, writes=[("yld", im)], dma=True)
            ps = PSO[mt % 4]

            def mm_o(ps=ps, ib=ib, im=im):
                inst = None
                for d in range(16):
                    inst = E["t"].matmul(
                        ps[:, :], yld[im][:, d * 128:(d + 1) * 128],
                        woT[ib][:, d * 512:(d + 1) * 512],
                        start=(d == 0), stop=(d == 15))
                return inst
            p.emit("t", mm_o, reads=[("yld", im), ("woT", ib)],
                   writes=[("PSO", mt % 4)])
            p.emit("v", lambda ps=ps, mt=mt: E["v"].tensor_copy(
                scr[mt % 2][:, :], ps[:, :]),
                reads=[("PSO", mt % 4)], writes=[SK[mt % 2]])
            p.emit("g", lambda mt=mt, eg=eg: E["g"].dma_start(
                out=d_out[mt * 128:(mt + 1) * 128, eg * 512:(eg + 1) * 512],
                in_=scr[mt % 2][:, :], accum_op=OP.add),
                reads=[SK[mt % 2]], rw=[("out", mt)], dma=True)
    stkP3.__exit__(None, None, None)
    stk3.__exit__(None, None, None)
    p.gate()

    stkF = ExitStack()
    stkF.__enter__()
    garena = sball(stkF, "garena", [128, ARN * FF], BF16)
    gA = garena[:, :]
    gH = garena[:, :].bitcast(FP16)

    stk4 = ExitStack()
    stk4.__enter__()
    kxf = sball(stk4, "kxf", [128, MT * NT * 128], BF16)
    w1stage = [sball(stk4, f"w1stage{i}", [128, 512], F32)
               for i in range(2)]
    w1qT = [sball(stk4, f"w1qT{i}", [128, NT * 256], BF16) for i in range(2)]
    qpos = scr_bf[0][:, :]
    qneg = scr_bf[1][:, :]
    rowa = sball(stk4, "rowa4", [1, 256], F32)[0:1, :]
    rowbf3 = sball(stk4, "rowbf4", [1, 768], BF16)
    rowbf = [rowbf3[0:1, 0:256], rowbf3[0:1, 256:512], rowbf3[0:1, 512:768]]
    sqs4 = sball(stk4, "sqs44", [128, 256], F32)[:, :]
    stkP4 = ExitStack()
    stkP4.__enter__()
    PSH = [stkP4.enter_context(nc.psum_tensor(f"psh{i}", [128, 256], F32))
           for i in range(8)]

    def w1_rowa(fc):
        p.emit("s", lambda fc=fc: E["s"].dma_start(
            out=rowa, in_=d_b1[0:1, fc * 256:(fc + 1) * 256]),
            writes=[("rowa",)], dma=True, lazy=True)

    def w1_rowbf(fc):
        p.flush("sd")
        p.emit("v", lambda ib=fc % 2: E["v"].tensor_single_scalar(
            rowbf[ib], rowa, sc(SC_NR1), OP.mult),
            reads=[("rowa",), sk(SC_NR1)], writes=[("rowbf", fc % 2)])

    def w1_load_l(fc, l):
        fb, q = l // 4, l % 4
        r0 = fc * 256 + fb * 128
        p.emit("s", lambda r0=r0, q=q, l=l: E["s"].dma_start(
            out=w1stage[l % 2][:, :],
            in_=d_w1[r0:r0 + 128, q * 512:(q + 1) * 512]),
            writes=[("w1s", l % 2)], dma=True, lazy=True)

    def w1_quant_l(fc, l):
        ib = fc % 2
        fb, q = l // 4, l % 4
        p.flush("sd")
        p.emit("v", lambda l=l: E["v"].tensor_single_scalar(
            qpos[:, :], w1stage[l % 2][:, :], thc(TH1), OP.is_ge),
            reads=[("w1s", l % 2), ("thbc",)], writes=[("qp",)])
        p.emit("v", lambda l=l: E["v"].tensor_single_scalar(
            qneg[:, :], w1stage[l % 2][:, :], thc(NTH1), OP.is_le),
            reads=[("w1s", l % 2), ("thbc",)], writes=[("qn",)])
        p.emit("g", lambda: E["g"].tensor_sub(
            qpos[:, :], qneg[:, :], qpos[:, :]),
            reads=[("qn",)], rw=[("qp",)])
        dst = w1qT[ib][:, :].rearrange("q (c t) -> q c t", t=256)[
            :, 4 * q:4 * q + 4, fb * 128:(fb + 1) * 128]
        p.emit("y", lambda dst=dst: E["y"].dma_start_transpose(
            dst, qpos[:, :]),
            reads=[("qp",)], rw=[("w1qT", ib)], xp=True)

    def w1_prep(fc, step):
        if fc >= FC1:
            return
        if step == 0:
            w1_rowa(fc)
            w1_load_l(fc, 0)
            w1_load_l(fc, 1)
            w1_rowbf(fc)
        w1_quant_l(fc, step)
        if step + 2 < 8:
            w1_load_l(fc, step + 2)

    for mt in range(MT):
        p.emit("g", lambda mt=mt: E["g"].dma_start(
            out=kxf[:, mt * 2048:(mt + 1) * 2048], in_=d_kx[mt, :, :]),
            reads=[("kxd", mt)], writes=[("kxf", mt)], dma=True, lazy=True)
    p.flush("gd")
    for st in range(8):
        w1_prep(0, st)

    for fc in range(FC1):
        ib = fc % 2
        for mt in range(MT):
            ps = PSH[(fc * MT + mt) % 8]

            def mm_h(ps=ps, ib=ib, mt=mt):
                inst = None
                for d in range(16):
                    inst = E["t"].matmul(
                        ps[:, :],
                        kxf[:, mt * 2048 + d * 128: mt * 2048 + (d + 1) * 128],
                        w1qT[ib][:, d * 256:(d + 1) * 256],
                        start=(d == 0), stop=False,
                        skip_group_check=True)
                return inst
            pk = (fc * MT + mt) % 8
            p.emit("t", mm_h, reads=[("kxf", mt), ("w1qT", ib)],
                   rw=[("PSH", pk)])
            p.emit("t", lambda ps=ps, mt=mt, ib=ib: E["t"].matmul(
                ps[:, :], snegrow[0:1, mt * 128:(mt + 1) * 128],
                rowbf[ib], start=False, stop=True,
                skip_group_check=True),
                reads=[("snegrow",), ("rowbf", ib)], rw=[("PSH", pk)])
            gdst = gA[:, mt * FF + fc * 256: mt * FF + (fc + 1) * 256]
            p.emit("s", lambda ps=ps, mt=mt, gdst=gdst: E["s"].activation(
                gdst, ps[:, :], AF.Gelu, scale=col(COL_R1, mt)),
                reads=[("PSH", pk), ck_(COL_R1, mt)],
                writes=[("ar", mt, fc // 2)])
            p.emit("v", lambda mt=mt, fc=fc, gdst=gdst: E["v"].reduce_sum(
                stats1[:, (mt * FC1 + fc) * 2:(mt * FC1 + fc) * 2 + 1],
                gdst, axis=AX),
                reads=[("ar", mt, fc // 2)], rw=[("stats1", mt)])
            p.emit("v", lambda gdst=gdst: E["v"].tensor_mul(
                sqs4, gdst, gdst),
                reads=[("ar", mt, fc // 2)], writes=[("sqs4",)])
            p.emit("v", lambda mt=mt, fc=fc: E["v"].reduce_sum(
                stats1[:, (mt * FC1 + fc) * 2 + 1:(mt * FC1 + fc) * 2 + 2],
                sqs4, axis=AX),
                reads=[("sqs4",)], rw=[("stats1", mt)])
            w1_prep(fc + 1, mt)
    stkP4.__exit__(None, None, None)
    stk4.__exit__(None, None, None)
    p.gate()

    stk56 = ExitStack()
    stk56.__enter__()
    s2negrow = sball(stk56, "s2negrow", [1, MQ], BF16)
    stk5 = ExitStack()
    stk5.__enter__()
    gp = sball(stk5, "gp", [128, 512], F32)
    scr_i = sball(stk5, "scr_i5", [128, 512], I32)
    scr = [sball(stk5, f"scr5_{i}", [128, 512], F32) for i in range(4)]
    bp = sball(stk5, "bp", [128, 512], F32)
    rowg = sball(stk5, "rowg", [1, 512], F32)
    rowh = sball(stk5, "rowh", [1, 512], F32)
    stkP5 = ExitStack()
    stkP5.__enter__()
    PSB = [stkP5.enter_context(nc.psum_tensor(f"psb{i}", [128, 512], F32))
           for i in range(2)]
    ps_tr2 = stkP5.enter_context(nc.psum_tensor("ps_tr2", [1, 128], BF16))

    for mt in range(MT):
        base = mt * FC1 * 2
        p.emit("v", lambda base=base: E["v"].reduce_sum(
            lc(3), stats1[:, base:base + FC1 * 2:2], axis=AX),
            reads=[("stats1", mt)], writes=[lk(3)])
        p.emit("v", lambda base=base: E["v"].reduce_sum(
            lc(5), stats1[:, base + 1:base + FC1 * 2:2], axis=AX),
            reads=[("stats1", mt)], writes=[lk(5)])
        p.emit("v", lambda: E["v"].tensor_scalar_mul(lc(0), lc(3), 1.0 / FF),
               reads=[lk(3)], writes=[lk(0)])
        p.emit("v", lambda: E["v"].tensor_mul(lc(6), lc(0), lc(0)),
               reads=[lk(0)], writes=[lk(6)])
        p.emit("v", lambda: E["v"].scalar_tensor_tensor(
            lc(9), lc(5), 1.0 / FF, lc(6), OP.mult, OP.subtract),
            reads=[lk(5), lk(6)], writes=[lk(9)])
        p.emit("s", lambda: E["s"].activation(
            lc(2), lc(9), AF.Sqrt, bias=eps_col[:, :], scale=1.0),
            reads=[lk(9), ("c", 8)], writes=[lk(2)])
        p.emit("v", lambda mt=mt: E["v"].reciprocal(col(COL_A, mt), lc(2)),
               reads=[lk(2)], writes=[ck_(COL_A, mt)])
        p.emit("v", lambda mt=mt: E["v"].scalar_tensor_tensor(
            col(COL_BB, mt), lc(0), -1.0, col(COL_A, mt), OP.mult, OP.mult),
            reads=[lk(0), ck_(COL_A, mt)], writes=[ck_(COL_BB, mt)])

    for fc in range(16):
        p.emit("y", lambda fc=fc: E["y"].dma_start(
            out=rowg[0:1, :], in_=d_gm[0:1, fc * 512:(fc + 1) * 512]),
            writes=[("rowg",)], dma=True)
        p.emit("y", lambda fc=fc: E["y"].dma_start(
            out=rowh[0:1, :], in_=d_bt[0:1, fc * 512:(fc + 1) * 512]),
            writes=[("rowh",)], dma=True)
        p.emit("t", lambda: E["t"].matmul(
            PSB[0][:, :], ones_r_f[:, :], rowg[0:1, :], start=True, stop=True),
            reads=[("rowg",), ("c", 3)], writes=[("PSB", 0)])
        p.emit("s", lambda: E["s"].activation(gp[:, :], PSB[0][:, :], AF.Copy),
               reads=[("PSB", 0)], writes=[("gp",)])
        p.emit("t", lambda: E["t"].matmul(
            PSB[1][:, :], ones_r_f[:, :], rowh[0:1, :], start=True, stop=True),
            reads=[("rowh",), ("c", 3)], writes=[("PSB", 1)])
        p.emit("s", lambda: E["s"].activation(bp[:, :], PSB[1][:, :], AF.Copy),
               reads=[("PSB", 1)], writes=[("bp",)])
        for mt in range(MT):
            gsl = gA[:, mt * FF + fc * 512: mt * FF + (fc + 1) * 512]
            hsl = gH[:, mt * FF + fc * 512: mt * FF + (fc + 1) * 512]
            ark = ("ar", mt, fc)
            p.emit("v", lambda gsl=gsl, mt=mt: E["v"].tensor_scalar(
                scr[0][:, :], gsl, col(COL_A, mt), col(COL_BB, mt),
                OP.mult, OP.add),
                reads=[ark, ck_(COL_A, mt), ck_(COL_BB, mt)], writes=[SK[0]])
            p.emit("g", lambda: E["g"].tensor_mul(
                scr[1][:, :], scr[0][:, :], gp[:, :]),
                reads=[SK[0], ("gp",)], writes=[SK[1]])
            p.emit("v", lambda hsl=hsl: E["v"].tensor_add(
                hsl, scr[1][:, :], bp[:, :]),
                reads=[SK[1], ("bp",)], rw=[ark])
            p.emit("v", lambda hsl=hsl: E["v"].reduce_max(
                lc(3), hsl, axis=AX, apply_absolute_value=True),
                reads=[ark], writes=[lk(3)])
            if fc == 0:
                p.emit("v", lambda mt=mt: E["v"].tensor_copy(
                    col(COL_M2, mt), lc(3)),
                    reads=[lk(3)], writes=[ck_(COL_M2, mt)])
            else:
                p.emit("v", lambda mt=mt: E["v"].tensor_max(
                    col(COL_M2, mt), col(COL_M2, mt), lc(3)),
                    reads=[lk(3)], rw=[ck_(COL_M2, mt)])
    for mt in range(MT):
        p.emit("v", lambda mt=mt: E["v"].tensor_scalar_max(
            lc(5), col(COL_M2, mt), float(EPS)),
            reads=[ck_(COL_M2, mt)], writes=[lk(5)])
        p.emit("v", lambda: E["v"].reciprocal(lc(6), lc(5)),
               reads=[lk(5)], writes=[lk(6)])
        p.emit("v", lambda mt=mt: E["v"].tensor_scalar_mul(
            col(COL_S2, mt), lc(6), 127.0),
            reads=[lk(6)], writes=[ck_(COL_S2, mt)])
        p.emit("v", lambda mt=mt: E["v"].tensor_mul(
            lc(7), thc(MEAN2), lc(5)),
            reads=[lk(5), ("thbc",)], writes=[lk(7)])
        p.emit("v", lambda mt=mt: E["v"].tensor_scalar_mul(
            col(COL_R2, mt), lc(7), 1.0 / 127.0),
            reads=[lk(7)], writes=[ck_(COL_R2, mt)])
        p.emit("v", lambda mt=mt: E["v"].tensor_scalar_mul(
            lc(8), col(COL_S2, mt), -1.0),
            reads=[ck_(COL_S2, mt)], writes=[lk(8)])
        p.emit("v", lambda: E["v"].tensor_copy(tiny_bf[:, :], lc(8)),
               reads=[lk(8)], writes=[("tinybf",)])
        p.emit("t", lambda: E["t"].transpose(
            ps_tr2[:, :], tiny_bf[:, :], ident_bf[:, :]),
            reads=[("tinybf",), ("c", 5)], writes=[("PStr2",)])
        p.emit("v", lambda mt=mt: E["v"].tensor_copy(
            s2negrow[0:1, mt * 128:(mt + 1) * 128], ps_tr2[:, :]),
            reads=[("PStr2",)], rw=[("s2negrow",)])

    for mt in range(MT - 1, -1, -1):
        for fc in range(16):
            hsl = gH[:, mt * FF + fc * 512: mt * FF + (fc + 1) * 512]
            ksl = gA[:, mt * FF + fc * 512: mt * FF + (fc + 1) * 512]
            ark = ("ar", mt, fc)
            p.emit("v", lambda hsl=hsl, mt=mt: E["v"].tensor_single_scalar(
                scr[0][:, :], hsl, col(COL_S2, mt), OP.mult),
                reads=[ark, ck_(COL_S2, mt)], writes=[SK[0]])
            p.emit("v", lambda: E["v"].tensor_copy(scr_i[:, :], scr[0][:, :]),
                   reads=[SK[0]], writes=[("scri",)])
            p.emit("v", lambda: E["v"].tensor_copy(scr[1][:, :], scr_i[:, :]),
                   reads=[("scri",)], writes=[SK[1]])
            p.emit("g", lambda: E["g"].tensor_sub(
                scr[2][:, :], scr[0][:, :], scr[1][:, :]),
                reads=[SK[0], SK[1]], writes=[SK[2]])
            p.emit("v", lambda: E["v"].scalar_tensor_tensor(
                scr[3][:, :], scr[2][:, :], 0.5, scr[1][:, :],
                OP.is_ge, OP.add),
                reads=[SK[2], SK[1]], writes=[SK[3]])
            p.emit("v", lambda ksl=ksl: E["v"].scalar_tensor_tensor(
                ksl, scr[2][:, :], -0.5, scr[3][:, :], OP.is_le, OP.subtract),
                reads=[SK[2], SK[3]], rw=[ark])
        src = gA[:, mt * FF:(mt + 1) * FF]
        dst = gA[:, (mt + 1) * FF:(mt + 2) * FF].rearrange(
            "q (c t) -> q c t", t=128)
        slot_reads = [("ar", mt, k) for k in range(16)]
        slot_writes = [("ar", mt + 1, k) for k in range(16)]
        p.emit("y", lambda src=src, dst=dst: E["y"].dma_start_transpose(
            dst, src),
            reads=slot_reads, writes=[("khT", mt)] + slot_writes, xp=True)
    stkP5.__exit__(None, None, None)
    stk5.__exit__(None, None, None)
    p.gate()

    stk6 = ExitStack()
    stk6.__enter__()
    w2stage = [sball(stk6, f"w2stage{i}", [128, 512], F32) for i in range(2)]
    scr = [sball(stk6, f"scr6_{i}", [128, 512], F32) for i in range(4)]
    q2p = sball(stk6, "q2p", [128, 512], BF16)
    q2n = sball(stk6, "q2n", [128, 512], BF16)
    w2qT = [sball(stk6, f"w2qT{i}", [128, 64 * 128], BF16) for i in range(2)]
    rowa2 = sball(stk6, "rowa2", [1, 128], F32)
    rowbf2 = [sball(stk6, f"rowbf2{i}", [1, 128], BF16) for i in range(2)]
    stkP6 = ExitStack()
    stkP6.__enter__()
    PS2 = [stkP6.enter_context(nc.psum_tensor(f"ps2_{i}", [128, 512], F32))
           for i in range(2)]
    khview = gA[:, :].rearrange("q (sl t) -> q sl t", t=FF)

    def w2_rowa(dc):
        p.emit("s", lambda dc=dc: E["s"].dma_start(
            out=rowa2[0:1, :], in_=d_b2[0:1, dc * 128:(dc + 1) * 128]),
            writes=[("rowa2",)], dma=True, lazy=True)

    def w2_rowbf(dc):
        p.flush("sd")
        p.emit("v", lambda ib=dc % 2: E["v"].tensor_single_scalar(
            rowbf2[ib][0:1, :], rowa2[0:1, :], sc(SC_NR2), OP.mult),
            reads=[("rowa2",), sk(SC_NR2)], writes=[("rowbf2", dc % 2)])

    def w2_load_l(dc, l):
        p.emit("s", lambda dc=dc, l=l: E["s"].dma_start(
            out=w2stage[l % 2][:, :],
            in_=d_w2[dc * 128:(dc + 1) * 128, l * 512:(l + 1) * 512]),
            writes=[("w2s", l % 2)], dma=True, lazy=True)

    def w2_quant_l(dc, l):
        ib = dc % 2
        p.flush("sd")
        p.emit("v", lambda l=l: E["v"].tensor_single_scalar(
            q2p[:, :], w2stage[l % 2][:, :], thc(TH2), OP.is_ge),
            reads=[("w2s", l % 2), ("thbc",)], writes=[("q2p",)])
        p.emit("v", lambda l=l: E["v"].tensor_single_scalar(
            q2n[:, :], w2stage[l % 2][:, :], thc(NTH2), OP.is_le),
            reads=[("w2s", l % 2), ("thbc",)], writes=[("q2n",)])
        p.emit("g", lambda: E["g"].tensor_sub(
            q2p[:, :], q2n[:, :], q2p[:, :]),
            reads=[("q2n",)], rw=[("q2p",)])
        dst = w2qT[ib][:, :].rearrange("q (c t) -> q c t", t=128)[
            :, 4 * l:4 * l + 4, :]
        p.emit("y", lambda dst=dst: E["y"].dma_start_transpose(
            dst, q2p[:, :]),
            reads=[("q2p",)], rw=[("w2qT", ib)], xp=True)

    def w2_prep(dc, step):
        if dc >= 16:
            return
        if step == 0:
            w2_rowa(dc)
            w2_load_l(dc, 0)
            w2_load_l(dc, 1)
            w2_rowbf(dc)
        w2_quant_l(dc, step)
        if step + 2 < 16:
            w2_load_l(dc, step + 2)

    for st in range(16):
        w2_prep(0, st)

    for dc in range(16):
        ib = dc % 2
        for pr in range(2):
            ps = PS2[pr]
            rhs = khview[:, 4 * pr + 1: 4 * pr + 5,
                         0:FF].rearrange("q sl (c t) -> q sl c t", t=128)
            khr = [("khT", 4 * pr + k) for k in range(4)]

            def mm_2(ps=ps, ib=ib, rhs=rhs, fg_=0):
                inst = None
                for f in range(fg_ * 16, fg_ * 16 + 16):
                    inst = E["t"].matmul(
                        ps[:, :], w2qT[ib][:, f * 128:(f + 1) * 128],
                        rhs[:, :, f, :], start=(f == 0), stop=False,
                        skip_group_check=True)
                return inst
            for fg in range(4):
                p.emit("t", lambda fg=fg: mm_2(fg_=fg),
                       reads=khr + [("w2qT", ib)], rw=[("PS2", pr)])
                for st_ in range(2):
                    w2_prep(dc + 1, pr * 8 + fg * 2 + st_)
            p.emit("t", lambda ps=ps, pr=pr, ib=ib: E["t"].matmul(
                ps[:, :], rowbf2[ib][0:1, :],
                s2negrow[0:1, pr * 512:(pr + 1) * 512],
                start=False, stop=True, skip_group_check=True),
                reads=[("rowbf2", ib), ("s2negrow",)], rw=[("PS2", pr)])
            p.emit("v", lambda ps=ps: E["v"].tensor_copy(
                scr_bf[0][:, :], ps[:, :]),
                reads=[("PS2", pr)], writes=[SB[0]])
            dstb = scr_bf[1][:, :].rearrange("q (c t) -> q c t", t=128)
            p.emit("y", lambda dstb=dstb: E["y"].dma_start_transpose(
                dstb, scr_bf[0][:, :]),
                reads=[SB[0]], writes=[SB[1]], xp=True)
            for tb in range(4):
                mt = 4 * pr + tb
                p.emit("v", lambda tb=tb, mt=mt: E["v"].tensor_single_scalar(
                    scr[tb][:, 0:128], scr_bf[1][:, tb * 128:(tb + 1) * 128],
                    col(COL_R2, mt), OP.mult),
                    reads=[SB[1], ck_(COL_R2, mt)], writes=[SK[tb]])
                p.emit("g", lambda tb=tb, mt=mt, dc=dc: E["g"].dma_start(
                    out=d_out[mt * 128:(mt + 1) * 128,
                              dc * 128:(dc + 1) * 128],
                    in_=scr[tb][:, 0:128], accum_op=OP.add),
                    reads=[SK[tb]], rw=[("out", mt)], dma=True, lazy=True)
            p.flush("gd")
    stkP6.__exit__(None, None, None)
    stk6.__exit__(None, None, None)
    stk56.__exit__(None, None, None)
    stkF.__exit__(None, None, None)
    p.gate()

    for _sn in ("gd", "yd", "sd", "xd", "t", "v", "s", "y"):
        E["g"].wait_ge(p.sems[_sn], p.cnt[_sn])
    nc.gpsimd.sem_clear(p.sems["bt"])
    stk.__exit__(None, None, None)
    return nc


_NC_CACHE = [None]


def kernel(**inputs):
    x = np.ascontiguousarray(np.asarray(inputs["L0_attn_x"], np.float32))
    wqkv = np.ascontiguousarray(np.asarray(inputs["L0_attn_w_qkv"], np.float32))
    wout = np.ascontiguousarray(np.asarray(inputs["L0_attn_w_out"], np.float32))
    w1 = np.ascontiguousarray(np.asarray(inputs["L0_ffn_w1"], np.float32))
    b1 = np.ascontiguousarray(np.asarray(inputs["L0_ffn_b1"],
                                         np.float32).reshape(1, FF))
    gm = np.ascontiguousarray(np.asarray(inputs["L0_ffn_gamma"],
                                         np.float32).reshape(1, FF))
    bt = np.ascontiguousarray(np.asarray(inputs["L0_ffn_beta"],
                                         np.float32).reshape(1, FF))
    w2 = np.ascontiguousarray(np.asarray(inputs["L0_ffn_w2"], np.float32))
    b2 = np.ascontiguousarray(np.asarray(inputs["L0_ffn_b2"],
                                         np.float32).reshape(1, D))

    kr = np.arange(128)[:, None]
    qr = np.arange(128)[None, :]
    masks = {}
    for half in range(2):
        masks[half] = (
            np.ascontiguousarray(
                (kr <= 2 * qr + half).astype(ml_dtypes.bfloat16)),
            np.ascontiguousarray(
                (kr <= 2 * qr + half - 128).astype(ml_dtypes.bfloat16)))

    if _NC_CACHE[0] is None:
        _NC_CACHE[0] = build_program()
    nc = _NC_CACHE[0]

    in_maps = []
    for c in range(8):
        b, half = c // 2, c % 2
        in_maps.append({
            "x": x[b],
            "xmine": np.ascontiguousarray(x[b, half::2]),
            "wqkv": wqkv, "wout": wout, "w1": w1, "w2": w2,
            "b1": b1, "gamma": gm, "beta": bt, "b2": b2,
            "mask0": masks[half][0],
            "mask1": masks[half][1],
        })
    res = run_bass_kernel_spmd(nc, in_maps, core_ids=list(range(8)))
    out = np.empty((B, S, D), np.float32)
    for c in range(8):
        b, half = c // 2, c % 2
        out[b, half::2] = res.results[c]["out"]
    return out
